# version 11
# baseline (speedup 1.0000x reference)
import contextlib
import numpy as np
import ml_dtypes
import concourse.bass as bass
import concourse.mybir as mybir
from concourse.bass_utils import run_bass_kernel_spmd

F32 = mybir.dt.float32
BF16 = mybir.dt.bfloat16
AF = mybir.ActivationFunctionType
ALU = mybir.AluOpType
AX = mybir.AxisListType

D = 1024
DT = 8
L = 4096
T = 512
NTILE = L // T
N = 16
NH, DK, DV = 4, 256, 256
NEXP = 8
STAGE = "full"

ENGS = ("sync", "gpsimd", "vector", "scalar", "tensor")
SEM_OF = {"vector": "v", "scalar": "s", "gpsimd": "g", "tensor": "t", "sync": "q"}

_IO = {
    "tensor_copy": ((0,), (1,), ()),
    "tensor_tensor": ((0,), (1, 2), ()),
    "tensor_add": ((0,), (1, 2), ()),
    "tensor_sub": ((0,), (1, 2), ()),
    "tensor_mul": ((0,), (1, 2), ()),
    "tensor_scalar": ((0,), (1, 2, 3), ()),
    "scalar_tensor_tensor": ((0,), (1, 2, 3), ()),
    "tensor_tensor_scan": ((0,), (1, 2, 3), ()),
    "reduce_max": ((0,), (1,), ()),
    "reduce_sum": ((0,), (1,), ()),
    "tensor_reduce": ((0,), (1,), ()),
    "reciprocal": ((0,), (1,), ()),
    "memset": ((0,), (), ()),
    "iota": ((0,), (), ()),
    "affine_select": ((0,), (1,), ()),
    "activation": ((0,), (1,), ("bias", "scale", "accum_out")),
    "copy": ((0,), (1,), ()),
    "square": ((0,), (1,), ()),
    "matmul": ((0,), (1, 2), ()),
    "transpose": ((0,), (1, 2), ()),
    "select": ((0,), (1, 2, 3), ()),
    "dma_start": ((), (), ()),
    "nop": ((), (), ()),
}


def _ap_region(ap):
    if ap is None or isinstance(ap, (int, float)):
        return None
    try:
        t = ap.tensor
        space = str(ap.space)
        name = t.name
        p0 = ap.base_partition()
        shape = ap.shape
        pat = [list(d) for d in ap.ap]
    except Exception:
        return None
    if "DRAM" in space.upper() or "Dram" in space:
        return None
    npart = pat[0][1] if pat else 1
    span = 0
    for (st, cnt) in pat[1:]:
        span += abs(int(st)) * (int(cnt) - 1)
    esz = 2 if "bfloat16" in str(ap.dtype) else 4
    foff = int(ap.offset) - int(p0) * int(pat[0][0]) if pat else int(ap.offset)
    b0 = foff * esz
    b1 = b0 + (span + 1) * esz
    return (name, space, int(p0), int(p0) + int(npart), b0, b1)


def _overlap(r1, r2):
    return (r1[0] == r2[0] and r1[2] < r2[3] and r2[2] < r1[3]
            and r1[4] < r2[5] and r2[4] < r1[5])


class KB:
    def __init__(self, nc):
        self.nc = nc
        self.plan = {e: [] for e in ENGS}
        self.cnt = {}
        self.semnames = []
        for s in ("q", "gq", "qe", "qb", "qf", "v", "s", "t", "g", "store"):
            self.sem(s)
        self.recs = {}
        self.sat = {e: {} for e in ENGS}
        self.dma_groups = {s: [] for s in ("q", "gq", "qe", "qb", "qf", "store")}

    def sem(self, name):
        if name not in self.cnt:
            self.cnt[name] = 0
            self.semnames.append(name)
        return name

    def now(self, name):
        return self.cnt[name]

    def dma_barrier(self, sem):
        if self.cnt[sem]:
            g = self.dma_groups[sem]
            if not g or g[-1] != self.cnt[sem]:
                g.append(self.cnt[sem])

    def _lift(self, sem, val):
        if sem not in self.dma_groups:
            return val
        for b in self.dma_groups[sem]:
            if b >= val:
                return b
        return self.cnt[sem]

    def _auto_waits(self, eng, outs, ins):
        need = {}

        def add(psem, pval):
            need[psem] = max(need.get(psem, 0), pval)

        names = {r[0] for r in outs} | {r[0] for r in ins}
        for nm in names:
            for (reg, kind, peng, psem, pval) in self.recs.get(nm, ()):
                if kind == 'W':
                    hit = False
                    for r in ins:
                        if _overlap(reg, r):
                            hit = True
                            break
                    if hit and not (peng == eng == "tensor"):
                        add(psem, pval)
                for r in outs:
                    if _overlap(reg, r):
                        if peng == eng == "tensor" and kind == 'W':
                            break
                        add(psem, pval)
                        break
        sat = self.sat[eng]
        out = []
        for s_, v_ in need.items():
            v_ = self._lift(s_, v_)
            if sat.get(s_, 0) < v_:
                out.append((s_, v_))
        return out

    def _record(self, reg, kind, eng, sem, val):
        lst = self.recs.setdefault(reg[0], [])
        lst.append((reg, kind, eng, sem, val))
        if len(lst) > 96:
            old, keep = lst[:48], lst[48:]
            coarse = {}
            for (r, k, pe, ps, pv) in old:
                key = (pe, ps, k)
                prev = coarse.get(key)
                if prev is None or pv > prev[4]:
                    big = (r[0], r[1], 0, 128, 0, 1 << 30)
                    coarse[key] = (big, k, pe, ps, pv)
            self.recs[reg[0]] = list(coarse.values()) + keep

    def op(self, eng, meth, args=(), kwargs=None, waits=(), inc=None, incn=1):
        kwargs = kwargs or {}
        if inc is None and meth not in ("dma_start",):
            inc = SEM_OF[eng]
        if inc is not None:
            self.cnt[inc] += incn
            tgt = self.cnt[inc]
        else:
            tgt = None
        oidx, iidx, ikw = _IO.get(meth, ((), (), ()))
        outs, ins = [], []
        for i, a in enumerate(args):
            r = _ap_region(a)
            if r is None:
                continue
            (outs if i in oidx else ins).append(r)
        for k in ikw:
            r = _ap_region(kwargs.get(k))
            if r is not None:
                (outs if k == "accum_out" else ins).append(r)
        if meth == "dma_start":
            r = _ap_region(kwargs.get("out"))
            if r is not None:
                outs.append(r)
            r = _ap_region(kwargs.get("in_"))
            if r is not None:
                ins.append(r)
        auto = self._auto_waits(eng, outs, ins) if (outs or ins) else []
        if meth == "dma_start" and inc in self.dma_groups:
            g = self.dma_groups[inc]
            prev_total = self.cnt[inc] - incn
            if g and g[-1] == prev_total and prev_total > 0:
                auto = list(auto) + [(inc, prev_total)]
        w = {}
        for (s_, v_) in tuple(waits) + tuple(auto):
            if not v_ or v_ <= 0:
                continue
            v_ = self._lift(s_, v_)
            if self.sat[eng].get(s_, 0) < v_:
                w[s_] = max(w.get(s_, 0), v_)
        for s_, v_ in w.items():
            self.sat[eng][s_] = v_
        self.plan[eng].append((meth, args, kwargs, tuple(w.items()), inc, tgt, incn))
        if inc is not None:
            for r in outs:
                self._record(r, 'W', eng, inc, tgt)
            for r in ins:
                self._record(r, 'R', eng, inc, tgt)
        return tgt

    def replay(self, eng_handle, eng_name, sems):
        for meth, args, kwargs, waits, inc, tgt, incn in self.plan[eng_name]:
            for s, v in waits:
                eng_handle.wait_ge(sems[s], v)
            inst = getattr(eng_handle, meth)(*args, **kwargs)
            if inc is not None:
                inst.then_inc(sems[inc], incn)


def build_core_program():
    nc = bass.Bass("TRN2", target_bir_lowering=False)
    kb = KB(nc)
    dram = {}

    def din(name, shape, dtype=F32):
        dram[name] = nc.dram_tensor(name, list(shape), dtype, kind="ExternalInput").ap()
        return dram[name]

    x0_d = din("x0", [L, D])
    conv_w_d = din("aw_conv_w", [D, 1, 4])
    conv_b_d = din("aw_conv_b", [D])
    xpw_d = din("aw_xpw", [96, D])
    dtw_d = din("aw_dtw", [D, 64])
    dtb_d = din("aw_dtb", [D])
    Alog_d = din("aw_Alog", [D, N])
    Dskip_d = din("aw_Dskip", [D])
    rw_d = din("fw_rw", [D, NEXP])
    rb_d = din("fw_rb", [NEXP])
    ew_d = din("fw_ew", [NEXP, D, D])
    eb_d = din("fw_eb", [NEXP, D])
    Wq_d = din("bw_Wq", [D, D])
    Wk_d = din("bw_Wk", [D, D])
    Wv_d = din("bw_Wv", [D, D])
    cwq_d = din("bw_cwq", [D, 4])
    cwk_d = din("bw_cwk", [D, 4])
    cwv_d = din("bw_cwv", [D, 4])
    Wb_d = din("bw_Wb", [D, NH])
    Wid_d = din("bw_Wid", [D, D])
    alpha_d = din("bw_alpha", [NH])
    firs_d = din("bw_firs", [NH, DV, 3])
    firl_d = din("bw_firl", [NH, DV, 63])
    gW1_d = din("bw_gW1", [D + 16, D])
    gb1_d = din("bw_gb1", [D])
    gW2_d = din("bw_gW2", [D, 4])
    gb2_d = din("bw_gb2", [4])
    ltemp_d = din("bw_ltemp", [NH, 4])
    floor_d = din("bw_floor", [NH, 4])
    cres_d = din("bw_cres", [NH])
    onw_d = din("bw_onw", [DV])
    Wo_d = din("bw_Wo", [D, D])
    y_d = nc.dram_tensor("y", [L, D], F32, kind="ExternalOutput").ap()
    fir_dram = nc.dram_tensor("fir_scratch", [DT, 128, 66 * 128], BF16, kind="Internal").ap()
    xptf_dram = nc.dram_tensor("xptf_scratch", [128, 4 * DT * 112], BF16, kind="Internal").ap()

    AL = nc.alloc_sbuf_tensor

    x0f = AL("x0f", [128, DT * T], BF16)
    x1f = AL("x1f", [128, DT * T], BF16)
    xtm = AL("xtm", [128, 4 * D], F32)
    u_f = AL("u_f", [128, DT * (T + 3)], BF16)
    lnt1 = AL("lnt1", [128, T], F32)
    xnf = AL("xnf", [128, T], F32)
    sqs = AL("sqs", [128, T], F32)
    rowbuf = AL("rowbuf", [1, 3 * T], F32)
    delta = AL("delta", [128, 2 * T], F32)
    dtBC = AL("dtBC", [64, T], BF16)
    bc48 = AL("bc48", [48, T], BF16)
    bmcb = AL("bmcb", [128, 2 * N * T], BF16)
    dAt = AL("dAt", [128, T], BF16)
    ubm = AL("ubm", [128, T], BF16)
    bbt = AL("bbt", [128, T], BF16)
    Ht = AL("Ht", [128, (N // 2) * T], BF16)
    ymt = AL("ymt", [128, (N // 2) * T], BF16)
    Hc = AL("Hc", [128, DT * N], F32)
    zt = AL("zt", [128, T], F32)
    at_ = AL("at_", [128, T], F32)
    guard = AL("guard", [128, 1], F32)

    k_f = AL("k_f", [128, DT * (T + 3)], BF16)
    q_f = AL("q_f", [128, DT * (T + 3)], BF16)
    v_f = AL("v_f", [128, DT * (T + 62)], BF16)

    xpw = AL("xpw", [96, D], BF16)
    xpT = None
    cw_c = AL("cw_c", [128, DT * 4], F32)
    dtw_sb = AL("dtw_sb", [128, DT * 64], BF16)
    dtwT = AL("dtwT", [64, DT * 128], BF16)
    dtb_c = AL("dtb_c", [128, DT], F32)
    Dskip_c = AL("Dskip_c", [128, DT], F32)
    Dsk_dg = AL("Dsk_dg", [128, DT * 128], BF16)
    Alog_c = AL("Alog_c", [128, DT * N], F32)
    A_c = AL("A_c", [128, DT * N], F32)
    nA_c = AL("nA_c", [128, DT * N], F32)
    invA_c = AL("invA_c", [128, DT * N], F32)
    cb96 = AL("cb96", [112, 1], F32)
    cbrow = AL("cbrow", [128, DT], F32)
    cbrow_b = AL("cbrow_b", [128, DT], BF16)
    enB = AL("enB", [48, N * 128], BF16)
    enC = AL("enC", [48, N * 128], BF16)
    diag01 = AL("diag01", [128, 128], BF16)
    identf = AL("identf", [128, 128], F32)
    onescf = AL("onescf", [128, 1], F32)
    onescb = AL("onescb", [128, 1], BF16)
    onesrow = AL("onesrow", [1, 128], F32)
    cone = AL("cone", [128, 1], F32)
    cepsln = AL("cepsln", [128, 1], F32)
    vec8a = AL("vec8a", [8, 128], F32)
    vec8b = AL("vec8b", [8, 128], F32)
    vec8c = AL("vec8c", [8, 128], F32)

    S_f = AL("S_f", [128, NH * 2 * 256], F32)
    qtails = AL("qtails", [128, DT * 3], BF16)
    ktails = AL("ktails", [128, DT * 3], BF16)
    vrawtails = Alog_c[:, 0:DT * 3 // 2].bitcast(BF16)
    cvscr = xnf[:, 0:3].bitcast(BF16)
    wb_sb = AL("wb_sb", [128, DT * NH], BF16)
    cwq_c = AL("cwq_c", [128, DT * 4], F32)
    cwk_c = AL("cwk_c", [128, DT * 4], F32)
    cwv_c = AL("cwv_c", [128, DT * 4], F32)
    acol = AL("acol", [128, NH], F32)
    scres = AL("scres", [128, NH], F32)
    onw_c = AL("onw_c", [128, 2], F32)
    gb1_c = AL("gb1_c", [128, DT], F32)
    gb2_c = AL("gb2_c", [4, 1], F32)
    gw2_sb = AL("gw2_sb", [128, DT * 4], BF16)
    invt_c = AL("invt_c", [4, NH], F32)
    fl_bc = AL("fl_bc", [128, 16], F32)
    fs_bc = AL("fs_bc", [128, NH], F32)
    gw1s_p1 = AL("gw1s_p1", [128, D], BF16)
    gw1s_p2 = AL("gw1s_p2", [4, D], BF16)
    fircl = bmcb[:, 0:1056].bitcast(F32)
    selst = AL("selst", [128, 12 * 68], BF16)
    mask_sl = AL("mask_sl", [128, 128], BF16)
    mask_ui = AL("mask_ui", [128, 128], BF16)
    E4 = AL("E4", [4, NH * 128], BF16)
    cepsl2 = AL("cepsl2", [128, 1], F32)
    cepso = AL("cepso", [128, 1], F32)
    srow16 = AL("srow16", [1, 16], F32)
    fsrow4 = AL("srow4b", [1, 4], F32)

    rw_sb = AL("rw_sb", [128, DT * NEXP], F32)
    rb_c = AL("rb_c", [8, 1], F32)
    eb_sb = AL("eb_sb", [8, DT * 128], BF16)
    E8 = AL("E8", [8, NEXP * 128], BF16)
    ew_slot = AL("ew_slot", [128, DT * D], BF16)
    xnb = AL("xnb", [128, DT * T], BF16)
    xeb = AL("xeb", [128, 2 * T], BF16)
    lgrow = AL("lgrow", [8, T], F32)
    lgtm = AL("lgtm", [128, 4 * NEXP], F32)
    mrow = AL("mrow", [128, NEXP], F32)
    mtmb = AL("mtmb", [128, 4 * NEXP], BF16)
    mrow4 = AL("mrow4", [128, 4 * NEXP], F32)
    srow4 = AL("srow4", [128, 4 * NEXP], F32)
    rc16 = AL("rc16", [128, 16], F32)
    m_fm = dtBC[0:8, 0:T]
    rcol = AL("rcol", [128, 4], F32)
    mbc_all = bmcb
    eslot2 = bmcb[:, 4096:4096 + DT * D]
    dAt2 = xeb[:, 0:T]

    PSH = [nc.alloc_psum_tensor(f"bank{i}", [128, 512], F32) for i in range(8)]
    PS = [h[:] for h in PSH]

    bank_free = {i: ("t", 0) for i in range(8)}

    def mm(bank, out_ap, lhsT, rhs, start, stop, waits=(), **kw):
        w = list(waits)
        if start and bank is not None:
            w.append(bank_free[bank])
        return kb.op("tensor", "matmul", (out_ap, lhsT, rhs),
                     dict(start=start, stop=stop, **kw), waits=w, inc="t")

    def tp(bank, out_ap, in_ap, ident_ap, waits=()):
        w = list(waits)
        if bank is not None:
            w.append(bank_free[bank])
        return kb.op("tensor", "transpose", (out_ap, in_ap, ident_ap), waits=w, inc="t")

    def evac(eng, meth, args, kwargs=None, waits=(), bank=None):
        sem = SEM_OF[eng]
        tgt = kb.op(eng, meth, args, kwargs, waits=waits, inc=sem)
        if bank is not None:
            bank_free[bank] = (sem, tgt)
        return tgt

    def vguard(ap_col):
        return kb.op("vector", "tensor_copy", (guard[:ap_col.shape[0], 0:1], ap_col), inc="v")

    def sguard(ap_col):
        return kb.op("scalar", "copy", (guard[:ap_col.shape[0], 0:1], ap_col), inc="s")

    def load(dst, src, waits=(), sem="q"):
        return kb.op("sync", "dma_start", (), dict(out=dst, in_=src), waits=waits,
                     inc=sem, incn=16)

    def gload(dst, src, waits=(), sem="gq"):
        return kb.op("gpsimd", "dma_start", (), dict(out=dst, in_=src),
                     waits=waits, inc=sem, incn=16)

    xpT = Ht[:, 0:DT * 112]

    load(vec8a[0:8, :], dtb_d.rearrange("(dt p) -> dt p", p=128))
    load(vec8b[0:8, :], conv_b_d.rearrange("(dt p) -> dt p", p=128))
    load(vec8c[0:8, :], Dskip_d.rearrange("(dt p) -> dt p", p=128))
    load(Alog_c[:, :].rearrange("p (dt n) -> p dt n", n=N),
         Alog_d.rearrange("(dt p) n -> p dt n", p=128))
    load(cw_c[:, :].rearrange("p (dt k) -> p dt k", k=4),
         conv_w_d.rearrange("(dt p) o k -> p dt (o k)", p=128))
    load(rw_sb[:, :].rearrange("p (dt e) -> p dt e", e=NEXP),
         rw_d.rearrange("(dt p) e -> p dt e", p=128))
    load(rb_c[0:8, 0:1], rb_d.rearrange("(e o) -> e o", o=1))
    gload(xpw[:, :], xpw_d)
    gload(dtw_sb[:, :].rearrange("p (dt r) -> p dt r", r=64),
          dtw_d.rearrange("(dt p) r -> p dt r", p=128))
    gload(eb_sb[0:8, :].rearrange("e (ct q) -> e ct q", q=128),
          eb_d.rearrange("e (ct q) -> e ct q", q=128))
    ldq = kb.now("q")
    gldq = kb.now("gq")
    kb.dma_barrier("q")
    kb.dma_barrier("gq")

    kb.op("gpsimd", "memset", (diag01[:], 1.0), inc="g")
    kb.op("gpsimd", "affine_select",
          (diag01[:], diag01[:], [[-1, 128]], ALU.is_equal, 0.0),
          dict(base=0, channel_multiplier=1), waits=[("g", kb.now("g"))], inc="g")
    kb.op("gpsimd", "memset", (enB[:], 1.0), inc="g")
    kb.op("gpsimd", "memset", (enC[:], 1.0), inc="g")
    kb.op("gpsimd", "memset", (E8[:], 1.0), inc="g")
    g_c0 = kb.now("g")
    for n in range(N):
        kb.op("gpsimd", "affine_select",
              (enB[0:48, n * 128:(n + 1) * 128], enB[0:48, n * 128:(n + 1) * 128],
               [[0, 128]], ALU.is_equal, 0.0),
              dict(base=-n, channel_multiplier=1), waits=[("g", g_c0)], inc="g")
        kb.op("gpsimd", "affine_select",
              (enC[0:48, n * 128:(n + 1) * 128], enC[0:48, n * 128:(n + 1) * 128],
               [[0, 128]], ALU.is_equal, 0.0),
              dict(base=-(32 + n), channel_multiplier=1), waits=[("g", g_c0)], inc="g")
    for e in range(NEXP):
        kb.op("gpsimd", "affine_select",
              (E8[0:8, e * 128:(e + 1) * 128], E8[0:8, e * 128:(e + 1) * 128],
               [[0, 128]], ALU.is_equal, 0.0),
              dict(base=-e, channel_multiplier=1), waits=[("g", g_c0)], inc="g")
    g_en = kb.now("g")

    kb.op("vector", "memset", (onescf[:], 1.0), inc="v")
    kb.op("vector", "memset", (onescb[:], 1.0), inc="v")
    kb.op("vector", "memset", (onesrow[:], 1.0), inc="v")
    kb.op("vector", "memset", (cone[:], 1.0), inc="v")
    kb.op("vector", "memset", (cepsln[:], 1e-5), inc="v")
    kb.op("vector", "memset", (Hc[:], 0.0), inc="v")
    kb.op("vector", "memset", (u_f[:], 0.0), inc="v")
    kb.op("vector", "tensor_copy", (identf[:], diag01[:]), waits=[("g", g_c0)], inc="v")
    v_c0 = kb.now("v")
    gv_id = vguard(identf[:, 127:128])

    s_nA = kb.op("scalar", "activation", (nA_c[:], Alog_c[:], AF.Exp),
                 waits=[("q", ldq)], inc="s")
    kb.op("vector", "tensor_scalar", (A_c[:], nA_c[:], -1.0, None, ALU.mult),
          waits=[("s", s_nA)], inc="v")
    kb.op("vector", "reciprocal", (invA_c[:], A_c[:]), inc="v")
    v_A = kb.now("v")

    tp(3, PS[3][0:128, 0:8], vec8a[0:8, :], identf[0:8, 0:8],
       waits=[("q", ldq), ("v", gv_id)])
    tp(None, PS[3][0:128, 8:16], vec8b[0:8, :], identf[0:8, 0:8])
    tp(None, PS[3][0:128, 16:24], vec8c[0:8, :], identf[0:8, 0:8])
    t_vec = kb.now("t")
    evac("scalar", "copy", (dtb_c[:], PS[3][0:128, 0:8]), waits=[("t", t_vec)])
    evac("scalar", "copy", (cbrow[:], PS[3][0:128, 8:16]))
    evac("scalar", "copy", (Dskip_c[:], PS[3][0:128, 16:24]), bank=3)
    s_vec = kb.now("s")

    for dt in range(DT):
        kb.op("vector", "tensor_tensor",
              (Dsk_dg[:, dt * 128:(dt + 1) * 128],
               Dskip_c[:, dt:dt + 1].broadcast_to((128, 128)), diag01[:], ALU.mult),
              waits=[("s", s_vec)], inc="v")
    v_dsk = kb.now("v")

    for dt in range(DT):
        tp(3, PS[3][0:128, 0:48].bitcast(BF16), xpw[0:96, dt * 128:(dt + 1) * 128],
           diag01[0:96, 0:96], waits=[("gq", gldq), ("v", gv_id)])
        tnow = kb.now("t")
        evac("scalar", "copy", (xpT[:, dt * 112: dt * 112 + 64],
                                PS[3][0:128, 0:32].bitcast(BF16)), waits=[("t", tnow)])
        evac("scalar", "copy", (xpT[:, dt * 112 + 64: dt * 112 + 80],
                                PS[3][0:128, 32:40].bitcast(BF16)))
        evac("scalar", "copy", (xpT[:, dt * 112 + 96: dt * 112 + 112],
                                PS[3][0:128, 40:48].bitcast(BF16)), bank=3)
        kb.op("vector", "memset", (xpT[:, dt * 112 + 80: dt * 112 + 96], 0.0),
              waits=[("s", kb.now("s"))], inc="v")
    s_xpT = kb.now("s")
    v_xpT = kb.now("v")

    xpTf = k_f
    for tap in range(4):
        for dt in range(DT):
            kb.op("vector", "scalar_tensor_tensor",
                  (xpTf[:, (tap * DT + dt) * 112:(tap * DT + dt + 1) * 112],
                   xpT[:, dt * 112:(dt + 1) * 112],
                   cw_c[:, dt * 4 + tap: dt * 4 + tap + 1],
                   xpT[:, dt * 112:(dt + 1) * 112], ALU.mult, ALU.bypass),
                  waits=[("s", s_xpT), ("v", v_xpT), ("q", ldq)], inc="v")
    gv_fold = kb.now("v")
    kb.op("sync", "dma_start", (),
          dict(out=xptf_dram, in_=xpTf[:, 0: 4 * DT * 112]),
          waits=[("v", gv_fold)], inc="q", incn=16)
    kb.dma_barrier("q")

    kb.op("vector", "tensor_copy", (cbrow_b[:], cbrow[:]), waits=[("s", s_vec)], inc="v")
    gv_cb = vguard(cbrow_b[:, DT - 1:DT])
    for dt in range(DT):
        mm(2 if dt == 0 else None, PS[2][0:112, 0:1],
           xpT[:, dt * 112:(dt + 1) * 112], cbrow_b[:, dt:dt + 1],
           start=(dt == 0), stop=(dt == DT - 1), waits=[("v", gv_cb), ("s", s_xpT)])
    e_cb = evac("scalar", "copy", (cb96[:, 0:1], PS[2][0:112, 0:1]),
                waits=[("t", kb.now("t"))], bank=2)

    for dt in range(DT):
        tp(3, PS[3][0:64, 0:64].bitcast(BF16), dtw_sb[:, dt * 64:(dt + 1) * 64],
           diag01[:], waits=[("gq", gldq), ("v", gv_id)])
        evac("scalar", "copy", (dtwT[0:64, dt * 128:(dt + 1) * 128],
                                PS[3][0:64, 0:64].bitcast(BF16)),
             waits=[("t", kb.now("t"))], bank=3)

    load(cwq_c[:, :].rearrange("p (dt k) -> p dt k", k=4),
         cwq_d.rearrange("(dt p) k -> p dt k", p=128))
    load(cwk_c[:, :].rearrange("p (dt k) -> p dt k", k=4),
         cwk_d.rearrange("(dt p) k -> p dt k", p=128))
    load(cwv_c[:, :].rearrange("p (dt k) -> p dt k", k=4),
         cwv_d.rearrange("(dt p) k -> p dt k", p=128))
    gload(wb_sb[:, :].rearrange("p (dt h) -> p dt h", h=NH),
          Wb_d.rearrange("(dt p) h -> p dt h", p=128))
    gload(gw2_sb[:, :].rearrange("p (dt h) -> p dt h", h=4),
          gW2_d.rearrange("(dt p) h -> p dt h", p=128))
    load(gb2_c[0:4, 0:1], gb2_d.rearrange("(e o) -> e o", o=1))
    load(fircl[:, 0:504].rearrange("p (h dc k) -> p h dc k", h=NH, k=63),
         firl_d.rearrange("h (dc p) k -> p h dc k", p=128))
    load(fircl[:, 504:528].rearrange("p (h dc k) -> p h dc k", h=NH, k=3),
         firs_d.rearrange("h (dc p) k -> p h dc k", p=128))
    alpha_r = Ht[0:1, 896:904].bitcast(F32)
    cres_r = Ht[0:1, 904:912].bitcast(F32)
    lt_r = Ht[0:4, 912:920].bitcast(F32)
    fp_r = Ht[0:1, 920:952].bitcast(F32)
    onw_r = Ht[0:2, 960:1216].bitcast(F32)
    gb1_r = Ht[0:8, 1216:1472].bitcast(F32)
    load(alpha_r[0:1, :], alpha_d.rearrange("(o h) -> o h", o=1))
    load(cres_r[0:1, :], cres_d.rearrange("(o h) -> o h", o=1))
    load(lt_r[0:4, :], ltemp_d)
    load(fp_r[0:1, :], floor_d.rearrange("h b -> (h b)").rearrange("(o s) -> o s", o=1))
    load(onw_r[0:2, :], onw_d.rearrange("(c p) -> c p", p=128))
    load(gb1_r[0:8, :], gb1_d.rearrange("(dt p) -> dt p", p=128))
    kb.op("vector", "memset", (gw1s_p1[:, :], 0.0))
    kb.op("vector", "memset", (gw1s_p2[:, :], 0.0))
    for b in range(4):
        gload(gw1s_p1[b:b + 1, :], gW1_d[D + b * 4 + 0: D + b * 4 + 1, :])
        gload(gw1s_p1[32 + b:33 + b, :], gW1_d[D + b * 4 + 1: D + b * 4 + 2, :])
        gload(gw1s_p1[64 + b:65 + b, :], gW1_d[D + b * 4 + 2: D + b * 4 + 3, :])
        gload(gw1s_p2[b:b + 1, :], gW1_d[D + b * 4 + 3: D + b * 4 + 4, :])
    ldq2 = kb.now("q")
    gldq2 = kb.now("gq")
    kb.dma_barrier("q")
    kb.dma_barrier("gq")

    kb.op("vector", "memset", (cepsl2[:], 1e-6))
    kb.op("vector", "memset", (cepso[:], 1e-5))
    kb.op("vector", "memset", (S_f[:], 0.0))
    kb.op("vector", "memset", (qtails[:], 0.0))
    kb.op("vector", "memset", (ktails[:], 0.0))
    kb.op("vector", "memset", (vrawtails[:], 0.0))
    for ct in range(DT):
        kb.op("vector", "memset", (v_f[:, ct * (T + 62): ct * (T + 62) + 62], 0.0))
    kb.op("gpsimd", "memset", (mask_sl[:], 1.0))
    kb.op("gpsimd", "affine_select",
          (mask_sl[:], mask_sl[:], [[-1, 128]], ALU.is_gt, 0.0),
          dict(base=0, channel_multiplier=1))
    kb.op("gpsimd", "memset", (mask_ui[:], 1.0))
    kb.op("gpsimd", "affine_select",
          (mask_ui[:], mask_ui[:], [[1, 128]], ALU.is_gt, 0.0),
          dict(base=1, channel_multiplier=-1))
    kb.op("gpsimd", "memset", (E4[:], 1.0))
    for b in range(NH):
        kb.op("gpsimd", "affine_select",
              (E4[0:4, b * 128:(b + 1) * 128], E4[0:4, b * 128:(b + 1) * 128],
               [[0, 128]], ALU.is_equal, 0.0),
              dict(base=-b, channel_multiplier=1))
    for b in range(4):
        for qi, base_row in ((0, 0), (1, 32), (2, 64)):
            sl = selst[:, (b * 3 + qi) * 68:(b * 3 + qi + 1) * 68]
            kb.op("gpsimd", "memset", (sl, 1.0 / 256.0))
            kb.op("gpsimd", "affine_select", (sl, sl, [[1, 68]], ALU.is_equal, 0.0),
                  dict(base=-(base_row + b), channel_multiplier=0))
    kb.op("scalar", "activation", (cres_r[0:1, :], cres_r[0:1, :], AF.Sigmoid),
          waits=[("q", ldq2)])
    mm(1, PS[1][0:128, 0:4], onesrow[0:1, :], alpha_r[0:1, :], start=True, stop=True,
       waits=[("q", ldq2)])
    mm(5, PS[5][0:128, 0:4], onesrow[0:1, :], cres_r[0:1, :], start=True, stop=True,
       waits=[("q", ldq2)])
    evac("scalar", "copy", (acol[:, :], PS[1][0:128, 0:4]), bank=1)
    evac("scalar", "copy", (scres[:, :], PS[5][0:128, 0:4]), bank=5)
    kb.op("scalar", "activation", (srow16[0:1, :], fp_r[0:1, :], AF.Sigmoid),
          waits=[("q", ldq2)])
    kb.op("vector", "tensor_scalar", (srow16[0:1, :], srow16[0:1, :], 0.05, None, ALU.mult))
    kb.op("vector", "tensor_reduce",
          (fsrow4[0:1, 0:4].rearrange("o (h u) -> o h u", u=1),
           srow16[0:1, :].rearrange("o (h b) -> o h b", b=4)),
          dict(axis=AX.X, op=ALU.add))
    kb.op("vector", "tensor_scalar", (fsrow4[0:1, :], fsrow4[0:1, :], -1.0, 1.0,
                                      ALU.mult, ALU.add))
    mm(1, PS[1][0:128, 0:16], onesrow[0:1, :], srow16[0:1, :], start=True, stop=True)
    evac("scalar", "copy", (fl_bc[:, :], PS[1][0:128, 0:16]), bank=1)
    mm(1, PS[1][0:128, 0:4], onesrow[0:1, :], fsrow4[0:1, :], start=True, stop=True)
    evac("scalar", "copy", (fs_bc[:, :], PS[1][0:128, 0:4]), bank=1)
    tp(1, PS[1][0:4, 0:4], lt_r[0:4, 0:4], identf[0:4, 0:4], waits=[("q", ldq2)])
    evac("scalar", "copy", (invt_c[0:4, :], PS[1][0:4, 0:4]), bank=1)
    kb.op("scalar", "activation", (invt_c[0:4, :], invt_c[0:4, :], AF.Exp),
          dict(scale=-1.0))
    tp(1, PS[1][0:128, 0:2], onw_r[0:2, :], identf[0:2, 0:2])
    evac("scalar", "copy", (onw_c[:, :], PS[1][0:128, 0:2]), bank=1)
    tp(1, PS[1][0:128, 0:8], gb1_r[0:8, :], identf[0:8, 0:8])
    evac("scalar", "copy", (gb1_c[:, :], PS[1][0:128, 0:8]), bank=1)
    for ct in range(DT):
        for halfsel in range(2):
            lo, hi = (0, 33) if halfsel == 0 else (33, 66)
            for tau in range(lo, hi):
                col = fircl[:, ct * 63 + tau: ct * 63 + tau + 1] if tau < 63 else                     fircl[:, 504 + ct * 3 + (tau - 63): 504 + ct * 3 + (tau - 63) + 1]
                kb.op("gpsimd", "tensor_tensor",
                      (ew_slot[:, (tau - lo) * 128:(tau - lo + 1) * 128],
                       col.broadcast_to((128, 128)), diag01[:], ALU.mult),
                      waits=[("q", ldq2)])
            kb.op("sync", "dma_start", (),
                  dict(out=fir_dram[ct][:, lo * 128: hi * 128],
                       in_=ew_slot[:, 0:(hi - lo) * 128]),
                  waits=[("g", kb.now("g"))], inc="qf", incn=16)
            kb.dma_barrier("qf")

    def emit_ln(x_in, write_dt, prod=()):
        gs = kb.op("scalar", "copy", (guard[0:128, 0:1], x_in[:, DT * T - 1: DT * T]),
                   waits=prod, inc="s")
        for dt in range(DT):
            mm(2 if dt == 0 else None, PS[2][0:1, 0:T], onescb[:],
               x_in[:, dt * T:(dt + 1) * T],
               start=(dt == 0), stop=(dt == DT - 1), waits=[("s", gs)])
        t_mu = kb.now("t")
        for dt in range(DT):
            kb.op("scalar", "activation", (sqs[:], x_in[:, dt * T:(dt + 1) * T], AF.Square),
                  waits=[("s", gs), ("t", kb.now("t"))], inc="s")
            gq = sguard(sqs[:, T - 1:T])
            mm(6 if dt == 0 else None, PS[6][0:1, 0:T], onescf[:], sqs[:],
               start=(dt == 0), stop=(dt == DT - 1), waits=[("s", gq)])
        t_sq = kb.now("t")
        evac("scalar", "activation", (rowbuf[0:1, 0:T], PS[2][0:1, 0:T], AF.Copy),
             dict(scale=1.0 / D), waits=[("t", t_mu)], bank=2)
        s_musq = kb.op("scalar", "activation",
                       (rowbuf[0:1, T:2 * T], rowbuf[0:1, 0:T], AF.Square), inc="s")
        e_var = evac("vector", "scalar_tensor_tensor",
                     (rowbuf[0:1, 2 * T:3 * T], PS[6][0:1, 0:T], 1.0 / D,
                      rowbuf[0:1, T:2 * T], ALU.mult, ALU.subtract),
                     waits=[("t", t_sq), ("s", s_musq)], bank=6)
        s_sd = kb.op("scalar", "activation",
                     (rowbuf[0:1, T:2 * T], rowbuf[0:1, 2 * T:3 * T], AF.Sqrt),
                     dict(bias=cepsln[0:1, 0:1]), waits=[("v", e_var)], inc="s")
        s_sd = kb.op("scalar", "copy", (guard[0:1, 0:1], rowbuf[0:1, 2 * T - 1:2 * T]),
                     inc="s")
        kb.op("vector", "reciprocal", (rowbuf[0:1, 2 * T:3 * T], rowbuf[0:1, T:2 * T]),
              waits=[("s", s_sd)], inc="v")
        kb.op("vector", "scalar_tensor_tensor",
              (rowbuf[0:1, T:2 * T], rowbuf[0:1, 0:T], -1.0,
               rowbuf[0:1, 2 * T:3 * T], ALU.mult, ALU.mult), inc="v")
        gv_r = vguard(rowbuf[0:1, 3 * T - 1:3 * T])
        mm(2, PS[2][:, 0:T], onesrow[0:1, :], rowbuf[0:1, 2 * T:3 * T],
           start=True, stop=True, waits=[("v", gv_r)])
        mm(6, PS[6][:, 0:T], onesrow[0:1, :], rowbuf[0:1, T:2 * T],
           start=True, stop=True, waits=[("v", gv_r)])
        t_bc = kb.now("t")
        for dt in range(DT):
            write_dt(dt, t_bc)
        v_done = kb.now("v")
        bank_free[2] = ("v", v_done)
        bank_free[6] = ("v", v_done)
        return v_done

    def emit_mamba(x_in, x_out, prod=()):
        def wdt(dt, t_bc):
            kb.op("vector", "tensor_tensor",
                  (lnt1[:], x_in[:, dt * T:(dt + 1) * T], PS[2][:, 0:T], ALU.mult),
                  waits=[("t", t_bc)], inc="v")
            kb.op("vector", "tensor_tensor",
                  (u_f[:, dt * (T + 3) + 3: dt * (T + 3) + 3 + T], lnt1[:],
                   PS[6][:, 0:T], ALU.add), inc="v")
        emit_ln(x_in, wdt, prod=prod)
        gv_u = vguard(u_f[:, DT * (T + 3) - 1: DT * (T + 3)])

        for tap in range(4):
            for dt in range(DT):
                first = (tap == 0 and dt == 0)
                mm(0 if first else None, PS[0][0:112, 0:T],
                   xpTf[:, (tap * DT + dt) * 112:(tap * DT + dt + 1) * 112],
                   u_f[:, dt * (T + 3) + tap: dt * (T + 3) + tap + T],
                   start=first, stop=(tap == 3 and dt == DT - 1),
                   waits=[("v", max(gv_fold, gv_u))])
        t_proj = kb.now("t")
        evac("scalar", "activation", (dtBC[0:64, 0:T], PS[0][0:64, 0:T], AF.Identity),
             dict(bias=cb96[0:64, 0:1]), waits=[("t", t_proj), ("s", e_cb)])
        e_proj = evac("scalar", "activation", (bc48[0:48, 0:T], PS[0][64:112, 0:T],
                                               AF.Identity),
                      dict(bias=cb96[64:112, 0:1]), bank=0)
        gs_proj = sguard(bc48[0:48, T - 1:T])

        for n in range(N):
            mm(2, PS[2][:, 0:T], enB[:, n * 128:(n + 1) * 128], bc48[0:48, 0:T],
               start=True, stop=True, waits=[("s", gs_proj), ("g", g_en)])
            evac("scalar", "copy", (bmcb[:, n * T:(n + 1) * T], PS[2][:, 0:T]),
                 waits=[("t", kb.now("t"))], bank=2)
            mm(6, PS[6][:, 0:T], enC[:, n * 128:(n + 1) * 128], bc48[0:48, 0:T],
               start=True, stop=True, waits=[("s", gs_proj), ("g", g_en)])
            evac("scalar", "copy", (bmcb[:, (N + n) * T:(N + n + 1) * T], PS[6][:, 0:T]),
                 waits=[("t", kb.now("t"))], bank=6)
        s_bc = kb.now("s")

        for dt in range(DT):
            dsl = delta[:, (dt % 2) * T:(dt % 2 + 1) * T]
            mm(5, PS[5][:, 0:T], dtwT[0:64, dt * 128:(dt + 1) * 128], dtBC[0:64, 0:T],
               start=True, stop=True, waits=[("s", gs_proj)])
            t_dt = kb.now("t")
            e_z = evac("scalar", "activation", (zt[:], PS[5][:, 0:T], AF.Identity),
                       dict(bias=dtb_c[:, dt:dt + 1]),
                       waits=[("t", t_dt), ("s", s_vec)], bank=5)
            kb.op("scalar", "activation", (sqs[:], zt[:], AF.Abs), inc="s")
            kb.op("scalar", "activation", (at_[:], sqs[:], AF.Exp), dict(scale=-1.0), inc="s")
            s_l = kb.op("scalar", "activation", (sqs[:], at_[:], AF.Ln),
                        dict(bias=cone[0:128, 0:1]), inc="s")
            kb.op("vector", "scalar_tensor_tensor",
                  (dsl, zt[:], 0.0, sqs[:], ALU.max, ALU.add),
                  waits=[("s", s_l)], inc="v")
            for half in range(2):
                for nn in range(N // 2):
                    n = half * (N // 2) + nn
                    col = dt * N + n
                    dab = dAt[:, 0:T] if n % 2 == 0 else dAt2
                    s_dA = kb.op("scalar", "activation",
                                 (dab, dsl, AF.Exp),
                                 dict(scale=A_c[:, col:col + 1]), inc="s")
                    kb.op("vector", "scalar_tensor_tensor",
                          (ubm[:], u_f[:, dt * (T + 3) + 3: dt * (T + 3) + 3 + T],
                           invA_c[:, col:col + 1], bmcb[:, n * T:(n + 1) * T],
                           ALU.mult, ALU.mult),
                          waits=[("s", s_bc)], inc="v")
                    kb.op("vector", "scalar_tensor_tensor",
                          (bbt[:], dab, -1.0, ubm[:], ALU.add, ALU.mult),
                          waits=[("s", s_dA)], inc="v")
                    kb.op("vector", "tensor_tensor_scan",
                          (Ht[:, nn * T:(nn + 1) * T], dab, bbt[:], Hc[:, col:col + 1],
                           ALU.mult, ALU.add), inc="v")
                    kb.op("vector", "tensor_tensor",
                          (ymt[:, nn * T:(nn + 1) * T], Ht[:, nn * T:(nn + 1) * T],
                           bmcb[:, (N + n) * T:(N + n + 1) * T], ALU.mult), inc="v")
                hlast = bass.AP(Ht, T - 1, [[(N // 2) * T, 128], [T, N // 2]])
                kb.op("vector", "tensor_copy",
                      (Hc[:, dt * N + half * (N // 2): dt * N + (half + 1) * (N // 2)],
                       hlast), inc="v")
                for nn in range(N // 2):
                    mm(4 if (half == 0 and nn == 0) else None, PS[4][:, 0:T], diag01[:],
                       ymt[:, nn * T:(nn + 1) * T],
                       start=(half == 0 and nn == 0), stop=False)
            mm(None, PS[4][:, 0:T], Dsk_dg[:, dt * 128:(dt + 1) * 128],
               u_f[:, dt * (T + 3) + 3: dt * (T + 3) + 3 + T],
               start=False, stop=True, waits=[("v", v_dsk)])
            evac("vector", "tensor_tensor",
                 (x_out[:, dt * T:(dt + 1) * T], x_in[:, dt * T:(dt + 1) * T],
                  PS[4][:, 0:T], ALU.add),
                 waits=[("t", kb.now("t"))], bank=4)
        t_last = kb.now("t")
        for dt in range(DT):
            kb.op("vector", "tensor_copy",
                  (u_f[:, dt * (T + 3): dt * (T + 3) + 3],
                   u_f[:, dt * (T + 3) + T: dt * (T + 3) + 3 + T]),
                  waits=[("t", t_last)] if dt == 0 else (), inc="v")
        return kb.now("s")

    def emit_ffn(x_in, x_out, ti, prod=()):
        def wdt(dt, t_bc):
            kb.op("vector", "tensor_tensor",
                  (lnt1[:], x_in[:, dt * T:(dt + 1) * T], PS[2][:, 0:T], ALU.mult),
                  waits=[("t", t_bc)], inc="v")
            kb.op("vector", "tensor_tensor", (xnf[:], lnt1[:], PS[6][:, 0:T], ALU.add),
                  waits=[("t", kb.now("t"))], inc="v")
            kb.op("vector", "tensor_copy",
                  (xnb[:, dt * T:(dt + 1) * T], xnf[:]), inc="v")
            gvx = vguard(xnf[:, T - 1:T])
            mm(1 if dt == 0 else None, PS[1][0:8, 0:T],
               rw_sb[:, dt * NEXP:(dt + 1) * NEXP], xnf[:],
               start=(dt == 0), stop=(dt == DT - 1), waits=[("v", gvx)])
        emit_ln(x_in, wdt, prod=prod)
        gv_xnb = vguard(xnb[:, DT * T - 1: DT * T])
        t_rt = kb.now("t")
        e_lg = evac("scalar", "activation", (lgrow[0:8, 0:T], PS[1][0:8, 0:T], AF.Identity),
                    dict(bias=rb_c[0:8, 0:1]), waits=[("t", t_rt), ("q", ldq)], bank=1)
        gs_lg = sguard(lgrow[0:8, T - 1:T])
        for pt in range(4):
            tp(1 if pt == 0 else None, PS[1][0:128, pt * 8:(pt + 1) * 8],
               lgrow[0:8, pt * 128:(pt + 1) * 128], identf[0:8, 0:8],
               waits=[("s", gs_lg)])
        t_lgt = kb.now("t")
        e_lgtm = evac("vector", "tensor_copy", (lgtm[:, :], PS[1][0:128, 0:32]),
                      waits=[("t", t_lgt)], bank=1)
        for pt in range(4):
            kb.op("vector", "reduce_max", (rc16[:, pt:pt + 1], lgtm[:, pt * 8:(pt + 1) * 8]),
                  dict(axis=AX.X), waits=[("v", e_lgtm)], inc="v")
            kb.op("vector", "tensor_scalar", (rc16[:, 4 + pt:5 + pt], rc16[:, pt:pt + 1],
                                              -1.0, None, ALU.mult), inc="v")
        v_nm = vguard(rc16[:, 7:8])
        for pt in range(4):
            kb.op("scalar", "activation", (mrow4[:, pt * 8:(pt + 1) * 8],
                                           lgtm[:, pt * 8:(pt + 1) * 8], AF.Exp),
                  dict(bias=rc16[:, 4 + pt:5 + pt]), waits=[("v", v_nm)], inc="s")
        for pt in range(4):
            kb.op("scalar", "activation", (srow4[:, pt * 8:(pt + 1) * 8],
                                           lgtm[:, pt * 8:(pt + 1) * 8], AF.Sign),
                  dict(bias=rc16[:, 4 + pt:5 + pt]), inc="s")
        s_es = sguard(srow4[:, 31:32])
        for pt in range(4):
            kb.op("vector", "reduce_sum", (rc16[:, 8 + pt:9 + pt],
                                           mrow4[:, pt * 8:(pt + 1) * 8]),
                  dict(axis=AX.X), waits=[("s", s_es)], inc="v")
            kb.op("vector", "reciprocal", (rc16[:, 12 + pt:13 + pt],
                                           rc16[:, 8 + pt:9 + pt]), inc="v")
            kb.op("vector", "tensor_scalar", (srow4[:, pt * 8:(pt + 1) * 8],
                                              srow4[:, pt * 8:(pt + 1) * 8], 1.0, None,
                                              ALU.add), inc="v")
            kb.op("vector", "tensor_scalar", (mtmb[:, pt * 8:(pt + 1) * 8],
                                              srow4[:, pt * 8:(pt + 1) * 8],
                                              rc16[:, 12 + pt:13 + pt], None, ALU.mult),
                  inc="v")
        gv_m = vguard(mtmb[:, 31:32])
        for pt in range(4):
            tp(1, PS[1][0:8, pt * 64:(pt + 1) * 64].bitcast(BF16),
               mtmb[:, pt * 8:(pt + 1) * 8], diag01[:], waits=[("v", gv_m)])
        t_mtp = kb.now("t")
        for pt in range(4):
            evac("scalar", "copy", (m_fm[0:8, pt * 128:(pt + 1) * 128],
                                    PS[1][0:8, pt * 64:(pt + 1) * 64].bitcast(BF16)),
                 waits=[("t", t_mtp)], bank=1)
        gs_mfm = sguard(m_fm[0:8, T - 1:T])

        for e in range(NEXP):
            mm(1, PS[1][:, 0:T], E8[:, e * 128:(e + 1) * 128], m_fm[0:8, 0:T],
               start=True, stop=True, waits=[("s", gs_mfm), ("g", g_en)])
            evac("scalar", "copy", (mbc_all[:, e * T:(e + 1) * T], PS[1][:, 0:T]),
                 waits=[("t", kb.now("t"))], bank=1)
        gs_mbc = sguard(mbc_all[:, NEXP * T - 1:NEXP * T])
        slot_consumed = [0, 0]

        def eload(e):
            slot = ew_slot if e % 2 == 0 else eslot2
            for i in range(DT):
                for ph in range(2):
                    gload(slot[ph * 64:(ph + 1) * 64, i * DT * 128:(i + 1) * DT * 128],
                          ew_d[e][i * 128 + ph * 64: i * 128 + (ph + 1) * 64, :],
                          waits=([("g", g_en)] if i == 0 and ph == 0 else ()),
                          sem="qe")
            kb.dma_barrier("qe")
            return kb.now("qe"), slot

        elq = {0: eload(0), 1: eload(1)}
        for e in range(NEXP):
            lqe, slot = elq.pop(e)
            for dt in range(DT):
                sl = dt % 2
                wv = [("s", gs_mbc), ("v", gv_xnb), ("t", slot_consumed[sl])]
                kb.op("vector", "tensor_tensor",
                      (xeb[:, sl * T:(sl + 1) * T], xnb[:, dt * T:(dt + 1) * T],
                       mbc_all[:, e * T:(e + 1) * T], ALU.mult), waits=wv, inc="v")
                gv_xe = vguard(xeb[:, (sl + 1) * T - 1:(sl + 1) * T])
                for ct in range(DT):
                    mm(ct if (e == 0 and dt == 0) else None, PS[ct][:, 0:T],
                       slot[:, (dt * DT + ct) * 128:(dt * DT + ct + 1) * 128],
                       xeb[:, sl * T:(sl + 1) * T],
                       start=(e == 0 and dt == 0), stop=False,
                       waits=[("v", gv_xe), ("qe", lqe)])
                slot_consumed[sl] = kb.now("t")
            if e + 2 < NEXP:
                elq[e + 2] = eload(e + 2)
        for ct in range(DT):
            mm(None, PS[ct][:, 0:T], eb_sb[0:8, ct * 128:(ct + 1) * 128], m_fm[0:8, 0:T],
               start=False, stop=True, waits=[("gq", gldq)])
            evac("vector", "tensor_tensor",
                 (x_out[:, ct * T:(ct + 1) * T], x_in[:, ct * T:(ct + 1) * T],
                  PS[ct][:, 0:T], ALU.add),
                 waits=[("t", kb.now("t"))], bank=ct)
        return kb.now("s")


    hsb = xnb
    wslot = ew_slot
    fir_slot = bmcb[:, 0:8448]
    pb_bc = bmcb[:, 8448:10496]
    q_f = bmcb[:, 10496:14616]
    hsW = bmcb[:, 0:4096]
    gelu_sb = bmcb[:, 4096:8192]
    do_f = x1f
    ls_f = ymt[:, 0:(N // 2) * T]
    ll_f = Ht[:, 0:(N // 2) * T]
    st68 = lnt1
    Xb = zt[:, 0:256].bitcast(BF16)
    uadj_sb = zt[:, 256:384].bitcast(BF16)
    kTM = zt[:, 384:512].bitcast(BF16)
    A_sb = at_[:, 0:64].bitcast(BF16)
    AT_sb = at_[:, 64:128].bitcast(BF16)
    Mp = [at_[:, 128:192].bitcast(BF16), at_[:, 192:256].bitcast(BF16)]
    MpT = [at_[:, 256:320].bitcast(BF16), at_[:, 320:384].bitcast(BF16)]
    wTn = at_[:, 384:512].bitcast(BF16)
    Sb_s = dAt[:, 0:T]
    attnT = ubm[:, 0:128]
    bT_c = rc16
    stat_mu = xnf
    lgt4 = lgtm

    def emit_attnb(x_in, x_out, ti):
        def wdt(dt, t_bc):
            kb.op("vector", "tensor_tensor",
                  (xnf[:], x_in[:, dt * T:(dt + 1) * T], PS[2][:, 0:T], ALU.mult),
                  waits=[("t", t_bc)], inc="v")
            kb.op("vector", "tensor_tensor",
                  (hsb[:, dt * T:(dt + 1) * T], xnf[:], PS[6][:, 0:T], ALU.add), inc="v")
        emit_ln(x_in, wdt, prod=[("s", kb.now("s"))])

        for ct in range(DT):
            kb.op("vector", "tensor_copy",
                  (q_f[:, ct * (T + 3): ct * (T + 3) + 3],
                   qtails[:, ct * 3:(ct + 1) * 3]), inc="v")
            kb.op("vector", "tensor_copy",
                  (k_f[:, ct * (T + 3): ct * (T + 3) + 3],
                   ktails[:, ct * 3:(ct + 1) * 3]), inc="v")

        def wload(W_d, slot):
            for i in range(DT):
                for ph in range(2):
                    gload(slot[ph * 64:(ph + 1) * 64, i * D:(i + 1) * D],
                          W_d[i * 128 + ph * 64: i * 128 + (ph + 1) * 64, :],
                          sem="qb")
            kb.dma_barrier("qb")
            return kb.now("qb")

        def proj_conv(lw, slot, cw_col, dst, span, off, tails, do_silu=True):
            for ct in range(DT):
                for dt in range(DT):
                    mm(0 if dt == 0 else None, PS[0][:, 0:T],
                       slot[:, dt * D + ct * 128: dt * D + (ct + 1) * 128],
                       hsb[:, dt * T:(dt + 1) * T],
                       start=(dt == 0), stop=(dt == DT - 1), waits=[("qb", lw)])
                base = ct * span + off
                evac("scalar", "copy", (dst[:, base: base + T], PS[0][:, 0:T]),
                     waits=[("t", kb.now("t"))], bank=0)
                if off == 62:
                    kb.op("vector", "tensor_copy",
                          (cvscr[:, 0:3], tails[:, ct * 3:(ct + 1) * 3]), inc="v")
                    kb.op("vector", "tensor_copy",
                          (cvscr[:, 3:6], dst[:, base: base + 3]), inc="v")
                kb.op("vector", "tensor_copy",
                      (tails[:, ct * 3:(ct + 1) * 3],
                       dst[:, base + T - 3: base + T]), inc="v")
                for i in range(4):
                    if i == 0:
                        kb.op("vector", "scalar_tensor_tensor",
                              (sqs[:], dst[:, base - 3 + i: base - 3 + i + T],
                               cw_col[:, ct * 4 + i: ct * 4 + i + 1],
                               dst[:, base - 3 + i: base - 3 + i + T],
                               ALU.mult, ALU.bypass), inc="v")
                    else:
                        kb.op("vector", "scalar_tensor_tensor",
                              (sqs[:], dst[:, base - 3 + i: base - 3 + i + T],
                               cw_col[:, ct * 4 + i: ct * 4 + i + 1],
                               sqs[:], ALU.mult, ALU.add), inc="v")
                if off == 62:
                    for i in range(4):
                        kb.op("vector", "scalar_tensor_tensor",
                              (sqs[:, 0:3], cvscr[:, i:i + 3],
                               cw_col[:, ct * 4 + i: ct * 4 + i + 1],
                               cvscr[:, i:i + 3] if i == 0 else sqs[:, 0:3],
                               ALU.mult, ALU.bypass if i == 0 else ALU.add),
                              inc="v")
                if do_silu:
                    kb.op("scalar", "activation", (dst[:, base: base + T], sqs[:],
                                                   AF.Silu), inc="s")
                else:
                    kb.op("vector", "tensor_copy", (dst[:, base: base + T], sqs[:]),
                          inc="v")

        pslot2 = bmcb[:, 0:DT * D]
        lq_w = wload(Wq_d, wslot)
        lk_w = wload(Wk_d, pslot2)
        proj_conv(lq_w, wslot, cwq_c, q_f, T + 3, 3, qtails)
        lv_w = wload(Wv_d, wslot)
        proj_conv(lk_w, pslot2, cwk_c, k_f, T + 3, 3, ktails)
        proj_conv(lv_w, wslot, cwv_c, v_f, T + 62, 62, vrawtails)

        for dt in range(DT):
            mm(1 if dt == 0 else None, PS[1][0:4, 0:T],
               wb_sb[:, dt * 4:(dt + 1) * 4], hsb[:, dt * T:(dt + 1) * T],
               start=(dt == 0), stop=(dt == DT - 1), waits=[("gq", gldq2)])
        evac("scalar", "activation", (lgrow[0:4, 0:T], PS[1][0:4, 0:T], AF.Sigmoid),
             waits=[("t", kb.now("t"))], bank=1)
        for c in range(4):
            tp(1, PS[1][0:128, 0:4], lgrow[0:4, c * 128:(c + 1) * 128],
               identf[0:4, 0:4])
            evac("scalar", "copy", (bT_c[:, c * 4:(c + 1) * 4], PS[1][0:128, 0:4]),
                 waits=[("t", kb.now("t"))], bank=1)
        kb.op("vector", "tensor_scalar", (mrow4[:, 0:16], bT_c[:, 0:16], -1.0, None,
                                          ALU.mult), inc="v")
        negbT = mrow4

        for th, dst in ((0, q_f), (1, k_f)):
            for h in range(NH):
                for half in range(2):
                    ct = 2 * h + half
                    kb.op("scalar", "activation",
                          (sqs[:], dst[:, ct * (T + 3) + 3: ct * (T + 3) + 3 + T],
                           AF.Square), inc="s")
                    mm(2 if half == 0 else None, PS[2][0:1, 0:T], onescf[:], sqs[:],
                       start=(half == 0), stop=(half == 1))
                e_r = evac("scalar", "activation",
                           (rowbuf[0:1, 0:T], PS[2][0:1, 0:T], AF.Sqrt),
                           dict(bias=cepsl2[0:1, 0:1]),
                           waits=[("t", kb.now("t"))], bank=2)
                kb.op("vector", "reciprocal", (rowbuf[0:1, T:2 * T], rowbuf[0:1, 0:T]),
                      inc="v")
                mm(2, PS[2][:, 0:T], onesrow[0:1, :], rowbuf[0:1, T:2 * T],
                   start=True, stop=True)
                e_bc = evac("scalar", "copy", (sqs[:], PS[2][:, 0:T]),
                            waits=[("t", kb.now("t"))], bank=2)
                for half in range(2):
                    ct = 2 * h + half
                    sl = dst[:, ct * (T + 3) + 3: ct * (T + 3) + 3 + T]
                    kb.op("vector", "tensor_tensor", (sl, sl, sqs[:], ALU.mult),
                          inc="v")

        def fload(ct, half):
            base = half * 33 * 128
            for i in range(4):
                for ph in range(2):
                    load(fir_slot[ph * 64:(ph + 1) * 64,
                                  base + i * 1056: base + (i + 1) * 1056],
                         fir_dram[ct][ph * 64:(ph + 1) * 64,
                                      base + i * 1056: base + (i + 1) * 1056],
                         sem="qf")
            kb.dma_barrier("qf")
            return kb.now("qf")

        lfh = {(0, 0): fload(0, 0), (0, 1): fload(0, 1)}
        for ct in range(DT):
            base = ct * (T + 62)
            for tau in range(33):
                mm(7 if tau == 0 else None, PS[7][:, 0:T],
                   fir_slot[:, tau * 128:(tau + 1) * 128],
                   v_f[:, base + tau: base + tau + T],
                   start=(tau == 0), stop=False, waits=[("qf", lfh[(ct, 0)])])
            if ct + 1 < DT:
                lfh[(ct + 1, 0)] = fload(ct + 1, 0)
            for tau in range(33, 63):
                mm(None, PS[7][:, 0:T],
                   fir_slot[:, tau * 128:(tau + 1) * 128],
                   v_f[:, base + tau: base + tau + T],
                   start=False, stop=(tau == 62), waits=[("qf", lfh[(ct, 1)])])
            evac("scalar", "copy", (ll_f[:, ct * T:(ct + 1) * T], PS[7][:, 0:T]),
                 waits=[("t", kb.now("t"))], bank=7)
            for i in range(3):
                mm(7 if i == 0 else None, PS[7][:, 0:T],
                   fir_slot[:, (63 + i) * 128:(64 + i) * 128],
                   v_f[:, base + 60 + i: base + 60 + i + T],
                   start=(i == 0), stop=(i == 2), waits=[("qf", lfh[(ct, 1)])])
            evac("scalar", "copy", (ls_f[:, ct * T:(ct + 1) * T], PS[7][:, 0:T]),
                 waits=[("t", kb.now("t"))], bank=7)
            if ct + 1 < DT:
                lfh[(ct + 1, 1)] = fload(ct + 1, 1)

        for i in range(DT):
            for ph in range(2):
                gload(wslot[ph * 64:(ph + 1) * 64, i * D:(i + 1) * D],
                  gW1_d[i * 128 + ph * 64: i * 128 + (ph + 1) * 64, :],
                  sem="qb")
        kb.dma_barrier("qb")
        lw1 = kb.now("qb")
        for ct in range(DT):
            for dt in range(DT):
                mm(0 if dt == 0 else None, PS[0][:, 0:T],
                   wslot[:, dt * D + ct * 128: dt * D + (ct + 1) * 128],
                   hsb[:, dt * T:(dt + 1) * T],
                   start=(dt == 0), stop=(dt == DT - 1), waits=[("qb", lw1)])
            evac("scalar", "copy", (hsW[:, ct * T:(ct + 1) * T], PS[0][:, 0:T]),
                 waits=[("t", kb.now("t"))], bank=0)
        for i in range(DT):
            for ph in range(2):
                gload(wslot[ph * 64:(ph + 1) * 64, i * D:(i + 1) * D],
                  Wid_d[i * 128 + ph * 64: i * 128 + (ph + 1) * 64, :],
                  sem="qb")
        kb.dma_barrier("qb")
        lid = kb.now("qb")
        wslot2 = wslot
        dscr = delta[:, :].bitcast(BF16)
        obuf1 = bbt
        obuf2 = dscr[:, 2 * T:3 * T]
        stb = dscr[:, 3 * T:4 * T]
        l2b = dscr[0:4, T:2 * T]
        for h in range(NH):
            kb.op("vector", "tensor_copy", (Sb_s[:, 0:T], S_f[:, h * 512: h * 512 + 512]),
                  inc="v")
            for c in range(4):
                c0 = c * 128
                kcol = lambda half: k_f[:, (2 * h + half) * (T + 3) + 3 + c0:
                                        (2 * h + half) * (T + 3) + 3 + c0 + 128]
                qcol = lambda half: q_f[:, (2 * h + half) * (T + 3) + 3 + c0:
                                        (2 * h + half) * (T + 3) + 3 + c0 + 128]
                vcol = lambda half: v_f[:, (2 * h + half) * (T + 62) + 62 + c0:
                                        (2 * h + half) * (T + 62) + 62 + c0 + 128]
                bcol = bT_c[:, c * 4 + h: c * 4 + h + 1]
                nbcol = negbT[:, c * 4 + h: c * 4 + h + 1]
                for half in range(2):
                    mm(1 if half == 0 else None, PS[1][:, 0:128], kcol(half), kcol(half),
                       start=(half == 0), stop=(half == 1))
                evac("vector", "scalar_tensor_tensor",
                     (A_sb[:, :], PS[1][:, 0:128], nbcol, mask_sl[:], ALU.mult,
                      ALU.mult), waits=[("t", kb.now("t"))], bank=1)
                for half in range(2):
                    mm(1 if half == 0 else None, PS[1][:, 0:128], kcol(half), qcol(half),
                       start=(half == 0), stop=(half == 1))
                evac("vector", "tensor_tensor", (attnT[:, :], PS[1][:, 0:128],
                                                 mask_ui[:], ALU.mult),
                     waits=[("t", kb.now("t"))], bank=1)
                for half in range(2):
                    tp(1, PS[1][0:128, half * 64:(half + 1) * 64].bitcast(BF16),
                       kcol(half), diag01[:])
                t_ktp = kb.now("t")
                evac("scalar", "copy", (kTM[:, 0:256], PS[1][0:128, 0:128].bitcast(BF16)),
                     waits=[("t", t_ktp)], bank=1)
                kb.op("vector", "scalar_tensor_tensor",
                      (Xb[:, 256:512], kTM[:, 0:256], bcol, kTM[:, 0:256],
                       ALU.mult, ALU.bypass), inc="v")
                for half in range(2):
                    tp(1, PS[1][0:128, half * 64:(half + 1) * 64].bitcast(BF16),
                       vcol(half), diag01[:])
                t_vtp = kb.now("t")
                evac("vector", "scalar_tensor_tensor",
                     (Xb[:, 0:256], PS[1][0:128, 0:128].bitcast(BF16), bcol,
                      kTM[:, 0:256], ALU.mult, ALU.bypass),
                     waits=[("t", t_vtp)], bank=1)
                tp(1, PS[1][0:128, 0:64].bitcast(BF16), A_sb[:, :], diag01[:])
                evac("scalar", "copy", (AT_sb[:, :], PS[1][0:128, 0:64].bitcast(BF16)),
                     waits=[("t", kb.now("t"))], bank=1)
                if STAGE == "dbgb" and ti == 0 and h == 0 and c == 0:
                    kb.op("gpsimd", "dma_start", (),
                          dict(out=y_d[776:904, 0:512], in_=Xb[:, 0:512]),
                          waits=[("v", kb.now("v")), ("s", kb.now("s"))],
                          inc="store", incn=16)
                    kb.op("gpsimd", "dma_start", (),
                          dict(out=y_d[904:1032, 0:128], in_=A_sb[:, :]),
                          inc="store", incn=16)
                    kb.op("gpsimd", "dma_start", (),
                          dict(out=y_d[1032:1160, 0:128], in_=attnT[:, :]),
                          inc="store", incn=16)
                    kb.dma_barrier("store")
                    kb.op("vector", "nop", (), waits=[("store", kb.now("store"))], inc="v")
                    kb.op("scalar", "nop", (), waits=[("store", kb.now("store"))], inc="s")
                cur, nxt = 0, 1
                kb.op("vector", "tensor_copy", (Mp[0][:, :], A_sb[:, :]), inc="v")
                kb.op("vector", "tensor_copy", (MpT[0][:, :], AT_sb[:, :]), inc="v")
                for j in range(7):
                    mm(4, PS[4][:, 0:T], MpT[cur][:, :], Xb[:, 0:512],
                       start=True, stop=True)
                    evac("vector", "tensor_tensor",
                         (Xb[:, 0:512], Xb[:, 0:512], PS[4][:, 0:T], ALU.add),
                         waits=[("t", kb.now("t"))], bank=4)
                    if j < 6:
                        mm(1, PS[1][:, 0:128], MpT[cur][:, :],
                           Mp[cur][:, :], start=True, stop=True)
                        mm(5, PS[5][:, 0:128], Mp[cur][:, :],
                           MpT[cur][:, :], start=True, stop=True)
                        evac("scalar", "copy", (Mp[nxt][:, :], PS[1][:, 0:128]),
                             waits=[("t", kb.now("t"))], bank=1)
                        evac("scalar", "copy", (MpT[nxt][:, :], PS[5][:, 0:128]),
                             bank=5)
                        cur, nxt = nxt, cur
                for half in range(2):
                    tp(1, PS[1][0:128, half * 64:(half + 1) * 64].bitcast(BF16),
                       Xb[:, 256 + half * 128: 256 + (half + 1) * 128], diag01[:])
                evac("vector", "tensor_scalar",
                     (wTn[:, 0:256], PS[1][0:128, 0:128].bitcast(BF16), -1.0, None,
                      ALU.mult), waits=[("t", kb.now("t"))], bank=1)
                for half in range(2):
                    mm(5 if half == 0 else None, PS[5][:, 0:256],
                       wTn[:, half * 128:(half + 1) * 128],
                       Sb_s[:, half * 256:(half + 1) * 256],
                       start=(half == 0), stop=(half == 1))
                evac("vector", "tensor_tensor",
                     (uadj_sb[:, 0:256], Xb[:, 0:256], PS[5][:, 0:256], ALU.add),
                     waits=[("t", kb.now("t"))], bank=5)
                if STAGE == "dbgb" and ti == 0 and h == 0 and c == 0:
                    kb.op("gpsimd", "dma_start", (),
                          dict(out=y_d[1160:1288, 0:512], in_=Xb[:, 0:512]),
                          waits=[("v", kb.now("v")), ("s", kb.now("s"))],
                          inc="store", incn=16)
                    kb.op("gpsimd", "dma_start", (),
                          dict(out=y_d[1288:1416, 0:256], in_=uadj_sb[:, 0:256]),
                          inc="store", incn=16)
                    kb.dma_barrier("store")
                    kb.op("vector", "nop", (), waits=[("store", kb.now("store"))], inc="v")
                    kb.op("scalar", "nop", (), waits=[("store", kb.now("store"))], inc="s")
                for half in range(2):
                    mm(5, PS[5][:, 0:256], kTM[:, half * 128:(half + 1) * 128],
                       uadj_sb[:, 0:256], start=True, stop=True)
                    kb.op("vector", "tensor_tensor",
                          (S_f[:, h * 512 + half * 256: h * 512 + (half + 1) * 256],
                           S_f[:, h * 512 + half * 256: h * 512 + (half + 1) * 256],
                           PS[5][:, 0:256], ALU.add),
                          waits=[("t", kb.now("t"))], inc="v")
                    bank_free[5] = ("v", kb.now("v"))
                for half in range(2):
                    mm(4 if half == 0 else None, PS[4][:, 0:256], qcol(half),
                       Sb_s[:, half * 256:(half + 1) * 256],
                       start=(half == 0), stop=False)
                mm(None, PS[4][:, 0:256], attnT[:, :], uadj_sb[:, 0:256],
                   start=False, stop=True)
                evac("scalar", "copy", (kTM[:, 0:256], PS[4][:, 0:256]),
                     waits=[("t", kb.now("t"))], bank=4)
                for half in range(2):
                    tp(1, PS[1][0:128, half * 64:(half + 1) * 64].bitcast(BF16),
                       kTM[:, half * 128:(half + 1) * 128], diag01[:])
                t_otp = kb.now("t")
                for half in range(2):
                    evac("scalar", "copy",
                         (do_f[:, (2 * h + half) * T + c0: (2 * h + half) * T + c0 + 128],
                          PS[1][0:128, half * 64:(half + 1) * 64].bitcast(BF16)),
                         waits=[("t", t_otp)], bank=(1 if half == 1 else None))
                kb.op("vector", "tensor_copy", (Sb_s[:, 0:T],
                                                S_f[:, h * 512: h * 512 + 512]), inc="v")


            first = True
            for bi, brt in ((0, ls_f), (1, ll_f), (2, do_f), (3, None)):
                for half in range(2):
                    ct = 2 * h + half
                    if brt is None:
                        xs = v_f[:, ct * (T + 62) + 62: ct * (T + 62) + 62 + T]
                    else:
                        xs = brt[:, ct * T:(ct + 1) * T]
                    mm(6 if first else None, PS[6][0:68, 0:T],
                       selst[:, (bi * 3 + 0) * 68:(bi * 3 + 1) * 68], xs,
                       start=first, stop=False)
                    first = False
                    sqb = dscr[:, 0:T]
                    kb.op("scalar", "activation", (sqb, xs, AF.Square), inc="s")
                    mm(None, PS[6][0:68, 0:T],
                       selst[:, (bi * 3 + 1) * 68:(bi * 3 + 2) * 68], sqb,
                       start=False, stop=False)
                    kb.op("scalar", "activation", (sqb, xs, AF.Abs), inc="s")
                    last = (bi == 3 and half == 1)
                    mm(None, PS[6][0:68, 0:T],
                       selst[:, (bi * 3 + 2) * 68:(bi * 3 + 3) * 68], sqb,
                       start=False, stop=last)
            evac("scalar", "copy", (st68[0:68, 0:T], PS[6][0:68, 0:T]),
                 waits=[("t", kb.now("t"))], bank=6)
            kb.op("scalar", "activation", (stat_mu[32:36, 0:T], st68[32:36, 0:T],
                                           AF.Sqrt), dict(scale=256.0), inc="s")
            kb.op("vector", "tensor_copy", (l2b, stat_mu[32:36, 0:T]), inc="v")
            kb.op("scalar", "activation", (stat_mu[32:36, 0:T], st68[0:4, 0:T],
                                           AF.Square), inc="s")
            kb.op("vector", "tensor_sub", (st68[32:36, 0:T], st68[32:36, 0:T],
                                           stat_mu[32:36, 0:T]), inc="v")
            kb.op("vector", "tensor_copy", (stb[:, 0:T], st68[:, 0:T]), inc="v")
            for ct in range(DT):
                mm(0, PS[0][:, 0:T], diag01[:], hsW[:, ct * T:(ct + 1) * T],
                   start=True, stop=False)
                mm(None, PS[0][:, 0:T], gw1s_p1[:, ct * 128:(ct + 1) * 128],
                   stb[:, 0:T], start=False, stop=False)
                mm(None, PS[0][:, 0:T], gw1s_p2[:, ct * 128:(ct + 1) * 128],
                   l2b, start=False, stop=True)
                evac("scalar", "activation", (gelu_sb[:, ct * T:(ct + 1) * T],
                                              PS[0][:, 0:T], AF.Gelu),
                     dict(bias=gb1_c[:, ct:ct + 1]),
                     waits=[("t", kb.now("t"))], bank=0)
            for ct in range(DT):
                mm(3 if ct == 0 else None, PS[3][0:4, 0:T],
                   gw2_sb[:, ct * 4:(ct + 1) * 4], gelu_sb[:, ct * T:(ct + 1) * T],
                   start=(ct == 0), stop=(ct == DT - 1), waits=[("gq", gldq2)])
            evac("scalar", "activation", (lgrow[0:4, 0:T], PS[3][0:4, 0:T],
                                          AF.Identity), dict(bias=gb2_c[0:4, 0:1]),
                 waits=[("t", kb.now("t"))], bank=3)
            kb.op("vector", "tensor_scalar", (lgrow[0:4, 0:T], lgrow[0:4, 0:T],
                                              invt_c[0:4, h:h + 1], None, ALU.mult),
                  inc="v")
            for c in range(4):
                tp(3, PS[3][0:128, c * 4:(c + 1) * 4],
                   lgrow[0:4, c * 128:(c + 1) * 128], identf[0:4, 0:4])
            evac("vector", "tensor_copy", (lgt4[:, 0:16], PS[3][0:128, 0:16]),
                 waits=[("t", kb.now("t"))], bank=3)
            for c in range(4):
                sl = lgt4[:, c * 4:(c + 1) * 4]
                kb.op("vector", "reduce_max", (rcol[:, 0:1], sl), dict(axis=AX.X),
                      inc="v")
                kb.op("vector", "tensor_scalar", (rcol[:, 1:2], rcol[:, 0:1], -1.0,
                                                  None, ALU.mult), inc="v")
                kb.op("scalar", "activation", (sl, sl, AF.Exp), dict(bias=rcol[:, 1:2]),
                      inc="s")
                kb.op("vector", "reduce_sum", (rcol[:, 2:3], sl), dict(axis=AX.X),
                      inc="v")
                kb.op("vector", "reciprocal", (rcol[:, 3:4], rcol[:, 2:3]), inc="v")
                kb.op("vector", "tensor_scalar", (sl, sl, rcol[:, 3:4], None,
                                                  ALU.mult), inc="v")
                kb.op("vector", "tensor_scalar", (sl, sl, fs_bc[:, h:h + 1], None,
                                                  ALU.mult), inc="v")
                kb.op("vector", "tensor_add", (sl, sl, fl_bc[:, h * 4:(h + 1) * 4]),
                      inc="v")
            kb.op("vector", "tensor_copy", (mtmb[:, 0:16], lgt4[:, 0:16]), inc="v")
            for c in range(4):
                tp(3, PS[3][0:4, c * 64:(c + 1) * 64].bitcast(BF16),
                   mtmb[:, c * 4:(c + 1) * 4], diag01[:])
            t_ptp = kb.now("t")
            evac("scalar", "copy", (m_fm[0:4, 0:T],
                                    PS[3][0:4, 0:256].bitcast(BF16)),
                 waits=[("t", t_ptp)], bank=3)
            for b in range(4):
                mm(3, PS[3][:, 0:T], E4[:, b * 128:(b + 1) * 128], m_fm[0:4, 0:T],
                   start=True, stop=True)
                evac("scalar", "copy", (pb_bc[:, b * T:(b + 1) * T], PS[3][:, 0:T]),
                     waits=[("t", kb.now("t"))], bank=3)
            for half in range(2):
                ct = 2 * h + half
                for dt in range(DT):
                    mm(7 if dt == 0 else None, PS[7][:, 0:T],
                       wslot2[:, dt * D + ct * 128: dt * D + (ct + 1) * 128],
                       hsb[:, dt * T:(dt + 1) * T],
                       start=(dt == 0), stop=(dt == DT - 1), waits=[("qb", lid)])
                vsl = v_f[:, ct * (T + 62) + 62: ct * (T + 62) + 62 + T]
                kb.op("vector", "tensor_tensor", (obuf1[:, 0:T],
                      ls_f[:, ct * T:(ct + 1) * T], pb_bc[:, 0:T], ALU.mult), inc="v")
                kb.op("vector", "tensor_tensor", (obuf2[:, 0:T],
                      ll_f[:, ct * T:(ct + 1) * T], pb_bc[:, T:2 * T], ALU.mult),
                      inc="v")
                kb.op("vector", "tensor_add", (obuf1[:, 0:T], obuf1[:, 0:T],
                                               obuf2[:, 0:T]), inc="v")
                kb.op("vector", "tensor_tensor", (obuf2[:, 0:T],
                      do_f[:, ct * T:(ct + 1) * T], pb_bc[:, 2 * T:3 * T], ALU.mult),
                      inc="v")
                kb.op("vector", "tensor_add", (obuf1[:, 0:T], obuf1[:, 0:T],
                                               obuf2[:, 0:T]), inc="v")
                kb.op("vector", "tensor_tensor", (obuf2[:, 0:T], vsl,
                      pb_bc[:, 3 * T:4 * T], ALU.mult), inc="v")
                kb.op("vector", "tensor_add", (obuf1[:, 0:T], obuf1[:, 0:T],
                                               obuf2[:, 0:T]), inc="v")
                kb.op("vector", "scalar_tensor_tensor",
                      (obuf1[:, 0:T], ls_f[:, ct * T:(ct + 1) * T],
                       scres[:, h:h + 1], obuf1[:, 0:T], ALU.mult, ALU.add), inc="v")
                kb.op("vector", "scalar_tensor_tensor",
                      (do_f[:, ct * T:(ct + 1) * T], PS[7][:, 0:T],
                       acol[:, h:h + 1], obuf1[:, 0:T], ALU.mult, ALU.add),
                      waits=[("t", kb.now("t"))], inc="v")
                bank_free[7] = ("v", kb.now("v"))
            for half in range(2):
                ct = 2 * h + half
                kb.op("scalar", "activation", (sqs[:], do_f[:, ct * T:(ct + 1) * T],
                                               AF.Square), inc="s")
                mm(6 if half == 0 else None, PS[6][0:1, 0:T], onescf[:], sqs[:],
                   start=(half == 0), stop=(half == 1))
            e_ms = evac("scalar", "activation", (rowbuf[0:1, 0:T], PS[6][0:1, 0:T],
                                                 AF.Sqrt),
                        dict(scale=1.0 / DV, bias=cepso[0:1, 0:1]),
                        waits=[("t", kb.now("t"))], bank=6)
            kb.op("vector", "reciprocal", (rowbuf[0:1, T:2 * T], rowbuf[0:1, 0:T]),
                  inc="v")
            mm(6, PS[6][:, 0:T], onesrow[0:1, :], rowbuf[0:1, T:2 * T],
               start=True, stop=True)
            e_rb = evac("scalar", "copy", (obuf2[:, 0:T], PS[6][:, 0:T]),
                        waits=[("t", kb.now("t"))], bank=6)
            for half in range(2):
                ct = 2 * h + half
                kb.op("vector", "scalar_tensor_tensor",
                      (do_f[:, ct * T:(ct + 1) * T], do_f[:, ct * T:(ct + 1) * T],
                       onw_c[:, half:half + 1], obuf2[:, 0:T], ALU.mult, ALU.mult),
                      inc="v")

        for i in range(DT):
            for ph in range(2):
                gload(wslot[ph * 64:(ph + 1) * 64, i * D:(i + 1) * D],
                  Wo_d[i * 128 + ph * 64: i * 128 + (ph + 1) * 64, :],
                  sem="qb")
        kb.dma_barrier("qb")
        lwo = kb.now("qb")
        for ct in range(DT):
            for dt in range(DT):
                mm(ct if dt == 0 else None, PS[ct][:, 0:T],
                   wslot[:, dt * D + ct * 128: dt * D + (ct + 1) * 128],
                   do_f[:, dt * T:(dt + 1) * T],
                   start=(dt == 0), stop=(dt == DT - 1), waits=[("qb", lwo)])
        for ct in range(DT):
            evac("vector", "tensor_tensor",
                 (x_out[:, ct * T:(ct + 1) * T], x_in[:, ct * T:(ct + 1) * T],
                  PS[ct][:, 0:T], ALU.add),
                 waits=[("t", kb.now("t"))], bank=ct)

        for ct in range(DT):
            kb.op("vector", "tensor_copy",
                  (v_f[:, ct * (T + 62): ct * (T + 62) + 62],
                   v_f[:, ct * (T + 62) + T: ct * (T + 62) + T + 62]), inc="v")
        return kb.now("s")

    for ti in range(NTILE):
        for pt in range(4):
            w = [("store", kb.now("store"))]
            if pt == 0:
                w.append(("q", kb.now("q")))
            load(xtm[:, pt * D:(pt + 1) * D],
                 x0_d[ti * T + pt * 128: ti * T + (pt + 1) * 128, :], waits=w)
        lq = kb.now("q")
        kb.dma_barrier("q")
        for dt in range(DT):
            for pt in range(4):
                tp(3 if pt == 0 else None,
                   PS[3][0:128, pt * 128:(pt + 1) * 128],
                   xtm[:, pt * D + dt * 128: pt * D + (dt + 1) * 128],
                   identf[:], waits=[("q", lq), ("v", gv_id)])
            evac("scalar", "copy", (x0f[:, dt * T:(dt + 1) * T], PS[3][0:128, 0:512]),
                 waits=[("t", kb.now("t"))], bank=3)

        s_ld = kb.now("s")
        if ti > 0 and STAGE in ("x3", "x4", "full", "dbgb"):
            for i in range(4):
                load(k_f[:, i * DT * 112:(i + 1) * DT * 112],
                     xptf_dram[:, i * DT * 112:(i + 1) * DT * 112])
            kb.dma_barrier("q")
        emit_mamba(x0f, x1f, prod=[("s", s_ld)])
        if STAGE != "x1":
            emit_ffn(x1f, x0f, ti, prod=[("s", kb.now("s"))])
        if STAGE in ("x3", "x4", "full", "dbgb"):
            emit_attnb(x0f, x1f, ti)
        if STAGE == "dbgb" and ti == 0:
            for nm, buf, ncols in (("qf", bmcb[:, 10496:14616], 4120),
                                   ("kf", k_f[:, 0:4120], 4120),
                                   ("vf", v_f[:, 0:4592], 4592),
                                   ("dof", x1f[:, 0:4096], 4096),
                                   ("lsf", ymt[:, 0:4096], 4096),
                                   ("llf", Ht[:, 0:4096], 4096)):
                pass
            kb.op("gpsimd", "dma_start", (),
                  dict(out=y_d[0:128, :], in_=bmcb[:, 10496:11520]),
                  waits=[("v", kb.now("v")), ("s", kb.now("s"))], inc="store", incn=16)
            kb.op("gpsimd", "dma_start", (),
                  dict(out=y_d[128:256, :], in_=k_f[:, 0:1024]), inc="store", incn=16)
            kb.op("gpsimd", "dma_start", (),
                  dict(out=y_d[256:384, :], in_=v_f[:, 0:1024]), inc="store", incn=16)
            kb.op("gpsimd", "dma_start", (),
                  dict(out=y_d[384:512, :], in_=x1f[:, 0:1024]), inc="store", incn=16)
            kb.op("gpsimd", "dma_start", (),
                  dict(out=y_d[512:640, :], in_=ymt[:, 0:1024]), inc="store", incn=16)
            kb.op("gpsimd", "dma_start", (),
                  dict(out=y_d[640:768, :], in_=Ht[:, 0:1024]), inc="store", incn=16)
            kb.op("gpsimd", "dma_start", (),
                  dict(out=y_d[768:776, :], in_=hsb[:, 0:0 + 1024][0:8, :]),
                  inc="store", incn=16)
            kb.dma_barrier("store")
            kb.op("vector", "nop", (), waits=[("store", kb.now("store"))], inc="v")
            kb.op("scalar", "nop", (), waits=[("store", kb.now("store"))], inc="s")
        if STAGE in ("x4", "full"):
            emit_ffn(x1f, x0f, ti, prod=[("s", kb.now("s"))])

        if STAGE == "dbg" and ti == 0:
            kb.op("gpsimd", "dma_start", (),
                  dict(out=y_d[512:1024, :].rearrange("(g p) c -> p g c", p=128),
                       in_=xnb[:, :].rearrange("p (g c) -> p g c", c=1024)),
                  waits=[("v", kb.now("v"))], inc="store", incn=16)
            kb.op("gpsimd", "dma_start", (),
                  dict(out=y_d[128:256, 0:32], in_=lgtm[:, :]),
                  waits=[("s", kb.now("s"))], inc="store", incn=16)
            kb.op("gpsimd", "dma_start", (),
                  dict(out=y_d[256:264, 0:T], in_=m_fm[0:8, :]),
                  waits=[("s", kb.now("s"))], inc="store", incn=16)
            kb.op("gpsimd", "dma_start", (),
                  dict(out=y_d[384:512, 0:32], in_=mrow4[:, :]),
                  waits=[("s", kb.now("s"))], inc="store", incn=16)
            kb.op("gpsimd", "dma_start", (),
                  dict(out=y_d[384:512, 32:64], in_=srow4[:, :]),
                  waits=[("s", kb.now("s"))], inc="store", incn=16)
            kb.op("gpsimd", "dma_start", (),
                  dict(out=y_d[384:512, 64:80], in_=rc16[:, :]),
                  waits=[("s", kb.now("s"))], inc="store", incn=16)
            kb.op("gpsimd", "dma_start", (),
                  dict(out=y_d[384:512, 80:112], in_=mtmb[:, :]),
                  waits=[("s", kb.now("s"))], inc="store", incn=16)
            kb.op("gpsimd", "dma_start", (),
                  dict(out=y_d[1024:1536, :].rearrange("(g p) c -> p g c", p=128),
                       in_=x0f[:, :].rearrange("p (g c) -> p g c", c=1024)),
                  waits=[("s", kb.now("s"))], inc="store", incn=16)
        if STAGE == "dbg" and ti == 0:
            kb.op("vector", "nop", (), waits=[("store", kb.now("store"))], inc="v")
            kb.op("scalar", "nop", (), waits=[("store", kb.now("store"))], inc="s")
        if STAGE in ("dbg", "dbgb"):
            continue
        out_src = {"x1": x1f, "x2": x0f, "dbg": x0f, "x3": x1f, "x4": x0f, "full": x0f, "dbgb": x1f}[STAGE]
        gs_o = sguard(out_src[:, DT * T - 1: DT * T])
        for pt in range(4):
            for dt in range(DT):
                bank = 3 if dt == 0 else (7 if dt == 4 else None)
                dst = PS[3][0:128, dt * 64:(dt + 1) * 64].bitcast(BF16) if dt < 4 else \
                    PS[7][0:128, (dt - 4) * 64:(dt - 3) * 64].bitcast(BF16)
                tp(bank, dst, out_src[:, dt * T + pt * 128: dt * T + (pt + 1) * 128],
                   diag01[:], waits=[("s", gs_o)])
            t_o = kb.now("t")
            evac("vector", "tensor_copy",
                 (xtm[:, pt * D: pt * D + 512], PS[3][0:128, 0:256].bitcast(BF16)),
                 waits=[("t", t_o)], bank=3)
            e2 = evac("vector", "tensor_copy",
                      (xtm[:, pt * D + 512: pt * D + 1024],
                       PS[7][0:128, 0:256].bitcast(BF16)),
                      waits=[("t", t_o)], bank=7)
            kb.op("sync", "dma_start", (),
                  dict(out=y_d[ti * T + pt * 128: ti * T + (pt + 1) * 128, :],
                       in_=xtm[:, pt * D:(pt + 1) * D]),
                  waits=[("v", e2)], inc="store", incn=16)
        kb.dma_barrier("store")
    n_store = kb.now("store")

    with contextlib.ExitStack() as st_:
        sems = {s: st_.enter_context(nc.semaphore(s)) for s in kb.semnames}
        with nc.Block() as block:
            @block.sync
            def _(e):
                kb.replay(e, "sync", sems)
                e.wait_ge(sems["store"], n_store)

            @block.gpsimd
            def _(e):
                kb.replay(e, "gpsimd", sems)

            @block.vector
            def _(e):
                kb.replay(e, "vector", sems)

            @block.scalar
            def _(e):
                kb.replay(e, "scalar", sems)

            @block.tensor
            def _(e):
                kb.replay(e, "tensor", sems)

    return nc


def kernel(**inputs):
    nc = build_core_program()
    x0 = np.ascontiguousarray(np.asarray(inputs["L0_attn_a_u"], np.float32))
    base = {
        "aw_conv_w": inputs["L0_attn_a_conv_w"],
        "aw_conv_b": inputs["L0_attn_a_conv_b"],
        "aw_xpw": inputs["L0_attn_a_x_proj_w"],
        "aw_dtw": inputs["L0_attn_a_dt_proj_w"],
        "aw_dtb": inputs["L0_attn_a_dt_proj_b"],
        "aw_Alog": inputs["L0_attn_a_A_log"],
        "aw_Dskip": inputs["L0_attn_a_D_skip"],
        "fw_rw": inputs["L0_ffn_router_w"],
        "fw_rb": inputs["L0_ffn_router_b"],
        "fw_ew": inputs["L0_ffn_expert_w"],
        "fw_eb": inputs["L0_ffn_expert_b"],
        "bw_Wq": inputs["L0_attn_b_Wq"],
        "bw_Wk": inputs["L0_attn_b_Wk"],
        "bw_Wv": inputs["L0_attn_b_Wv"],
        "bw_cwq": inputs["L0_attn_b_cwq"],
        "bw_cwk": inputs["L0_attn_b_cwk"],
        "bw_cwv": inputs["L0_attn_b_cwv"],
        "bw_Wb": inputs["L0_attn_b_Wb"],
        "bw_Wid": inputs["L0_attn_b_Wid"],
        "bw_alpha": inputs["L0_attn_b_alpha_id"],
        "bw_firs": inputs["L0_attn_b_firs"],
        "bw_firl": inputs["L0_attn_b_firl"],
        "bw_gW1": inputs["L0_attn_b_gW1"],
        "bw_gb1": inputs["L0_attn_b_gb1"],
        "bw_gW2": inputs["L0_attn_b_gW2"],
        "bw_gb2": inputs["L0_attn_b_gb2"],
        "bw_ltemp": inputs["L0_attn_b_log_temp"],
        "bw_floor": inputs["L0_attn_b_floor_param"],
        "bw_cres": inputs["L0_attn_b_conv_res_logit"],
        "bw_onw": inputs["L0_attn_b_o_norm_w"],
        "bw_Wo": inputs["L0_attn_b_Wo"],
    }
    base = {k: np.ascontiguousarray(np.asarray(v, np.float32)) for k, v in base.items()}
    in_maps = []
    for c in range(8):
        m = dict(base)
        m["x0"] = np.ascontiguousarray(x0[c])
        in_maps.append(m)
    res = run_bass_kernel_spmd(nc, in_maps, core_ids=list(range(8)))
    return np.stack([res.results[c]["y"] for c in range(8)], axis=0)


# version 12
# speedup vs baseline: 1.0118x; 1.0118x over previous
import contextlib
import numpy as np
import ml_dtypes
import concourse.bass as bass
import concourse.mybir as mybir
from concourse.bass_utils import run_bass_kernel_spmd

F32 = mybir.dt.float32
BF16 = mybir.dt.bfloat16
AF = mybir.ActivationFunctionType
ALU = mybir.AluOpType
AX = mybir.AxisListType

D = 1024
DT = 8
L = 4096
T = 512
NTILE = L // T
N = 16
NH, DK, DV = 4, 256, 256
NEXP = 8
STAGE = "full"

ENGS = ("sync", "gpsimd", "vector", "scalar", "tensor")
SEM_OF = {"vector": "v", "scalar": "s", "gpsimd": "g", "tensor": "t", "sync": "q"}

_IO = {
    "tensor_copy": ((0,), (1,), ()),
    "tensor_tensor": ((0,), (1, 2), ()),
    "tensor_add": ((0,), (1, 2), ()),
    "tensor_sub": ((0,), (1, 2), ()),
    "tensor_mul": ((0,), (1, 2), ()),
    "tensor_scalar": ((0,), (1, 2, 3), ()),
    "scalar_tensor_tensor": ((0,), (1, 2, 3), ()),
    "tensor_tensor_scan": ((0,), (1, 2, 3), ()),
    "reduce_max": ((0,), (1,), ()),
    "reduce_sum": ((0,), (1,), ()),
    "tensor_reduce": ((0,), (1,), ()),
    "reciprocal": ((0,), (1,), ()),
    "memset": ((0,), (), ()),
    "iota": ((0,), (), ()),
    "affine_select": ((0,), (1,), ()),
    "activation": ((0,), (1,), ("bias", "scale", "accum_out")),
    "copy": ((0,), (1,), ()),
    "square": ((0,), (1,), ()),
    "matmul": ((0,), (1, 2), ()),
    "transpose": ((0,), (1, 2), ()),
    "select": ((0,), (1, 2, 3), ()),
    "dma_start": ((), (), ()),
    "nop": ((), (), ()),
}


def _ap_region(ap):
    if ap is None or isinstance(ap, (int, float)):
        return None
    try:
        t = ap.tensor
        space = str(ap.space)
        name = t.name
        p0 = ap.base_partition()
        shape = ap.shape
        pat = [list(d) for d in ap.ap]
    except Exception:
        return None
    if "DRAM" in space.upper() or "Dram" in space:
        return None
    npart = pat[0][1] if pat else 1
    span = 0
    for (st, cnt) in pat[1:]:
        span += abs(int(st)) * (int(cnt) - 1)
    esz = 2 if "bfloat16" in str(ap.dtype) else 4
    foff = int(ap.offset) - int(p0) * int(pat[0][0]) if pat else int(ap.offset)
    b0 = foff * esz
    b1 = b0 + (span + 1) * esz
    return (name, space, int(p0), int(p0) + int(npart), b0, b1)


def _overlap(r1, r2):
    return (r1[0] == r2[0] and r1[2] < r2[3] and r2[2] < r1[3]
            and r1[4] < r2[5] and r2[4] < r1[5])


class KB:
    def __init__(self, nc):
        self.nc = nc
        self.plan = {e: [] for e in ENGS}
        self.cnt = {}
        self.semnames = []
        for s in ("q", "gq", "qe", "qb", "qf", "v", "s", "t", "g", "store"):
            self.sem(s)
        self.recs = {}
        self.sat = {e: {} for e in ENGS}
        self.dma_groups = {s: [] for s in ("q", "gq", "qe", "qb", "qf", "store")}

    def sem(self, name):
        if name not in self.cnt:
            self.cnt[name] = 0
            self.semnames.append(name)
        return name

    def now(self, name):
        return self.cnt[name]

    def dma_barrier(self, sem):
        if self.cnt[sem]:
            g = self.dma_groups[sem]
            if not g or g[-1] != self.cnt[sem]:
                g.append(self.cnt[sem])

    def _lift(self, sem, val):
        if sem not in self.dma_groups:
            return val
        for b in self.dma_groups[sem]:
            if b >= val:
                return b
        return self.cnt[sem]

    def _auto_waits(self, eng, outs, ins):
        need = {}

        def add(psem, pval):
            need[psem] = max(need.get(psem, 0), pval)

        names = {r[0] for r in outs} | {r[0] for r in ins}
        for nm in names:
            for (reg, kind, peng, psem, pval) in self.recs.get(nm, ()):
                if kind == 'W':
                    hit = False
                    for r in ins:
                        if _overlap(reg, r):
                            hit = True
                            break
                    if hit and not (peng == eng == "tensor"):
                        add(psem, pval)
                for r in outs:
                    if _overlap(reg, r):
                        if peng == eng == "tensor" and kind == 'W':
                            break
                        add(psem, pval)
                        break
        sat = self.sat[eng]
        out = []
        for s_, v_ in need.items():
            v_ = self._lift(s_, v_)
            if sat.get(s_, 0) < v_:
                out.append((s_, v_))
        return out

    def _record(self, reg, kind, eng, sem, val):
        lst = self.recs.setdefault(reg[0], [])
        lst.append((reg, kind, eng, sem, val))
        if len(lst) > 96:
            old, keep = lst[:48], lst[48:]
            coarse = {}
            for (r, k, pe, ps, pv) in old:
                key = (pe, ps, k)
                prev = coarse.get(key)
                if prev is None or pv > prev[4]:
                    big = (r[0], r[1], 0, 128, 0, 1 << 30)
                    coarse[key] = (big, k, pe, ps, pv)
            self.recs[reg[0]] = list(coarse.values()) + keep

    def op(self, eng, meth, args=(), kwargs=None, waits=(), inc=None, incn=1):
        kwargs = kwargs or {}
        if inc is None and meth not in ("dma_start",):
            inc = SEM_OF[eng]
        if inc is not None:
            self.cnt[inc] += incn
            tgt = self.cnt[inc]
        else:
            tgt = None
        oidx, iidx, ikw = _IO.get(meth, ((), (), ()))
        outs, ins = [], []
        for i, a in enumerate(args):
            r = _ap_region(a)
            if r is None:
                continue
            (outs if i in oidx else ins).append(r)
        for k in ikw:
            r = _ap_region(kwargs.get(k))
            if r is not None:
                (outs if k == "accum_out" else ins).append(r)
        if meth == "dma_start":
            r = _ap_region(kwargs.get("out"))
            if r is not None:
                outs.append(r)
            r = _ap_region(kwargs.get("in_"))
            if r is not None:
                ins.append(r)
        auto = self._auto_waits(eng, outs, ins) if (outs or ins) else []
        if meth == "dma_start" and inc in self.dma_groups:
            g = self.dma_groups[inc]
            prev_total = self.cnt[inc] - incn
            if g and g[-1] == prev_total and prev_total > 0:
                auto = list(auto) + [(inc, prev_total)]
        w = {}
        for (s_, v_) in tuple(waits) + tuple(auto):
            if not v_ or v_ <= 0:
                continue
            v_ = self._lift(s_, v_)
            if self.sat[eng].get(s_, 0) < v_:
                w[s_] = max(w.get(s_, 0), v_)
        for s_, v_ in w.items():
            self.sat[eng][s_] = v_
        self.plan[eng].append((meth, args, kwargs, tuple(w.items()), inc, tgt, incn))
        if inc is not None:
            for r in outs:
                self._record(r, 'W', eng, inc, tgt)
            for r in ins:
                self._record(r, 'R', eng, inc, tgt)
        return tgt

    def replay(self, eng_handle, eng_name, sems):
        for meth, args, kwargs, waits, inc, tgt, incn in self.plan[eng_name]:
            for s, v in waits:
                eng_handle.wait_ge(sems[s], v)
            inst = getattr(eng_handle, meth)(*args, **kwargs)
            if inc is not None:
                inst.then_inc(sems[inc], incn)


def build_core_program():
    nc = bass.Bass("TRN2", target_bir_lowering=False)
    kb = KB(nc)
    dram = {}

    def din(name, shape, dtype=F32):
        dram[name] = nc.dram_tensor(name, list(shape), dtype, kind="ExternalInput").ap()
        return dram[name]

    x0_d = din("x0", [L, D])
    conv_w_d = din("aw_conv_w", [D, 1, 4])
    conv_b_d = din("aw_conv_b", [D])
    xpw_d = din("aw_xpw", [96, D])
    dtw_d = din("aw_dtw", [D, 64])
    dtb_d = din("aw_dtb", [D])
    Alog_d = din("aw_Alog", [D, N])
    Dskip_d = din("aw_Dskip", [D])
    rw_d = din("fw_rw", [D, NEXP])
    rb_d = din("fw_rb", [NEXP])
    ew_d = din("fw_ew", [NEXP, D, D])
    eb_d = din("fw_eb", [NEXP, D])
    Wq_d = din("bw_Wq", [D, D])
    Wk_d = din("bw_Wk", [D, D])
    Wv_d = din("bw_Wv", [D, D])
    cwq_d = din("bw_cwq", [D, 4])
    cwk_d = din("bw_cwk", [D, 4])
    cwv_d = din("bw_cwv", [D, 4])
    Wb_d = din("bw_Wb", [D, NH])
    Wid_d = din("bw_Wid", [D, D])
    alpha_d = din("bw_alpha", [NH])
    firs_d = din("bw_firs", [NH, DV, 3])
    firl_d = din("bw_firl", [NH, DV, 63])
    gW1_d = din("bw_gW1", [D + 16, D])
    gb1_d = din("bw_gb1", [D])
    gW2_d = din("bw_gW2", [D, 4])
    gb2_d = din("bw_gb2", [4])
    ltemp_d = din("bw_ltemp", [NH, 4])
    floor_d = din("bw_floor", [NH, 4])
    cres_d = din("bw_cres", [NH])
    onw_d = din("bw_onw", [DV])
    Wo_d = din("bw_Wo", [D, D])
    y_d = nc.dram_tensor("y", [L, D], F32, kind="ExternalOutput").ap()
    fir_dram = nc.dram_tensor("fir_scratch", [DT, 128, 66 * 128], BF16, kind="Internal").ap()
    xptf_dram = nc.dram_tensor("xptf_scratch", [128, 4 * DT * 112], BF16, kind="Internal").ap()

    AL = nc.alloc_sbuf_tensor

    x0f = AL("x0f", [128, DT * T], BF16)
    x1f = AL("x1f", [128, DT * T], BF16)
    xtm = AL("xtm", [128, 4 * D], F32)
    u_f = AL("u_f", [128, DT * (T + 3)], BF16)
    lnt1 = AL("lnt1", [128, T], F32)
    xnf = AL("xnf", [128, T], F32)
    sqs = AL("sqs", [128, T], F32)
    rowbuf = AL("rowbuf", [1, 3 * T], F32)
    delta = AL("delta", [128, 2 * T], F32)
    dtBC = AL("dtBC", [64, T], BF16)
    bc48 = AL("bc48", [48, T], BF16)
    bmcb = AL("bmcb", [128, 2 * N * T], BF16)
    dAt = AL("dAt", [128, T], BF16)
    ubm = AL("ubm", [128, T], BF16)
    bbt = AL("bbt", [128, T], BF16)
    Ht = AL("Ht", [128, (N // 2) * T], BF16)
    ymt = AL("ymt", [128, (N // 2) * T], BF16)
    Hc = AL("Hc", [128, DT * N], F32)
    zt = AL("zt", [128, T], F32)
    at_ = AL("at_", [128, T], F32)
    guard = AL("guard", [128, 1], F32)

    k_f = AL("k_f", [128, DT * (T + 3)], BF16)
    q_f = AL("q_f", [128, DT * (T + 3)], BF16)
    v_f = AL("v_f", [128, DT * (T + 62)], BF16)

    xpw = AL("xpw", [96, D], BF16)
    xpT = None
    cw_c = AL("cw_c", [128, DT * 4], F32)
    dtw_sb = AL("dtw_sb", [128, DT * 64], BF16)
    dtwT = AL("dtwT", [64, DT * 128], BF16)
    dtb_c = AL("dtb_c", [128, DT], F32)
    Dskip_c = AL("Dskip_c", [128, DT], F32)
    Dsk_dg = AL("Dsk_dg", [128, DT * 128], BF16)
    Alog_c = AL("Alog_c", [128, DT * N], F32)
    A_c = AL("A_c", [128, DT * N], F32)
    nA_c = AL("nA_c", [128, DT * N], F32)
    invA_c = AL("invA_c", [128, DT * N], F32)
    cb96 = AL("cb96", [112, 1], F32)
    cbrow = AL("cbrow", [128, DT], F32)
    cbrow_b = AL("cbrow_b", [128, DT], BF16)
    enB = AL("enB", [48, N * 128], BF16)
    enC = AL("enC", [48, N * 128], BF16)
    diag01 = AL("diag01", [128, 128], BF16)
    identf = AL("identf", [128, 128], F32)
    onescf = AL("onescf", [128, 1], F32)
    onescb = AL("onescb", [128, 1], BF16)
    onesrow = AL("onesrow", [1, 128], F32)
    cone = AL("cone", [128, 1], F32)
    cepsln = AL("cepsln", [128, 1], F32)
    vec8a = AL("vec8a", [8, 128], F32)
    vec8b = AL("vec8b", [8, 128], F32)
    vec8c = AL("vec8c", [8, 128], F32)

    S_f = AL("S_f", [128, NH * 2 * 256], F32)
    qtails = AL("qtails", [128, DT * 3], BF16)
    ktails = AL("ktails", [128, DT * 3], BF16)
    vrawtails = Alog_c[:, 0:DT * 3 // 2].bitcast(BF16)
    cvscr = xnf[:, 0:3].bitcast(BF16)
    wb_sb = AL("wb_sb", [128, DT * NH], BF16)
    cwq_c = AL("cwq_c", [128, DT * 4], F32)
    cwk_c = AL("cwk_c", [128, DT * 4], F32)
    cwv_c = AL("cwv_c", [128, DT * 4], F32)
    acol = AL("acol", [128, NH], F32)
    scres = AL("scres", [128, NH], F32)
    onw_c = AL("onw_c", [128, 2], F32)
    gb1_c = AL("gb1_c", [128, DT], F32)
    gb2_c = AL("gb2_c", [4, 1], F32)
    gw2_sb = AL("gw2_sb", [128, DT * 4], BF16)
    invt_c = AL("invt_c", [4, NH], F32)
    fl_bc = AL("fl_bc", [128, 16], F32)
    fs_bc = AL("fs_bc", [128, NH], F32)
    gw1s_p1 = AL("gw1s_p1", [128, D], BF16)
    gw1s_p2 = AL("gw1s_p2", [4, D], BF16)
    fircl = bmcb[:, 0:1056].bitcast(F32)
    selst = AL("selst", [128, 12 * 68], BF16)
    mask_sl = AL("mask_sl", [128, 128], BF16)
    mask_ui = AL("mask_ui", [128, 128], BF16)
    E4 = AL("E4", [4, NH * 128], BF16)
    cepsl2 = AL("cepsl2", [128, 1], F32)
    cepso = AL("cepso", [128, 1], F32)
    srow16 = AL("srow16", [1, 16], F32)
    fsrow4 = AL("srow4b", [1, 4], F32)

    rw_sb = AL("rw_sb", [128, DT * NEXP], F32)
    rb_c = AL("rb_c", [8, 1], F32)
    eb_sb = AL("eb_sb", [8, DT * 128], BF16)
    E8 = AL("E8", [8, NEXP * 128], BF16)
    ew_slot = AL("ew_slot", [128, DT * D], BF16)
    xnb = AL("xnb", [128, DT * T], BF16)
    xeb = AL("xeb", [128, 2 * T], BF16)
    lgrow = AL("lgrow", [8, T], F32)
    lgtm = AL("lgtm", [128, 4 * NEXP], F32)
    mrow = AL("mrow", [128, NEXP], F32)
    mtmb = AL("mtmb", [128, 4 * NEXP], BF16)
    mrow4 = AL("mrow4", [128, 4 * NEXP], F32)
    srow4 = AL("srow4", [128, 4 * NEXP], F32)
    rc16 = AL("rc16", [128, 16], F32)
    m_fm = dtBC[0:8, 0:T]
    rcol = AL("rcol", [128, 4], F32)
    mbc_all = bmcb
    eslot2 = bmcb[:, 4096:4096 + DT * D]
    dAt2 = xeb[:, 0:T]
    ubm2 = xeb[:, T:2 * T]

    PSH = [nc.alloc_psum_tensor(f"bank{i}", [128, 512], F32) for i in range(8)]
    PS = [h[:] for h in PSH]

    bank_free = {i: ("t", 0) for i in range(8)}

    def mm(bank, out_ap, lhsT, rhs, start, stop, waits=(), **kw):
        w = list(waits)
        if start and bank is not None:
            w.append(bank_free[bank])
        return kb.op("tensor", "matmul", (out_ap, lhsT, rhs),
                     dict(start=start, stop=stop, **kw), waits=w, inc="t")

    def tp(bank, out_ap, in_ap, ident_ap, waits=()):
        w = list(waits)
        if bank is not None:
            w.append(bank_free[bank])
        return kb.op("tensor", "transpose", (out_ap, in_ap, ident_ap), waits=w, inc="t")

    def evac(eng, meth, args, kwargs=None, waits=(), bank=None):
        sem = SEM_OF[eng]
        tgt = kb.op(eng, meth, args, kwargs, waits=waits, inc=sem)
        if bank is not None:
            bank_free[bank] = (sem, tgt)
        return tgt

    def vguard(ap_col):
        return kb.op("vector", "tensor_copy", (guard[:ap_col.shape[0], 0:1], ap_col), inc="v")

    def sguard(ap_col):
        return kb.op("scalar", "copy", (guard[:ap_col.shape[0], 0:1], ap_col), inc="s")

    def load(dst, src, waits=(), sem="q"):
        return kb.op("sync", "dma_start", (), dict(out=dst, in_=src), waits=waits,
                     inc=sem, incn=16)

    def gload(dst, src, waits=(), sem="gq"):
        return kb.op("gpsimd", "dma_start", (), dict(out=dst, in_=src),
                     waits=waits, inc=sem, incn=16)

    xpT = Ht[:, 0:DT * 112]

    load(vec8a[0:8, :], dtb_d.rearrange("(dt p) -> dt p", p=128))
    load(vec8b[0:8, :], conv_b_d.rearrange("(dt p) -> dt p", p=128))
    load(vec8c[0:8, :], Dskip_d.rearrange("(dt p) -> dt p", p=128))
    load(Alog_c[:, :].rearrange("p (dt n) -> p dt n", n=N),
         Alog_d.rearrange("(dt p) n -> p dt n", p=128))
    load(cw_c[:, :].rearrange("p (dt k) -> p dt k", k=4),
         conv_w_d.rearrange("(dt p) o k -> p dt (o k)", p=128))
    load(rw_sb[:, :].rearrange("p (dt e) -> p dt e", e=NEXP),
         rw_d.rearrange("(dt p) e -> p dt e", p=128))
    load(rb_c[0:8, 0:1], rb_d.rearrange("(e o) -> e o", o=1))
    gload(xpw[:, :], xpw_d)
    gload(dtw_sb[:, :].rearrange("p (dt r) -> p dt r", r=64),
          dtw_d.rearrange("(dt p) r -> p dt r", p=128))
    gload(eb_sb[0:8, :].rearrange("e (ct q) -> e ct q", q=128),
          eb_d.rearrange("e (ct q) -> e ct q", q=128))
    ldq = kb.now("q")
    gldq = kb.now("gq")
    kb.dma_barrier("q")
    kb.dma_barrier("gq")

    kb.op("gpsimd", "memset", (diag01[:], 1.0), inc="g")
    kb.op("gpsimd", "affine_select",
          (diag01[:], diag01[:], [[-1, 128]], ALU.is_equal, 0.0),
          dict(base=0, channel_multiplier=1), waits=[("g", kb.now("g"))], inc="g")
    kb.op("gpsimd", "memset", (enB[:], 1.0), inc="g")
    kb.op("gpsimd", "memset", (enC[:], 1.0), inc="g")
    kb.op("gpsimd", "memset", (E8[:], 1.0), inc="g")
    g_c0 = kb.now("g")
    for n in range(N):
        kb.op("gpsimd", "affine_select",
              (enB[0:48, n * 128:(n + 1) * 128], enB[0:48, n * 128:(n + 1) * 128],
               [[0, 128]], ALU.is_equal, 0.0),
              dict(base=-n, channel_multiplier=1), waits=[("g", g_c0)], inc="g")
        kb.op("gpsimd", "affine_select",
              (enC[0:48, n * 128:(n + 1) * 128], enC[0:48, n * 128:(n + 1) * 128],
               [[0, 128]], ALU.is_equal, 0.0),
              dict(base=-(32 + n), channel_multiplier=1), waits=[("g", g_c0)], inc="g")
    for e in range(NEXP):
        kb.op("gpsimd", "affine_select",
              (E8[0:8, e * 128:(e + 1) * 128], E8[0:8, e * 128:(e + 1) * 128],
               [[0, 128]], ALU.is_equal, 0.0),
              dict(base=-e, channel_multiplier=1), waits=[("g", g_c0)], inc="g")
    g_en = kb.now("g")

    kb.op("vector", "memset", (onescf[:], 1.0), inc="v")
    kb.op("vector", "memset", (onescb[:], 1.0), inc="v")
    kb.op("vector", "memset", (onesrow[:], 1.0), inc="v")
    kb.op("vector", "memset", (cone[:], 1.0), inc="v")
    kb.op("vector", "memset", (cepsln[:], 1e-5), inc="v")
    kb.op("vector", "memset", (Hc[:], 0.0), inc="v")
    kb.op("vector", "memset", (u_f[:], 0.0), inc="v")
    kb.op("vector", "tensor_copy", (identf[:], diag01[:]), waits=[("g", g_c0)], inc="v")
    v_c0 = kb.now("v")
    gv_id = vguard(identf[:, 127:128])

    s_nA = kb.op("scalar", "activation", (nA_c[:], Alog_c[:], AF.Exp),
                 waits=[("q", ldq)], inc="s")
    kb.op("vector", "tensor_scalar", (A_c[:], nA_c[:], -1.0, None, ALU.mult),
          waits=[("s", s_nA)], inc="v")
    kb.op("vector", "reciprocal", (invA_c[:], A_c[:]), inc="v")
    v_A = kb.now("v")

    tp(3, PS[3][0:128, 0:8], vec8a[0:8, :], identf[0:8, 0:8],
       waits=[("q", ldq), ("v", gv_id)])
    tp(None, PS[3][0:128, 8:16], vec8b[0:8, :], identf[0:8, 0:8])
    tp(None, PS[3][0:128, 16:24], vec8c[0:8, :], identf[0:8, 0:8])
    t_vec = kb.now("t")
    evac("scalar", "copy", (dtb_c[:], PS[3][0:128, 0:8]), waits=[("t", t_vec)])
    evac("scalar", "copy", (cbrow[:], PS[3][0:128, 8:16]))
    evac("scalar", "copy", (Dskip_c[:], PS[3][0:128, 16:24]), bank=3)
    s_vec = kb.now("s")

    for dt in range(DT):
        kb.op("vector", "tensor_tensor",
              (Dsk_dg[:, dt * 128:(dt + 1) * 128],
               Dskip_c[:, dt:dt + 1].broadcast_to((128, 128)), diag01[:], ALU.mult),
              waits=[("s", s_vec)], inc="v")
    v_dsk = kb.now("v")

    for dt in range(DT):
        tp(3, PS[3][0:128, 0:48].bitcast(BF16), xpw[0:96, dt * 128:(dt + 1) * 128],
           diag01[0:96, 0:96], waits=[("gq", gldq), ("v", gv_id)])
        tnow = kb.now("t")
        evac("scalar", "copy", (xpT[:, dt * 112: dt * 112 + 64],
                                PS[3][0:128, 0:32].bitcast(BF16)), waits=[("t", tnow)])
        evac("scalar", "copy", (xpT[:, dt * 112 + 64: dt * 112 + 80],
                                PS[3][0:128, 32:40].bitcast(BF16)))
        evac("scalar", "copy", (xpT[:, dt * 112 + 96: dt * 112 + 112],
                                PS[3][0:128, 40:48].bitcast(BF16)), bank=3)
        kb.op("vector", "memset", (xpT[:, dt * 112 + 80: dt * 112 + 96], 0.0),
              waits=[("s", kb.now("s"))], inc="v")
    s_xpT = kb.now("s")
    v_xpT = kb.now("v")

    xpTf = k_f
    for tap in range(4):
        for dt in range(DT):
            kb.op("vector", "scalar_tensor_tensor",
                  (xpTf[:, (tap * DT + dt) * 112:(tap * DT + dt + 1) * 112],
                   xpT[:, dt * 112:(dt + 1) * 112],
                   cw_c[:, dt * 4 + tap: dt * 4 + tap + 1],
                   xpT[:, dt * 112:(dt + 1) * 112], ALU.mult, ALU.bypass),
                  waits=[("s", s_xpT), ("v", v_xpT), ("q", ldq)], inc="v")
    gv_fold = kb.now("v")
    kb.op("sync", "dma_start", (),
          dict(out=xptf_dram, in_=xpTf[:, 0: 4 * DT * 112]),
          waits=[("v", gv_fold)], inc="q", incn=16)
    kb.dma_barrier("q")

    kb.op("vector", "tensor_copy", (cbrow_b[:], cbrow[:]), waits=[("s", s_vec)], inc="v")
    gv_cb = vguard(cbrow_b[:, DT - 1:DT])
    for dt in range(DT):
        mm(2 if dt == 0 else None, PS[2][0:112, 0:1],
           xpT[:, dt * 112:(dt + 1) * 112], cbrow_b[:, dt:dt + 1],
           start=(dt == 0), stop=(dt == DT - 1), waits=[("v", gv_cb), ("s", s_xpT)])
    e_cb = evac("scalar", "copy", (cb96[:, 0:1], PS[2][0:112, 0:1]),
                waits=[("t", kb.now("t"))], bank=2)

    for dt in range(DT):
        tp(3, PS[3][0:64, 0:64].bitcast(BF16), dtw_sb[:, dt * 64:(dt + 1) * 64],
           diag01[:], waits=[("gq", gldq), ("v", gv_id)])
        evac("scalar", "copy", (dtwT[0:64, dt * 128:(dt + 1) * 128],
                                PS[3][0:64, 0:64].bitcast(BF16)),
             waits=[("t", kb.now("t"))], bank=3)

    load(cwq_c[:, :].rearrange("p (dt k) -> p dt k", k=4),
         cwq_d.rearrange("(dt p) k -> p dt k", p=128))
    load(cwk_c[:, :].rearrange("p (dt k) -> p dt k", k=4),
         cwk_d.rearrange("(dt p) k -> p dt k", p=128))
    load(cwv_c[:, :].rearrange("p (dt k) -> p dt k", k=4),
         cwv_d.rearrange("(dt p) k -> p dt k", p=128))
    gload(wb_sb[:, :].rearrange("p (dt h) -> p dt h", h=NH),
          Wb_d.rearrange("(dt p) h -> p dt h", p=128))
    gload(gw2_sb[:, :].rearrange("p (dt h) -> p dt h", h=4),
          gW2_d.rearrange("(dt p) h -> p dt h", p=128))
    load(gb2_c[0:4, 0:1], gb2_d.rearrange("(e o) -> e o", o=1))
    load(fircl[:, 0:504].rearrange("p (h dc k) -> p h dc k", h=NH, k=63),
         firl_d.rearrange("h (dc p) k -> p h dc k", p=128))
    load(fircl[:, 504:528].rearrange("p (h dc k) -> p h dc k", h=NH, k=3),
         firs_d.rearrange("h (dc p) k -> p h dc k", p=128))
    alpha_r = Ht[0:1, 896:904].bitcast(F32)
    cres_r = Ht[0:1, 904:912].bitcast(F32)
    lt_r = Ht[0:4, 912:920].bitcast(F32)
    fp_r = Ht[0:1, 920:952].bitcast(F32)
    onw_r = Ht[0:2, 960:1216].bitcast(F32)
    gb1_r = Ht[0:8, 1216:1472].bitcast(F32)
    load(alpha_r[0:1, :], alpha_d.rearrange("(o h) -> o h", o=1))
    load(cres_r[0:1, :], cres_d.rearrange("(o h) -> o h", o=1))
    load(lt_r[0:4, :], ltemp_d)
    load(fp_r[0:1, :], floor_d.rearrange("h b -> (h b)").rearrange("(o s) -> o s", o=1))
    load(onw_r[0:2, :], onw_d.rearrange("(c p) -> c p", p=128))
    load(gb1_r[0:8, :], gb1_d.rearrange("(dt p) -> dt p", p=128))
    kb.op("vector", "memset", (gw1s_p1[:, :], 0.0))
    kb.op("vector", "memset", (gw1s_p2[:, :], 0.0))
    for b in range(4):
        gload(gw1s_p1[b:b + 1, :], gW1_d[D + b * 4 + 0: D + b * 4 + 1, :])
        gload(gw1s_p1[32 + b:33 + b, :], gW1_d[D + b * 4 + 1: D + b * 4 + 2, :])
        gload(gw1s_p1[64 + b:65 + b, :], gW1_d[D + b * 4 + 2: D + b * 4 + 3, :])
        gload(gw1s_p2[b:b + 1, :], gW1_d[D + b * 4 + 3: D + b * 4 + 4, :])
    ldq2 = kb.now("q")
    gldq2 = kb.now("gq")
    kb.dma_barrier("q")
    kb.dma_barrier("gq")

    kb.op("vector", "memset", (cepsl2[:], 1e-6))
    kb.op("vector", "memset", (cepso[:], 1e-5))
    kb.op("vector", "memset", (S_f[:], 0.0))
    kb.op("vector", "memset", (qtails[:], 0.0))
    kb.op("vector", "memset", (ktails[:], 0.0))
    kb.op("vector", "memset", (vrawtails[:], 0.0))
    for ct in range(DT):
        kb.op("vector", "memset", (v_f[:, ct * (T + 62): ct * (T + 62) + 62], 0.0))
    kb.op("gpsimd", "memset", (mask_sl[:], 1.0))
    kb.op("gpsimd", "affine_select",
          (mask_sl[:], mask_sl[:], [[-1, 128]], ALU.is_gt, 0.0),
          dict(base=0, channel_multiplier=1))
    kb.op("gpsimd", "memset", (mask_ui[:], 1.0))
    kb.op("gpsimd", "affine_select",
          (mask_ui[:], mask_ui[:], [[1, 128]], ALU.is_gt, 0.0),
          dict(base=1, channel_multiplier=-1))
    kb.op("gpsimd", "memset", (E4[:], 1.0))
    for b in range(NH):
        kb.op("gpsimd", "affine_select",
              (E4[0:4, b * 128:(b + 1) * 128], E4[0:4, b * 128:(b + 1) * 128],
               [[0, 128]], ALU.is_equal, 0.0),
              dict(base=-b, channel_multiplier=1))
    for b in range(4):
        for qi, base_row in ((0, 0), (1, 32), (2, 64)):
            sl = selst[:, (b * 3 + qi) * 68:(b * 3 + qi + 1) * 68]
            kb.op("gpsimd", "memset", (sl, 1.0 / 256.0))
            kb.op("gpsimd", "affine_select", (sl, sl, [[1, 68]], ALU.is_equal, 0.0),
                  dict(base=-(base_row + b), channel_multiplier=0))
    kb.op("scalar", "activation", (cres_r[0:1, :], cres_r[0:1, :], AF.Sigmoid),
          waits=[("q", ldq2)])
    mm(1, PS[1][0:128, 0:4], onesrow[0:1, :], alpha_r[0:1, :], start=True, stop=True,
       waits=[("q", ldq2)])
    mm(5, PS[5][0:128, 0:4], onesrow[0:1, :], cres_r[0:1, :], start=True, stop=True,
       waits=[("q", ldq2)])
    evac("scalar", "copy", (acol[:, :], PS[1][0:128, 0:4]), bank=1)
    evac("scalar", "copy", (scres[:, :], PS[5][0:128, 0:4]), bank=5)
    kb.op("scalar", "activation", (srow16[0:1, :], fp_r[0:1, :], AF.Sigmoid),
          waits=[("q", ldq2)])
    kb.op("vector", "tensor_scalar", (srow16[0:1, :], srow16[0:1, :], 0.05, None, ALU.mult))
    kb.op("vector", "tensor_reduce",
          (fsrow4[0:1, 0:4].rearrange("o (h u) -> o h u", u=1),
           srow16[0:1, :].rearrange("o (h b) -> o h b", b=4)),
          dict(axis=AX.X, op=ALU.add))
    kb.op("vector", "tensor_scalar", (fsrow4[0:1, :], fsrow4[0:1, :], -1.0, 1.0,
                                      ALU.mult, ALU.add))
    mm(1, PS[1][0:128, 0:16], onesrow[0:1, :], srow16[0:1, :], start=True, stop=True)
    evac("scalar", "copy", (fl_bc[:, :], PS[1][0:128, 0:16]), bank=1)
    mm(1, PS[1][0:128, 0:4], onesrow[0:1, :], fsrow4[0:1, :], start=True, stop=True)
    evac("scalar", "copy", (fs_bc[:, :], PS[1][0:128, 0:4]), bank=1)
    tp(1, PS[1][0:4, 0:4], lt_r[0:4, 0:4], identf[0:4, 0:4], waits=[("q", ldq2)])
    evac("scalar", "copy", (invt_c[0:4, :], PS[1][0:4, 0:4]), bank=1)
    kb.op("scalar", "activation", (invt_c[0:4, :], invt_c[0:4, :], AF.Exp),
          dict(scale=-1.0))
    tp(1, PS[1][0:128, 0:2], onw_r[0:2, :], identf[0:2, 0:2])
    evac("scalar", "copy", (onw_c[:, :], PS[1][0:128, 0:2]), bank=1)
    tp(1, PS[1][0:128, 0:8], gb1_r[0:8, :], identf[0:8, 0:8])
    evac("scalar", "copy", (gb1_c[:, :], PS[1][0:128, 0:8]), bank=1)
    for ct in range(DT):
        for halfsel in range(2):
            lo, hi = (0, 33) if halfsel == 0 else (33, 66)
            for tau in range(lo, hi):
                col = fircl[:, ct * 63 + tau: ct * 63 + tau + 1] if tau < 63 else                     fircl[:, 504 + ct * 3 + (tau - 63): 504 + ct * 3 + (tau - 63) + 1]
                kb.op("gpsimd", "tensor_tensor",
                      (ew_slot[:, (tau - lo) * 128:(tau - lo + 1) * 128],
                       col.broadcast_to((128, 128)), diag01[:], ALU.mult),
                      waits=[("q", ldq2)])
            kb.op("sync", "dma_start", (),
                  dict(out=fir_dram[ct][:, lo * 128: hi * 128],
                       in_=ew_slot[:, 0:(hi - lo) * 128]),
                  waits=[("g", kb.now("g"))], inc="qf", incn=16)
            kb.dma_barrier("qf")

    def emit_ln(x_in, write_dt, prod=()):
        gs = kb.op("scalar", "copy", (guard[0:128, 0:1], x_in[:, DT * T - 1: DT * T]),
                   waits=prod, inc="s")
        for dt in range(DT):
            mm(2 if dt == 0 else None, PS[2][0:1, 0:T], onescb[:],
               x_in[:, dt * T:(dt + 1) * T],
               start=(dt == 0), stop=(dt == DT - 1), waits=[("s", gs)])
        t_mu = kb.now("t")
        for dt in range(DT):
            kb.op("scalar", "activation", (sqs[:], x_in[:, dt * T:(dt + 1) * T], AF.Square),
                  waits=[("s", gs), ("t", kb.now("t"))], inc="s")
            gq = sguard(sqs[:, T - 1:T])
            mm(6 if dt == 0 else None, PS[6][0:1, 0:T], onescf[:], sqs[:],
               start=(dt == 0), stop=(dt == DT - 1), waits=[("s", gq)])
        t_sq = kb.now("t")
        evac("scalar", "activation", (rowbuf[0:1, 0:T], PS[2][0:1, 0:T], AF.Copy),
             dict(scale=1.0 / D), waits=[("t", t_mu)], bank=2)
        s_musq = kb.op("scalar", "activation",
                       (rowbuf[0:1, T:2 * T], rowbuf[0:1, 0:T], AF.Square), inc="s")
        e_var = evac("vector", "scalar_tensor_tensor",
                     (rowbuf[0:1, 2 * T:3 * T], PS[6][0:1, 0:T], 1.0 / D,
                      rowbuf[0:1, T:2 * T], ALU.mult, ALU.subtract),
                     waits=[("t", t_sq), ("s", s_musq)], bank=6)
        s_sd = kb.op("scalar", "activation",
                     (rowbuf[0:1, T:2 * T], rowbuf[0:1, 2 * T:3 * T], AF.Sqrt),
                     dict(bias=cepsln[0:1, 0:1]), waits=[("v", e_var)], inc="s")
        s_sd = kb.op("scalar", "copy", (guard[0:1, 0:1], rowbuf[0:1, 2 * T - 1:2 * T]),
                     inc="s")
        kb.op("vector", "reciprocal", (rowbuf[0:1, 2 * T:3 * T], rowbuf[0:1, T:2 * T]),
              waits=[("s", s_sd)], inc="v")
        kb.op("vector", "scalar_tensor_tensor",
              (rowbuf[0:1, T:2 * T], rowbuf[0:1, 0:T], -1.0,
               rowbuf[0:1, 2 * T:3 * T], ALU.mult, ALU.mult), inc="v")
        gv_r = vguard(rowbuf[0:1, 3 * T - 1:3 * T])
        mm(2, PS[2][:, 0:T], onesrow[0:1, :], rowbuf[0:1, 2 * T:3 * T],
           start=True, stop=True, waits=[("v", gv_r)])
        mm(6, PS[6][:, 0:T], onesrow[0:1, :], rowbuf[0:1, T:2 * T],
           start=True, stop=True, waits=[("v", gv_r)])
        t_bc = kb.now("t")
        for dt in range(DT):
            write_dt(dt, t_bc)
        v_done = kb.now("v")
        bank_free[2] = ("v", v_done)
        bank_free[6] = ("v", v_done)
        return v_done

    def emit_mamba(x_in, x_out, prod=()):
        def wdt(dt, t_bc):
            kb.op("vector", "tensor_tensor",
                  (lnt1[:], x_in[:, dt * T:(dt + 1) * T], PS[2][:, 0:T], ALU.mult),
                  waits=[("t", t_bc)], inc="v")
            kb.op("vector", "tensor_tensor",
                  (u_f[:, dt * (T + 3) + 3: dt * (T + 3) + 3 + T], lnt1[:],
                   PS[6][:, 0:T], ALU.add), inc="v")
        emit_ln(x_in, wdt, prod=prod)
        gv_u = vguard(u_f[:, DT * (T + 3) - 1: DT * (T + 3)])

        for tap in range(4):
            for dt in range(DT):
                first = (tap == 0 and dt == 0)
                mm(0 if first else None, PS[0][0:112, 0:T],
                   xpTf[:, (tap * DT + dt) * 112:(tap * DT + dt + 1) * 112],
                   u_f[:, dt * (T + 3) + tap: dt * (T + 3) + tap + T],
                   start=first, stop=(tap == 3 and dt == DT - 1),
                   waits=[("v", max(gv_fold, gv_u))])
        t_proj = kb.now("t")
        evac("scalar", "activation", (dtBC[0:64, 0:T], PS[0][0:64, 0:T], AF.Identity),
             dict(bias=cb96[0:64, 0:1]), waits=[("t", t_proj), ("s", e_cb)])
        e_proj = evac("scalar", "activation", (bc48[0:48, 0:T], PS[0][64:112, 0:T],
                                               AF.Identity),
                      dict(bias=cb96[64:112, 0:1]), bank=0)
        gs_proj = sguard(bc48[0:48, T - 1:T])

        for n in range(N):
            mm(2, PS[2][:, 0:T], enB[:, n * 128:(n + 1) * 128], bc48[0:48, 0:T],
               start=True, stop=True, waits=[("s", gs_proj), ("g", g_en)])
            evac("scalar", "copy", (bmcb[:, n * T:(n + 1) * T], PS[2][:, 0:T]),
                 waits=[("t", kb.now("t"))], bank=2)
            mm(6, PS[6][:, 0:T], enC[:, n * 128:(n + 1) * 128], bc48[0:48, 0:T],
               start=True, stop=True, waits=[("s", gs_proj), ("g", g_en)])
            evac("scalar", "copy", (bmcb[:, (N + n) * T:(N + n + 1) * T], PS[6][:, 0:T]),
                 waits=[("t", kb.now("t"))], bank=6)
        s_bc = kb.now("s")

        for dt in range(DT):
            dsl = delta[:, (dt % 2) * T:(dt % 2 + 1) * T]
            mm(5, PS[5][:, 0:T], dtwT[0:64, dt * 128:(dt + 1) * 128], dtBC[0:64, 0:T],
               start=True, stop=True, waits=[("s", gs_proj)])
            t_dt = kb.now("t")
            e_z = evac("scalar", "activation", (zt[:], PS[5][:, 0:T], AF.Identity),
                       dict(bias=dtb_c[:, dt:dt + 1]),
                       waits=[("t", t_dt), ("s", s_vec)], bank=5)
            kb.op("scalar", "activation", (sqs[:], zt[:], AF.Abs), inc="s")
            kb.op("scalar", "activation", (at_[:], sqs[:], AF.Exp), dict(scale=-1.0), inc="s")
            s_l = kb.op("scalar", "activation", (sqs[:], at_[:], AF.Ln),
                        dict(bias=cone[0:128, 0:1]), inc="s")
            kb.op("vector", "scalar_tensor_tensor",
                  (dsl, zt[:], 0.0, sqs[:], ALU.max, ALU.add),
                  waits=[("s", s_l)], inc="v")
            for half in range(2):
                for nn in range(N // 2):
                    n = half * (N // 2) + nn
                    col = dt * N + n
                    dab = dAt[:, 0:T] if n % 2 == 0 else dAt2
                    s_dA = kb.op("scalar", "activation",
                                 (dab, dsl, AF.Exp),
                                 dict(scale=A_c[:, col:col + 1]), inc="s")
                    ub = ubm[:, 0:T] if n % 2 == 0 else ubm2
                    s_ub = kb.op("scalar", "activation",
                                 (ub, u_f[:, dt * (T + 3) + 3: dt * (T + 3) + 3 + T],
                                  AF.Identity), dict(scale=invA_c[:, col:col + 1]),
                                 inc="s")
                    kb.op("vector", "tensor_tensor",
                          (ub, ub, bmcb[:, n * T:(n + 1) * T], ALU.mult),
                          waits=[("s", max(s_ub, s_bc))], inc="v")
                    kb.op("vector", "scalar_tensor_tensor",
                          (bbt[:], dab, -1.0, ub, ALU.add, ALU.mult),
                          waits=[("s", s_dA)], inc="v")
                    kb.op("vector", "tensor_tensor_scan",
                          (Ht[:, nn * T:(nn + 1) * T], dab, bbt[:], Hc[:, col:col + 1],
                           ALU.mult, ALU.add), inc="v")
                    kb.op("vector", "tensor_tensor",
                          (ymt[:, nn * T:(nn + 1) * T], Ht[:, nn * T:(nn + 1) * T],
                           bmcb[:, (N + n) * T:(N + n + 1) * T], ALU.mult), inc="v")
                hlast = bass.AP(Ht, T - 1, [[(N // 2) * T, 128], [T, N // 2]])
                kb.op("vector", "tensor_copy",
                      (Hc[:, dt * N + half * (N // 2): dt * N + (half + 1) * (N // 2)],
                       hlast), inc="v")
                for nn in range(N // 2):
                    mm(4 if (half == 0 and nn == 0) else None, PS[4][:, 0:T], diag01[:],
                       ymt[:, nn * T:(nn + 1) * T],
                       start=(half == 0 and nn == 0), stop=False)
            mm(None, PS[4][:, 0:T], Dsk_dg[:, dt * 128:(dt + 1) * 128],
               u_f[:, dt * (T + 3) + 3: dt * (T + 3) + 3 + T],
               start=False, stop=True, waits=[("v", v_dsk)])
            evac("vector", "tensor_tensor",
                 (x_out[:, dt * T:(dt + 1) * T], x_in[:, dt * T:(dt + 1) * T],
                  PS[4][:, 0:T], ALU.add),
                 waits=[("t", kb.now("t"))], bank=4)
        t_last = kb.now("t")
        for dt in range(DT):
            kb.op("vector", "tensor_copy",
                  (u_f[:, dt * (T + 3): dt * (T + 3) + 3],
                   u_f[:, dt * (T + 3) + T: dt * (T + 3) + 3 + T]),
                  waits=[("t", t_last)] if dt == 0 else (), inc="v")
        return kb.now("s")

    def emit_ffn(x_in, x_out, ti, prod=()):
        def wdt(dt, t_bc):
            kb.op("vector", "tensor_tensor",
                  (lnt1[:], x_in[:, dt * T:(dt + 1) * T], PS[2][:, 0:T], ALU.mult),
                  waits=[("t", t_bc)], inc="v")
            kb.op("vector", "tensor_tensor", (xnf[:], lnt1[:], PS[6][:, 0:T], ALU.add),
                  waits=[("t", kb.now("t"))], inc="v")
            kb.op("vector", "tensor_copy",
                  (xnb[:, dt * T:(dt + 1) * T], xnf[:]), inc="v")
            gvx = vguard(xnf[:, T - 1:T])
            mm(1 if dt == 0 else None, PS[1][0:8, 0:T],
               rw_sb[:, dt * NEXP:(dt + 1) * NEXP], xnf[:],
               start=(dt == 0), stop=(dt == DT - 1), waits=[("v", gvx)])
        emit_ln(x_in, wdt, prod=prod)
        gv_xnb = vguard(xnb[:, DT * T - 1: DT * T])
        t_rt = kb.now("t")
        e_lg = evac("scalar", "activation", (lgrow[0:8, 0:T], PS[1][0:8, 0:T], AF.Identity),
                    dict(bias=rb_c[0:8, 0:1]), waits=[("t", t_rt), ("q", ldq)], bank=1)
        gs_lg = sguard(lgrow[0:8, T - 1:T])
        for pt in range(4):
            tp(1 if pt == 0 else None, PS[1][0:128, pt * 8:(pt + 1) * 8],
               lgrow[0:8, pt * 128:(pt + 1) * 128], identf[0:8, 0:8],
               waits=[("s", gs_lg)])
        t_lgt = kb.now("t")
        e_lgtm = evac("vector", "tensor_copy", (lgtm[:, :], PS[1][0:128, 0:32]),
                      waits=[("t", t_lgt)], bank=1)
        for pt in range(4):
            kb.op("vector", "reduce_max", (rc16[:, pt:pt + 1], lgtm[:, pt * 8:(pt + 1) * 8]),
                  dict(axis=AX.X), waits=[("v", e_lgtm)], inc="v")
            kb.op("vector", "tensor_scalar", (rc16[:, 4 + pt:5 + pt], rc16[:, pt:pt + 1],
                                              -1.0, None, ALU.mult), inc="v")
        v_nm = vguard(rc16[:, 7:8])
        for pt in range(4):
            kb.op("scalar", "activation", (mrow4[:, pt * 8:(pt + 1) * 8],
                                           lgtm[:, pt * 8:(pt + 1) * 8], AF.Exp),
                  dict(bias=rc16[:, 4 + pt:5 + pt]), waits=[("v", v_nm)], inc="s")
        for pt in range(4):
            kb.op("scalar", "activation", (srow4[:, pt * 8:(pt + 1) * 8],
                                           lgtm[:, pt * 8:(pt + 1) * 8], AF.Sign),
                  dict(bias=rc16[:, 4 + pt:5 + pt]), inc="s")
        s_es = sguard(srow4[:, 31:32])
        for pt in range(4):
            kb.op("vector", "reduce_sum", (rc16[:, 8 + pt:9 + pt],
                                           mrow4[:, pt * 8:(pt + 1) * 8]),
                  dict(axis=AX.X), waits=[("s", s_es)], inc="v")
            kb.op("vector", "reciprocal", (rc16[:, 12 + pt:13 + pt],
                                           rc16[:, 8 + pt:9 + pt]), inc="v")
            kb.op("vector", "tensor_scalar", (srow4[:, pt * 8:(pt + 1) * 8],
                                              srow4[:, pt * 8:(pt + 1) * 8], 1.0, None,
                                              ALU.add), inc="v")
            kb.op("vector", "tensor_scalar", (mtmb[:, pt * 8:(pt + 1) * 8],
                                              srow4[:, pt * 8:(pt + 1) * 8],
                                              rc16[:, 12 + pt:13 + pt], None, ALU.mult),
                  inc="v")
        gv_m = vguard(mtmb[:, 31:32])
        for pt in range(4):
            tp(1, PS[1][0:8, pt * 64:(pt + 1) * 64].bitcast(BF16),
               mtmb[:, pt * 8:(pt + 1) * 8], diag01[:], waits=[("v", gv_m)])
        t_mtp = kb.now("t")
        for pt in range(4):
            evac("scalar", "copy", (m_fm[0:8, pt * 128:(pt + 1) * 128],
                                    PS[1][0:8, pt * 64:(pt + 1) * 64].bitcast(BF16)),
                 waits=[("t", t_mtp)], bank=1)
        gs_mfm = sguard(m_fm[0:8, T - 1:T])

        for e in range(NEXP):
            mm(1, PS[1][:, 0:T], E8[:, e * 128:(e + 1) * 128], m_fm[0:8, 0:T],
               start=True, stop=True, waits=[("s", gs_mfm), ("g", g_en)])
            evac("scalar", "copy", (mbc_all[:, e * T:(e + 1) * T], PS[1][:, 0:T]),
                 waits=[("t", kb.now("t"))], bank=1)
        gs_mbc = sguard(mbc_all[:, NEXP * T - 1:NEXP * T])
        slot_consumed = [0, 0]

        def eload(e):
            slot = ew_slot if e % 2 == 0 else eslot2
            for i in range(DT):
                for ph in range(2):
                    gload(slot[ph * 64:(ph + 1) * 64, i * DT * 128:(i + 1) * DT * 128],
                          ew_d[e][i * 128 + ph * 64: i * 128 + (ph + 1) * 64, :],
                          waits=([("g", g_en)] if i == 0 and ph == 0 else ()),
                          sem="qe")
            kb.dma_barrier("qe")
            return kb.now("qe"), slot

        elq = {0: eload(0), 1: eload(1)}
        for e in range(NEXP):
            lqe, slot = elq.pop(e)
            for dt in range(DT):
                sl = dt % 2
                wv = [("s", gs_mbc), ("v", gv_xnb), ("t", slot_consumed[sl])]
                kb.op("vector", "tensor_tensor",
                      (xeb[:, sl * T:(sl + 1) * T], xnb[:, dt * T:(dt + 1) * T],
                       mbc_all[:, e * T:(e + 1) * T], ALU.mult), waits=wv, inc="v")
                gv_xe = vguard(xeb[:, (sl + 1) * T - 1:(sl + 1) * T])
                for ct in range(DT):
                    mm(ct if (e == 0 and dt == 0) else None, PS[ct][:, 0:T],
                       slot[:, (dt * DT + ct) * 128:(dt * DT + ct + 1) * 128],
                       xeb[:, sl * T:(sl + 1) * T],
                       start=(e == 0 and dt == 0), stop=False,
                       waits=[("v", gv_xe), ("qe", lqe)])
                slot_consumed[sl] = kb.now("t")
            if e + 2 < NEXP:
                elq[e + 2] = eload(e + 2)
        for ct in range(DT):
            mm(None, PS[ct][:, 0:T], eb_sb[0:8, ct * 128:(ct + 1) * 128], m_fm[0:8, 0:T],
               start=False, stop=True, waits=[("gq", gldq)])
            evac("vector", "tensor_tensor",
                 (x_out[:, ct * T:(ct + 1) * T], x_in[:, ct * T:(ct + 1) * T],
                  PS[ct][:, 0:T], ALU.add),
                 waits=[("t", kb.now("t"))], bank=ct)
        return kb.now("s")


    hsb = xnb
    wslot = ew_slot
    fir_slot = bmcb[:, 0:8448]
    pb_bc = bmcb[:, 8448:10496]
    q_f = bmcb[:, 10496:14616]
    hsW = bmcb[:, 0:4096]
    gelu_sb = bmcb[:, 4096:8192]
    do_f = x1f
    ls_f = ymt[:, 0:(N // 2) * T]
    ll_f = Ht[:, 0:(N // 2) * T]
    st68 = lnt1
    Xb = zt[:, 0:256].bitcast(BF16)
    uadj_sb = zt[:, 256:384].bitcast(BF16)
    kTM = zt[:, 384:512].bitcast(BF16)
    A_sb = at_[:, 0:64].bitcast(BF16)
    AT_sb = at_[:, 64:128].bitcast(BF16)
    Mp = [at_[:, 128:192].bitcast(BF16), at_[:, 192:256].bitcast(BF16)]
    MpT = [at_[:, 256:320].bitcast(BF16), at_[:, 320:384].bitcast(BF16)]
    wTn = at_[:, 384:512].bitcast(BF16)
    Sb_s = dAt[:, 0:T]
    attnT = ubm[:, 0:128]
    bT_c = rc16
    stat_mu = xnf
    lgt4 = lgtm

    def emit_attnb(x_in, x_out, ti):
        def wdt(dt, t_bc):
            kb.op("vector", "tensor_tensor",
                  (xnf[:], x_in[:, dt * T:(dt + 1) * T], PS[2][:, 0:T], ALU.mult),
                  waits=[("t", t_bc)], inc="v")
            kb.op("vector", "tensor_tensor",
                  (hsb[:, dt * T:(dt + 1) * T], xnf[:], PS[6][:, 0:T], ALU.add), inc="v")
        emit_ln(x_in, wdt, prod=[("s", kb.now("s"))])

        for ct in range(DT):
            kb.op("vector", "tensor_copy",
                  (q_f[:, ct * (T + 3): ct * (T + 3) + 3],
                   qtails[:, ct * 3:(ct + 1) * 3]), inc="v")
            kb.op("vector", "tensor_copy",
                  (k_f[:, ct * (T + 3): ct * (T + 3) + 3],
                   ktails[:, ct * 3:(ct + 1) * 3]), inc="v")

        def wload(W_d, slot):
            for i in range(DT):
                for ph in range(2):
                    gload(slot[ph * 64:(ph + 1) * 64, i * D:(i + 1) * D],
                          W_d[i * 128 + ph * 64: i * 128 + (ph + 1) * 64, :],
                          sem="qb")
            kb.dma_barrier("qb")
            return kb.now("qb")

        def proj_conv(lw, slot, cw_col, dst, span, off, tails, do_silu=True):
            for ct in range(DT):
                for dt in range(DT):
                    mm(0 if dt == 0 else None, PS[0][:, 0:T],
                       slot[:, dt * D + ct * 128: dt * D + (ct + 1) * 128],
                       hsb[:, dt * T:(dt + 1) * T],
                       start=(dt == 0), stop=(dt == DT - 1), waits=[("qb", lw)])
                base = ct * span + off
                evac("scalar", "copy", (dst[:, base: base + T], PS[0][:, 0:T]),
                     waits=[("t", kb.now("t"))], bank=0)
                if off == 62:
                    kb.op("vector", "tensor_copy",
                          (cvscr[:, 0:3], tails[:, ct * 3:(ct + 1) * 3]), inc="v")
                    kb.op("vector", "tensor_copy",
                          (cvscr[:, 3:6], dst[:, base: base + 3]), inc="v")
                kb.op("vector", "tensor_copy",
                      (tails[:, ct * 3:(ct + 1) * 3],
                       dst[:, base + T - 3: base + T]), inc="v")
                for i in range(4):
                    if i == 0:
                        kb.op("vector", "scalar_tensor_tensor",
                              (sqs[:], dst[:, base - 3 + i: base - 3 + i + T],
                               cw_col[:, ct * 4 + i: ct * 4 + i + 1],
                               dst[:, base - 3 + i: base - 3 + i + T],
                               ALU.mult, ALU.bypass), inc="v")
                    else:
                        kb.op("vector", "scalar_tensor_tensor",
                              (sqs[:], dst[:, base - 3 + i: base - 3 + i + T],
                               cw_col[:, ct * 4 + i: ct * 4 + i + 1],
                               sqs[:], ALU.mult, ALU.add), inc="v")
                if off == 62:
                    for i in range(4):
                        kb.op("vector", "scalar_tensor_tensor",
                              (sqs[:, 0:3], cvscr[:, i:i + 3],
                               cw_col[:, ct * 4 + i: ct * 4 + i + 1],
                               cvscr[:, i:i + 3] if i == 0 else sqs[:, 0:3],
                               ALU.mult, ALU.bypass if i == 0 else ALU.add),
                              inc="v")
                if do_silu:
                    kb.op("scalar", "activation", (dst[:, base: base + T], sqs[:],
                                                   AF.Silu), inc="s")
                else:
                    kb.op("vector", "tensor_copy", (dst[:, base: base + T], sqs[:]),
                          inc="v")

        pslot2 = bmcb[:, 0:DT * D]
        lq_w = wload(Wq_d, wslot)
        lk_w = wload(Wk_d, pslot2)
        proj_conv(lq_w, wslot, cwq_c, q_f, T + 3, 3, qtails)
        lv_w = wload(Wv_d, wslot)
        proj_conv(lk_w, pslot2, cwk_c, k_f, T + 3, 3, ktails)
        proj_conv(lv_w, wslot, cwv_c, v_f, T + 62, 62, vrawtails)

        for dt in range(DT):
            mm(1 if dt == 0 else None, PS[1][0:4, 0:T],
               wb_sb[:, dt * 4:(dt + 1) * 4], hsb[:, dt * T:(dt + 1) * T],
               start=(dt == 0), stop=(dt == DT - 1), waits=[("gq", gldq2)])
        evac("scalar", "activation", (lgrow[0:4, 0:T], PS[1][0:4, 0:T], AF.Sigmoid),
             waits=[("t", kb.now("t"))], bank=1)
        for c in range(4):
            tp(1, PS[1][0:128, 0:4], lgrow[0:4, c * 128:(c + 1) * 128],
               identf[0:4, 0:4])
            evac("scalar", "copy", (bT_c[:, c * 4:(c + 1) * 4], PS[1][0:128, 0:4]),
                 waits=[("t", kb.now("t"))], bank=1)
        kb.op("vector", "tensor_scalar", (mrow4[:, 0:16], bT_c[:, 0:16], -1.0, None,
                                          ALU.mult), inc="v")
        negbT = mrow4

        for th, dst in ((0, q_f), (1, k_f)):
            for h in range(NH):
                for half in range(2):
                    ct = 2 * h + half
                    kb.op("scalar", "activation",
                          (sqs[:], dst[:, ct * (T + 3) + 3: ct * (T + 3) + 3 + T],
                           AF.Square), inc="s")
                    mm(2 if half == 0 else None, PS[2][0:1, 0:T], onescf[:], sqs[:],
                       start=(half == 0), stop=(half == 1))
                e_r = evac("scalar", "activation",
                           (rowbuf[0:1, 0:T], PS[2][0:1, 0:T], AF.Sqrt),
                           dict(bias=cepsl2[0:1, 0:1]),
                           waits=[("t", kb.now("t"))], bank=2)
                kb.op("vector", "reciprocal", (rowbuf[0:1, T:2 * T], rowbuf[0:1, 0:T]),
                      inc="v")
                mm(2, PS[2][:, 0:T], onesrow[0:1, :], rowbuf[0:1, T:2 * T],
                   start=True, stop=True)
                e_bc = evac("scalar", "copy", (sqs[:], PS[2][:, 0:T]),
                            waits=[("t", kb.now("t"))], bank=2)
                for half in range(2):
                    ct = 2 * h + half
                    sl = dst[:, ct * (T + 3) + 3: ct * (T + 3) + 3 + T]
                    kb.op("vector", "tensor_tensor", (sl, sl, sqs[:], ALU.mult),
                          inc="v")

        def fload(ct, half):
            base = half * 33 * 128
            for i in range(4):
                for ph in range(2):
                    load(fir_slot[ph * 64:(ph + 1) * 64,
                                  base + i * 1056: base + (i + 1) * 1056],
                         fir_dram[ct][ph * 64:(ph + 1) * 64,
                                      base + i * 1056: base + (i + 1) * 1056],
                         sem="qf")
            kb.dma_barrier("qf")
            return kb.now("qf")

        lfh = {(0, 0): fload(0, 0), (0, 1): fload(0, 1)}
        for ct in range(DT):
            base = ct * (T + 62)
            for tau in range(33):
                mm(7 if tau == 0 else None, PS[7][:, 0:T],
                   fir_slot[:, tau * 128:(tau + 1) * 128],
                   v_f[:, base + tau: base + tau + T],
                   start=(tau == 0), stop=False, waits=[("qf", lfh[(ct, 0)])])
            if ct + 1 < DT:
                lfh[(ct + 1, 0)] = fload(ct + 1, 0)
            for tau in range(33, 63):
                mm(None, PS[7][:, 0:T],
                   fir_slot[:, tau * 128:(tau + 1) * 128],
                   v_f[:, base + tau: base + tau + T],
                   start=False, stop=(tau == 62), waits=[("qf", lfh[(ct, 1)])])
            evac("scalar", "copy", (ll_f[:, ct * T:(ct + 1) * T], PS[7][:, 0:T]),
                 waits=[("t", kb.now("t"))], bank=7)
            for i in range(3):
                mm(7 if i == 0 else None, PS[7][:, 0:T],
                   fir_slot[:, (63 + i) * 128:(64 + i) * 128],
                   v_f[:, base + 60 + i: base + 60 + i + T],
                   start=(i == 0), stop=(i == 2), waits=[("qf", lfh[(ct, 1)])])
            evac("scalar", "copy", (ls_f[:, ct * T:(ct + 1) * T], PS[7][:, 0:T]),
                 waits=[("t", kb.now("t"))], bank=7)
            if ct + 1 < DT:
                lfh[(ct + 1, 1)] = fload(ct + 1, 1)

        for i in range(DT):
            for ph in range(2):
                gload(wslot[ph * 64:(ph + 1) * 64, i * D:(i + 1) * D],
                  gW1_d[i * 128 + ph * 64: i * 128 + (ph + 1) * 64, :],
                  sem="qb")
        kb.dma_barrier("qb")
        lw1 = kb.now("qb")
        for ct in range(DT):
            for dt in range(DT):
                mm(0 if dt == 0 else None, PS[0][:, 0:T],
                   wslot[:, dt * D + ct * 128: dt * D + (ct + 1) * 128],
                   hsb[:, dt * T:(dt + 1) * T],
                   start=(dt == 0), stop=(dt == DT - 1), waits=[("qb", lw1)])
            evac("scalar", "copy", (hsW[:, ct * T:(ct + 1) * T], PS[0][:, 0:T]),
                 waits=[("t", kb.now("t"))], bank=0)
        for i in range(DT):
            for ph in range(2):
                gload(wslot[ph * 64:(ph + 1) * 64, i * D:(i + 1) * D],
                  Wid_d[i * 128 + ph * 64: i * 128 + (ph + 1) * 64, :],
                  sem="qb")
        kb.dma_barrier("qb")
        lid = kb.now("qb")
        wslot2 = wslot
        dscr = delta[:, :].bitcast(BF16)
        obuf1 = bbt
        obuf2 = dscr[:, 2 * T:3 * T]
        stb = dscr[:, 3 * T:4 * T]
        l2b = dscr[0:4, T:2 * T]
        for h in range(NH):
            kb.op("vector", "tensor_copy", (Sb_s[:, 0:T], S_f[:, h * 512: h * 512 + 512]),
                  inc="v")
            for c in range(4):
                c0 = c * 128
                kcol = lambda half: k_f[:, (2 * h + half) * (T + 3) + 3 + c0:
                                        (2 * h + half) * (T + 3) + 3 + c0 + 128]
                qcol = lambda half: q_f[:, (2 * h + half) * (T + 3) + 3 + c0:
                                        (2 * h + half) * (T + 3) + 3 + c0 + 128]
                vcol = lambda half: v_f[:, (2 * h + half) * (T + 62) + 62 + c0:
                                        (2 * h + half) * (T + 62) + 62 + c0 + 128]
                bcol = bT_c[:, c * 4 + h: c * 4 + h + 1]
                nbcol = negbT[:, c * 4 + h: c * 4 + h + 1]
                for half in range(2):
                    mm(1 if half == 0 else None, PS[1][:, 0:128], kcol(half), kcol(half),
                       start=(half == 0), stop=(half == 1))
                evac("vector", "scalar_tensor_tensor",
                     (A_sb[:, :], PS[1][:, 0:128], nbcol, mask_sl[:], ALU.mult,
                      ALU.mult), waits=[("t", kb.now("t"))], bank=1)
                for half in range(2):
                    mm(1 if half == 0 else None, PS[1][:, 0:128], kcol(half), qcol(half),
                       start=(half == 0), stop=(half == 1))
                evac("vector", "tensor_tensor", (attnT[:, :], PS[1][:, 0:128],
                                                 mask_ui[:], ALU.mult),
                     waits=[("t", kb.now("t"))], bank=1)
                for half in range(2):
                    tp(1, PS[1][0:128, half * 64:(half + 1) * 64].bitcast(BF16),
                       kcol(half), diag01[:])
                t_ktp = kb.now("t")
                evac("scalar", "copy", (kTM[:, 0:256], PS[1][0:128, 0:128].bitcast(BF16)),
                     waits=[("t", t_ktp)], bank=1)
                kb.op("vector", "scalar_tensor_tensor",
                      (Xb[:, 256:512], kTM[:, 0:256], bcol, kTM[:, 0:256],
                       ALU.mult, ALU.bypass), inc="v")
                for half in range(2):
                    tp(1, PS[1][0:128, half * 64:(half + 1) * 64].bitcast(BF16),
                       vcol(half), diag01[:])
                t_vtp = kb.now("t")
                evac("vector", "scalar_tensor_tensor",
                     (Xb[:, 0:256], PS[1][0:128, 0:128].bitcast(BF16), bcol,
                      kTM[:, 0:256], ALU.mult, ALU.bypass),
                     waits=[("t", t_vtp)], bank=1)
                tp(1, PS[1][0:128, 0:64].bitcast(BF16), A_sb[:, :], diag01[:])
                evac("scalar", "copy", (AT_sb[:, :], PS[1][0:128, 0:64].bitcast(BF16)),
                     waits=[("t", kb.now("t"))], bank=1)
                if STAGE == "dbgb" and ti == 0 and h == 0 and c == 0:
                    kb.op("gpsimd", "dma_start", (),
                          dict(out=y_d[776:904, 0:512], in_=Xb[:, 0:512]),
                          waits=[("v", kb.now("v")), ("s", kb.now("s"))],
                          inc="store", incn=16)
                    kb.op("gpsimd", "dma_start", (),
                          dict(out=y_d[904:1032, 0:128], in_=A_sb[:, :]),
                          inc="store", incn=16)
                    kb.op("gpsimd", "dma_start", (),
                          dict(out=y_d[1032:1160, 0:128], in_=attnT[:, :]),
                          inc="store", incn=16)
                    kb.dma_barrier("store")
                    kb.op("vector", "nop", (), waits=[("store", kb.now("store"))], inc="v")
                    kb.op("scalar", "nop", (), waits=[("store", kb.now("store"))], inc="s")
                cur, nxt = 0, 1
                kb.op("vector", "tensor_copy", (Mp[0][:, :], A_sb[:, :]), inc="v")
                kb.op("vector", "tensor_copy", (MpT[0][:, :], AT_sb[:, :]), inc="v")
                for j in range(7):
                    mm(4, PS[4][:, 0:T], MpT[cur][:, :], Xb[:, 0:512],
                       start=True, stop=True)
                    evac("vector", "tensor_tensor",
                         (Xb[:, 0:512], Xb[:, 0:512], PS[4][:, 0:T], ALU.add),
                         waits=[("t", kb.now("t"))], bank=4)
                    if j < 6:
                        mm(1, PS[1][:, 0:128], MpT[cur][:, :],
                           Mp[cur][:, :], start=True, stop=True)
                        mm(5, PS[5][:, 0:128], Mp[cur][:, :],
                           MpT[cur][:, :], start=True, stop=True)
                        evac("scalar", "copy", (Mp[nxt][:, :], PS[1][:, 0:128]),
                             waits=[("t", kb.now("t"))], bank=1)
                        evac("scalar", "copy", (MpT[nxt][:, :], PS[5][:, 0:128]),
                             bank=5)
                        cur, nxt = nxt, cur
                for half in range(2):
                    tp(1, PS[1][0:128, half * 64:(half + 1) * 64].bitcast(BF16),
                       Xb[:, 256 + half * 128: 256 + (half + 1) * 128], diag01[:])
                evac("vector", "tensor_scalar",
                     (wTn[:, 0:256], PS[1][0:128, 0:128].bitcast(BF16), -1.0, None,
                      ALU.mult), waits=[("t", kb.now("t"))], bank=1)
                for half in range(2):
                    mm(5 if half == 0 else None, PS[5][:, 0:256],
                       wTn[:, half * 128:(half + 1) * 128],
                       Sb_s[:, half * 256:(half + 1) * 256],
                       start=(half == 0), stop=(half == 1))
                evac("vector", "tensor_tensor",
                     (uadj_sb[:, 0:256], Xb[:, 0:256], PS[5][:, 0:256], ALU.add),
                     waits=[("t", kb.now("t"))], bank=5)
                if STAGE == "dbgb" and ti == 0 and h == 0 and c == 0:
                    kb.op("gpsimd", "dma_start", (),
                          dict(out=y_d[1160:1288, 0:512], in_=Xb[:, 0:512]),
                          waits=[("v", kb.now("v")), ("s", kb.now("s"))],
                          inc="store", incn=16)
                    kb.op("gpsimd", "dma_start", (),
                          dict(out=y_d[1288:1416, 0:256], in_=uadj_sb[:, 0:256]),
                          inc="store", incn=16)
                    kb.dma_barrier("store")
                    kb.op("vector", "nop", (), waits=[("store", kb.now("store"))], inc="v")
                    kb.op("scalar", "nop", (), waits=[("store", kb.now("store"))], inc="s")
                for half in range(2):
                    mm(5, PS[5][:, 0:256], kTM[:, half * 128:(half + 1) * 128],
                       uadj_sb[:, 0:256], start=True, stop=True)
                    kb.op("vector", "tensor_tensor",
                          (S_f[:, h * 512 + half * 256: h * 512 + (half + 1) * 256],
                           S_f[:, h * 512 + half * 256: h * 512 + (half + 1) * 256],
                           PS[5][:, 0:256], ALU.add),
                          waits=[("t", kb.now("t"))], inc="v")
                    bank_free[5] = ("v", kb.now("v"))
                for half in range(2):
                    mm(4 if half == 0 else None, PS[4][:, 0:256], qcol(half),
                       Sb_s[:, half * 256:(half + 1) * 256],
                       start=(half == 0), stop=False)
                mm(None, PS[4][:, 0:256], attnT[:, :], uadj_sb[:, 0:256],
                   start=False, stop=True)
                evac("scalar", "copy", (kTM[:, 0:256], PS[4][:, 0:256]),
                     waits=[("t", kb.now("t"))], bank=4)
                for half in range(2):
                    tp(1, PS[1][0:128, half * 64:(half + 1) * 64].bitcast(BF16),
                       kTM[:, half * 128:(half + 1) * 128], diag01[:])
                t_otp = kb.now("t")
                for half in range(2):
                    evac("scalar", "copy",
                         (do_f[:, (2 * h + half) * T + c0: (2 * h + half) * T + c0 + 128],
                          PS[1][0:128, half * 64:(half + 1) * 64].bitcast(BF16)),
                         waits=[("t", t_otp)], bank=(1 if half == 1 else None))
                kb.op("vector", "tensor_copy", (Sb_s[:, 0:T],
                                                S_f[:, h * 512: h * 512 + 512]), inc="v")


            first = True
            for bi, brt in ((0, ls_f), (1, ll_f), (2, do_f), (3, None)):
                for half in range(2):
                    ct = 2 * h + half
                    if brt is None:
                        xs = v_f[:, ct * (T + 62) + 62: ct * (T + 62) + 62 + T]
                    else:
                        xs = brt[:, ct * T:(ct + 1) * T]
                    mm(6 if first else None, PS[6][0:68, 0:T],
                       selst[:, (bi * 3 + 0) * 68:(bi * 3 + 1) * 68], xs,
                       start=first, stop=False)
                    first = False
                    sqb = dscr[:, 0:T]
                    kb.op("scalar", "activation", (sqb, xs, AF.Square), inc="s")
                    mm(None, PS[6][0:68, 0:T],
                       selst[:, (bi * 3 + 1) * 68:(bi * 3 + 2) * 68], sqb,
                       start=False, stop=False)
                    kb.op("scalar", "activation", (sqb, xs, AF.Abs), inc="s")
                    last = (bi == 3 and half == 1)
                    mm(None, PS[6][0:68, 0:T],
                       selst[:, (bi * 3 + 2) * 68:(bi * 3 + 3) * 68], sqb,
                       start=False, stop=last)
            evac("scalar", "copy", (st68[0:68, 0:T], PS[6][0:68, 0:T]),
                 waits=[("t", kb.now("t"))], bank=6)
            kb.op("scalar", "activation", (stat_mu[32:36, 0:T], st68[32:36, 0:T],
                                           AF.Sqrt), dict(scale=256.0), inc="s")
            kb.op("vector", "tensor_copy", (l2b, stat_mu[32:36, 0:T]), inc="v")
            kb.op("scalar", "activation", (stat_mu[32:36, 0:T], st68[0:4, 0:T],
                                           AF.Square), inc="s")
            kb.op("vector", "tensor_sub", (st68[32:36, 0:T], st68[32:36, 0:T],
                                           stat_mu[32:36, 0:T]), inc="v")
            kb.op("vector", "tensor_copy", (stb[:, 0:T], st68[:, 0:T]), inc="v")
            for ct in range(DT):
                mm(0, PS[0][:, 0:T], diag01[:], hsW[:, ct * T:(ct + 1) * T],
                   start=True, stop=False)
                mm(None, PS[0][:, 0:T], gw1s_p1[:, ct * 128:(ct + 1) * 128],
                   stb[:, 0:T], start=False, stop=False)
                mm(None, PS[0][:, 0:T], gw1s_p2[:, ct * 128:(ct + 1) * 128],
                   l2b, start=False, stop=True)
                evac("scalar", "activation", (gelu_sb[:, ct * T:(ct + 1) * T],
                                              PS[0][:, 0:T], AF.Gelu),
                     dict(bias=gb1_c[:, ct:ct + 1]),
                     waits=[("t", kb.now("t"))], bank=0)
            for ct in range(DT):
                mm(3 if ct == 0 else None, PS[3][0:4, 0:T],
                   gw2_sb[:, ct * 4:(ct + 1) * 4], gelu_sb[:, ct * T:(ct + 1) * T],
                   start=(ct == 0), stop=(ct == DT - 1), waits=[("gq", gldq2)])
            evac("scalar", "activation", (lgrow[0:4, 0:T], PS[3][0:4, 0:T],
                                          AF.Identity), dict(bias=gb2_c[0:4, 0:1]),
                 waits=[("t", kb.now("t"))], bank=3)
            kb.op("vector", "tensor_scalar", (lgrow[0:4, 0:T], lgrow[0:4, 0:T],
                                              invt_c[0:4, h:h + 1], None, ALU.mult),
                  inc="v")
            for c in range(4):
                tp(3, PS[3][0:128, c * 4:(c + 1) * 4],
                   lgrow[0:4, c * 128:(c + 1) * 128], identf[0:4, 0:4])
            evac("vector", "tensor_copy", (lgt4[:, 0:16], PS[3][0:128, 0:16]),
                 waits=[("t", kb.now("t"))], bank=3)
            for c in range(4):
                sl = lgt4[:, c * 4:(c + 1) * 4]
                kb.op("vector", "reduce_max", (rcol[:, 0:1], sl), dict(axis=AX.X),
                      inc="v")
                kb.op("vector", "tensor_scalar", (rcol[:, 1:2], rcol[:, 0:1], -1.0,
                                                  None, ALU.mult), inc="v")
                kb.op("scalar", "activation", (sl, sl, AF.Exp), dict(bias=rcol[:, 1:2]),
                      inc="s")
                kb.op("vector", "reduce_sum", (rcol[:, 2:3], sl), dict(axis=AX.X),
                      inc="v")
                kb.op("vector", "reciprocal", (rcol[:, 3:4], rcol[:, 2:3]), inc="v")
                kb.op("vector", "tensor_scalar", (sl, sl, rcol[:, 3:4], None,
                                                  ALU.mult), inc="v")
                kb.op("vector", "tensor_scalar", (sl, sl, fs_bc[:, h:h + 1], None,
                                                  ALU.mult), inc="v")
                kb.op("vector", "tensor_add", (sl, sl, fl_bc[:, h * 4:(h + 1) * 4]),
                      inc="v")
            kb.op("vector", "tensor_copy", (mtmb[:, 0:16], lgt4[:, 0:16]), inc="v")
            for c in range(4):
                tp(3, PS[3][0:4, c * 64:(c + 1) * 64].bitcast(BF16),
                   mtmb[:, c * 4:(c + 1) * 4], diag01[:])
            t_ptp = kb.now("t")
            evac("scalar", "copy", (m_fm[0:4, 0:T],
                                    PS[3][0:4, 0:256].bitcast(BF16)),
                 waits=[("t", t_ptp)], bank=3)
            for b in range(4):
                mm(3, PS[3][:, 0:T], E4[:, b * 128:(b + 1) * 128], m_fm[0:4, 0:T],
                   start=True, stop=True)
                evac("scalar", "copy", (pb_bc[:, b * T:(b + 1) * T], PS[3][:, 0:T]),
                     waits=[("t", kb.now("t"))], bank=3)
            for half in range(2):
                ct = 2 * h + half
                for dt in range(DT):
                    mm(7 if dt == 0 else None, PS[7][:, 0:T],
                       wslot2[:, dt * D + ct * 128: dt * D + (ct + 1) * 128],
                       hsb[:, dt * T:(dt + 1) * T],
                       start=(dt == 0), stop=(dt == DT - 1), waits=[("qb", lid)])
                vsl = v_f[:, ct * (T + 62) + 62: ct * (T + 62) + 62 + T]
                kb.op("vector", "tensor_tensor", (obuf1[:, 0:T],
                      ls_f[:, ct * T:(ct + 1) * T], pb_bc[:, 0:T], ALU.mult), inc="v")
                kb.op("vector", "tensor_tensor", (obuf2[:, 0:T],
                      ll_f[:, ct * T:(ct + 1) * T], pb_bc[:, T:2 * T], ALU.mult),
                      inc="v")
                kb.op("vector", "tensor_add", (obuf1[:, 0:T], obuf1[:, 0:T],
                                               obuf2[:, 0:T]), inc="v")
                kb.op("vector", "tensor_tensor", (obuf2[:, 0:T],
                      do_f[:, ct * T:(ct + 1) * T], pb_bc[:, 2 * T:3 * T], ALU.mult),
                      inc="v")
                kb.op("vector", "tensor_add", (obuf1[:, 0:T], obuf1[:, 0:T],
                                               obuf2[:, 0:T]), inc="v")
                kb.op("vector", "tensor_tensor", (obuf2[:, 0:T], vsl,
                      pb_bc[:, 3 * T:4 * T], ALU.mult), inc="v")
                kb.op("vector", "tensor_add", (obuf1[:, 0:T], obuf1[:, 0:T],
                                               obuf2[:, 0:T]), inc="v")
                kb.op("vector", "scalar_tensor_tensor",
                      (obuf1[:, 0:T], ls_f[:, ct * T:(ct + 1) * T],
                       scres[:, h:h + 1], obuf1[:, 0:T], ALU.mult, ALU.add), inc="v")
                kb.op("vector", "scalar_tensor_tensor",
                      (do_f[:, ct * T:(ct + 1) * T], PS[7][:, 0:T],
                       acol[:, h:h + 1], obuf1[:, 0:T], ALU.mult, ALU.add),
                      waits=[("t", kb.now("t"))], inc="v")
                bank_free[7] = ("v", kb.now("v"))
            for half in range(2):
                ct = 2 * h + half
                kb.op("scalar", "activation", (sqs[:], do_f[:, ct * T:(ct + 1) * T],
                                               AF.Square), inc="s")
                mm(6 if half == 0 else None, PS[6][0:1, 0:T], onescf[:], sqs[:],
                   start=(half == 0), stop=(half == 1))
            e_ms = evac("scalar", "activation", (rowbuf[0:1, 0:T], PS[6][0:1, 0:T],
                                                 AF.Sqrt),
                        dict(scale=1.0 / DV, bias=cepso[0:1, 0:1]),
                        waits=[("t", kb.now("t"))], bank=6)
            kb.op("vector", "reciprocal", (rowbuf[0:1, T:2 * T], rowbuf[0:1, 0:T]),
                  inc="v")
            mm(6, PS[6][:, 0:T], onesrow[0:1, :], rowbuf[0:1, T:2 * T],
               start=True, stop=True)
            e_rb = evac("scalar", "copy", (obuf2[:, 0:T], PS[6][:, 0:T]),
                        waits=[("t", kb.now("t"))], bank=6)
            for half in range(2):
                ct = 2 * h + half
                kb.op("vector", "scalar_tensor_tensor",
                      (do_f[:, ct * T:(ct + 1) * T], do_f[:, ct * T:(ct + 1) * T],
                       onw_c[:, half:half + 1], obuf2[:, 0:T], ALU.mult, ALU.mult),
                      inc="v")

        for i in range(DT):
            for ph in range(2):
                gload(wslot[ph * 64:(ph + 1) * 64, i * D:(i + 1) * D],
                  Wo_d[i * 128 + ph * 64: i * 128 + (ph + 1) * 64, :],
                  sem="qb")
        kb.dma_barrier("qb")
        lwo = kb.now("qb")
        for ct in range(DT):
            for dt in range(DT):
                mm(ct if dt == 0 else None, PS[ct][:, 0:T],
                   wslot[:, dt * D + ct * 128: dt * D + (ct + 1) * 128],
                   do_f[:, dt * T:(dt + 1) * T],
                   start=(dt == 0), stop=(dt == DT - 1), waits=[("qb", lwo)])
        for ct in range(DT):
            evac("vector", "tensor_tensor",
                 (x_out[:, ct * T:(ct + 1) * T], x_in[:, ct * T:(ct + 1) * T],
                  PS[ct][:, 0:T], ALU.add),
                 waits=[("t", kb.now("t"))], bank=ct)

        for ct in range(DT):
            kb.op("vector", "tensor_copy",
                  (v_f[:, ct * (T + 62): ct * (T + 62) + 62],
                   v_f[:, ct * (T + 62) + T: ct * (T + 62) + T + 62]), inc="v")
        return kb.now("s")

    for ti in range(NTILE):
        for pt in range(4):
            w = [("store", kb.now("store"))]
            if pt == 0:
                w.append(("q", kb.now("q")))
            load(xtm[:, pt * D:(pt + 1) * D],
                 x0_d[ti * T + pt * 128: ti * T + (pt + 1) * 128, :], waits=w)
        lq = kb.now("q")
        kb.dma_barrier("q")
        for dt in range(DT):
            for pt in range(4):
                tp(3 if pt == 0 else None,
                   PS[3][0:128, pt * 128:(pt + 1) * 128],
                   xtm[:, pt * D + dt * 128: pt * D + (dt + 1) * 128],
                   identf[:], waits=[("q", lq), ("v", gv_id)])
            evac("scalar", "copy", (x0f[:, dt * T:(dt + 1) * T], PS[3][0:128, 0:512]),
                 waits=[("t", kb.now("t"))], bank=3)

        s_ld = kb.now("s")
        if ti > 0 and STAGE in ("x3", "x4", "full", "dbgb"):
            for i in range(4):
                load(k_f[:, i * DT * 112:(i + 1) * DT * 112],
                     xptf_dram[:, i * DT * 112:(i + 1) * DT * 112])
            kb.dma_barrier("q")
        emit_mamba(x0f, x1f, prod=[("s", s_ld)])
        if STAGE != "x1":
            emit_ffn(x1f, x0f, ti, prod=[("s", kb.now("s"))])
        if STAGE in ("x3", "x4", "full", "dbgb"):
            emit_attnb(x0f, x1f, ti)
        if STAGE == "dbgb" and ti == 0:
            for nm, buf, ncols in (("qf", bmcb[:, 10496:14616], 4120),
                                   ("kf", k_f[:, 0:4120], 4120),
                                   ("vf", v_f[:, 0:4592], 4592),
                                   ("dof", x1f[:, 0:4096], 4096),
                                   ("lsf", ymt[:, 0:4096], 4096),
                                   ("llf", Ht[:, 0:4096], 4096)):
                pass
            kb.op("gpsimd", "dma_start", (),
                  dict(out=y_d[0:128, :], in_=bmcb[:, 10496:11520]),
                  waits=[("v", kb.now("v")), ("s", kb.now("s"))], inc="store", incn=16)
            kb.op("gpsimd", "dma_start", (),
                  dict(out=y_d[128:256, :], in_=k_f[:, 0:1024]), inc="store", incn=16)
            kb.op("gpsimd", "dma_start", (),
                  dict(out=y_d[256:384, :], in_=v_f[:, 0:1024]), inc="store", incn=16)
            kb.op("gpsimd", "dma_start", (),
                  dict(out=y_d[384:512, :], in_=x1f[:, 0:1024]), inc="store", incn=16)
            kb.op("gpsimd", "dma_start", (),
                  dict(out=y_d[512:640, :], in_=ymt[:, 0:1024]), inc="store", incn=16)
            kb.op("gpsimd", "dma_start", (),
                  dict(out=y_d[640:768, :], in_=Ht[:, 0:1024]), inc="store", incn=16)
            kb.op("gpsimd", "dma_start", (),
                  dict(out=y_d[768:776, :], in_=hsb[:, 0:0 + 1024][0:8, :]),
                  inc="store", incn=16)
            kb.dma_barrier("store")
            kb.op("vector", "nop", (), waits=[("store", kb.now("store"))], inc="v")
            kb.op("scalar", "nop", (), waits=[("store", kb.now("store"))], inc="s")
        if STAGE in ("x4", "full"):
            emit_ffn(x1f, x0f, ti, prod=[("s", kb.now("s"))])

        if STAGE == "dbg" and ti == 0:
            kb.op("gpsimd", "dma_start", (),
                  dict(out=y_d[512:1024, :].rearrange("(g p) c -> p g c", p=128),
                       in_=xnb[:, :].rearrange("p (g c) -> p g c", c=1024)),
                  waits=[("v", kb.now("v"))], inc="store", incn=16)
            kb.op("gpsimd", "dma_start", (),
                  dict(out=y_d[128:256, 0:32], in_=lgtm[:, :]),
                  waits=[("s", kb.now("s"))], inc="store", incn=16)
            kb.op("gpsimd", "dma_start", (),
                  dict(out=y_d[256:264, 0:T], in_=m_fm[0:8, :]),
                  waits=[("s", kb.now("s"))], inc="store", incn=16)
            kb.op("gpsimd", "dma_start", (),
                  dict(out=y_d[384:512, 0:32], in_=mrow4[:, :]),
                  waits=[("s", kb.now("s"))], inc="store", incn=16)
            kb.op("gpsimd", "dma_start", (),
                  dict(out=y_d[384:512, 32:64], in_=srow4[:, :]),
                  waits=[("s", kb.now("s"))], inc="store", incn=16)
            kb.op("gpsimd", "dma_start", (),
                  dict(out=y_d[384:512, 64:80], in_=rc16[:, :]),
                  waits=[("s", kb.now("s"))], inc="store", incn=16)
            kb.op("gpsimd", "dma_start", (),
                  dict(out=y_d[384:512, 80:112], in_=mtmb[:, :]),
                  waits=[("s", kb.now("s"))], inc="store", incn=16)
            kb.op("gpsimd", "dma_start", (),
                  dict(out=y_d[1024:1536, :].rearrange("(g p) c -> p g c", p=128),
                       in_=x0f[:, :].rearrange("p (g c) -> p g c", c=1024)),
                  waits=[("s", kb.now("s"))], inc="store", incn=16)
        if STAGE == "dbg" and ti == 0:
            kb.op("vector", "nop", (), waits=[("store", kb.now("store"))], inc="v")
            kb.op("scalar", "nop", (), waits=[("store", kb.now("store"))], inc="s")
        if STAGE in ("dbg", "dbgb"):
            continue
        out_src = {"x1": x1f, "x2": x0f, "dbg": x0f, "x3": x1f, "x4": x0f, "full": x0f, "dbgb": x1f}[STAGE]
        gs_o = sguard(out_src[:, DT * T - 1: DT * T])
        for pt in range(4):
            for dt in range(DT):
                bank = 3 if dt == 0 else (7 if dt == 4 else None)
                dst = PS[3][0:128, dt * 64:(dt + 1) * 64].bitcast(BF16) if dt < 4 else \
                    PS[7][0:128, (dt - 4) * 64:(dt - 3) * 64].bitcast(BF16)
                tp(bank, dst, out_src[:, dt * T + pt * 128: dt * T + (pt + 1) * 128],
                   diag01[:], waits=[("s", gs_o)])
            t_o = kb.now("t")
            evac("vector", "tensor_copy",
                 (xtm[:, pt * D: pt * D + 512], PS[3][0:128, 0:256].bitcast(BF16)),
                 waits=[("t", t_o)], bank=3)
            e2 = evac("vector", "tensor_copy",
                      (xtm[:, pt * D + 512: pt * D + 1024],
                       PS[7][0:128, 0:256].bitcast(BF16)),
                      waits=[("t", t_o)], bank=7)
            kb.op("sync", "dma_start", (),
                  dict(out=y_d[ti * T + pt * 128: ti * T + (pt + 1) * 128, :],
                       in_=xtm[:, pt * D:(pt + 1) * D]),
                  waits=[("v", e2)], inc="store", incn=16)
        kb.dma_barrier("store")
    n_store = kb.now("store")

    with contextlib.ExitStack() as st_:
        sems = {s: st_.enter_context(nc.semaphore(s)) for s in kb.semnames}
        with nc.Block() as block:
            @block.sync
            def _(e):
                kb.replay(e, "sync", sems)
                e.wait_ge(sems["store"], n_store)

            @block.gpsimd
            def _(e):
                kb.replay(e, "gpsimd", sems)

            @block.vector
            def _(e):
                kb.replay(e, "vector", sems)

            @block.scalar
            def _(e):
                kb.replay(e, "scalar", sems)

            @block.tensor
            def _(e):
                kb.replay(e, "tensor", sems)

    return nc


def kernel(**inputs):
    nc = build_core_program()
    x0 = np.ascontiguousarray(np.asarray(inputs["L0_attn_a_u"], np.float32))
    base = {
        "aw_conv_w": inputs["L0_attn_a_conv_w"],
        "aw_conv_b": inputs["L0_attn_a_conv_b"],
        "aw_xpw": inputs["L0_attn_a_x_proj_w"],
        "aw_dtw": inputs["L0_attn_a_dt_proj_w"],
        "aw_dtb": inputs["L0_attn_a_dt_proj_b"],
        "aw_Alog": inputs["L0_attn_a_A_log"],
        "aw_Dskip": inputs["L0_attn_a_D_skip"],
        "fw_rw": inputs["L0_ffn_router_w"],
        "fw_rb": inputs["L0_ffn_router_b"],
        "fw_ew": inputs["L0_ffn_expert_w"],
        "fw_eb": inputs["L0_ffn_expert_b"],
        "bw_Wq": inputs["L0_attn_b_Wq"],
        "bw_Wk": inputs["L0_attn_b_Wk"],
        "bw_Wv": inputs["L0_attn_b_Wv"],
        "bw_cwq": inputs["L0_attn_b_cwq"],
        "bw_cwk": inputs["L0_attn_b_cwk"],
        "bw_cwv": inputs["L0_attn_b_cwv"],
        "bw_Wb": inputs["L0_attn_b_Wb"],
        "bw_Wid": inputs["L0_attn_b_Wid"],
        "bw_alpha": inputs["L0_attn_b_alpha_id"],
        "bw_firs": inputs["L0_attn_b_firs"],
        "bw_firl": inputs["L0_attn_b_firl"],
        "bw_gW1": inputs["L0_attn_b_gW1"],
        "bw_gb1": inputs["L0_attn_b_gb1"],
        "bw_gW2": inputs["L0_attn_b_gW2"],
        "bw_gb2": inputs["L0_attn_b_gb2"],
        "bw_ltemp": inputs["L0_attn_b_log_temp"],
        "bw_floor": inputs["L0_attn_b_floor_param"],
        "bw_cres": inputs["L0_attn_b_conv_res_logit"],
        "bw_onw": inputs["L0_attn_b_o_norm_w"],
        "bw_Wo": inputs["L0_attn_b_Wo"],
    }
    base = {k: np.ascontiguousarray(np.asarray(v, np.float32)) for k, v in base.items()}
    in_maps = []
    for c in range(8):
        m = dict(base)
        m["x0"] = np.ascontiguousarray(x0[c])
        in_maps.append(m)
    res = run_bass_kernel_spmd(nc, in_maps, core_ids=list(range(8)))
    return np.stack([res.results[c]["y"] for c in range(8)], axis=0)


# version 13 (final)
# speedup vs baseline: 1.0131x; 1.0013x over previous
import contextlib
import numpy as np
import ml_dtypes
import concourse.bass as bass
import concourse.mybir as mybir
from concourse.bass_utils import run_bass_kernel_spmd

F32 = mybir.dt.float32
BF16 = mybir.dt.bfloat16
AF = mybir.ActivationFunctionType
ALU = mybir.AluOpType
AX = mybir.AxisListType

D = 1024
DT = 8
L = 4096
T = 512
NTILE = L // T
N = 16
NH, DK, DV = 4, 256, 256
NEXP = 8
STAGE = "full"

ENGS = ("sync", "gpsimd", "vector", "scalar", "tensor")
SEM_OF = {"vector": "v", "scalar": "s", "gpsimd": "g", "tensor": "t", "sync": "q"}

_IO = {
    "tensor_copy": ((0,), (1,), ()),
    "tensor_tensor": ((0,), (1, 2), ()),
    "tensor_add": ((0,), (1, 2), ()),
    "tensor_sub": ((0,), (1, 2), ()),
    "tensor_mul": ((0,), (1, 2), ()),
    "tensor_scalar": ((0,), (1, 2, 3), ()),
    "scalar_tensor_tensor": ((0,), (1, 2, 3), ()),
    "tensor_tensor_scan": ((0,), (1, 2, 3), ()),
    "reduce_max": ((0,), (1,), ()),
    "reduce_sum": ((0,), (1,), ()),
    "tensor_reduce": ((0,), (1,), ()),
    "reciprocal": ((0,), (1,), ()),
    "memset": ((0,), (), ()),
    "iota": ((0,), (), ()),
    "affine_select": ((0,), (1,), ()),
    "activation": ((0,), (1,), ("bias", "scale", "accum_out")),
    "copy": ((0,), (1,), ()),
    "square": ((0,), (1,), ()),
    "matmul": ((0,), (1, 2), ()),
    "transpose": ((0,), (1, 2), ()),
    "select": ((0,), (1, 2, 3), ()),
    "dma_start": ((), (), ()),
    "nop": ((), (), ()),
}


def _ap_region(ap):
    if ap is None or isinstance(ap, (int, float)):
        return None
    try:
        t = ap.tensor
        space = str(ap.space)
        name = t.name
        p0 = ap.base_partition()
        shape = ap.shape
        pat = [list(d) for d in ap.ap]
    except Exception:
        return None
    if "DRAM" in space.upper() or "Dram" in space:
        return None
    npart = pat[0][1] if pat else 1
    span = 0
    for (st, cnt) in pat[1:]:
        span += abs(int(st)) * (int(cnt) - 1)
    esz = 2 if "bfloat16" in str(ap.dtype) else 4
    foff = int(ap.offset) - int(p0) * int(pat[0][0]) if pat else int(ap.offset)
    b0 = foff * esz
    b1 = b0 + (span + 1) * esz
    return (name, space, int(p0), int(p0) + int(npart), b0, b1)


def _overlap(r1, r2):
    return (r1[0] == r2[0] and r1[2] < r2[3] and r2[2] < r1[3]
            and r1[4] < r2[5] and r2[4] < r1[5])


class KB:
    def __init__(self, nc):
        self.nc = nc
        self.plan = {e: [] for e in ENGS}
        self.cnt = {}
        self.semnames = []
        for s in ("q", "gq", "qe", "qb", "qf", "v", "s", "t", "g", "store"):
            self.sem(s)
        self.recs = {}
        self.sat = {e: {} for e in ENGS}
        self.dma_groups = {s: [] for s in ("q", "gq", "qe", "qb", "qf", "store")}

    def sem(self, name):
        if name not in self.cnt:
            self.cnt[name] = 0
            self.semnames.append(name)
        return name

    def now(self, name):
        return self.cnt[name]

    def dma_barrier(self, sem):
        if self.cnt[sem]:
            g = self.dma_groups[sem]
            if not g or g[-1] != self.cnt[sem]:
                g.append(self.cnt[sem])

    def _lift(self, sem, val):
        if sem not in self.dma_groups:
            return val
        for b in self.dma_groups[sem]:
            if b >= val:
                return b
        return self.cnt[sem]

    def _auto_waits(self, eng, outs, ins):
        need = {}

        def add(psem, pval):
            need[psem] = max(need.get(psem, 0), pval)

        names = {r[0] for r in outs} | {r[0] for r in ins}
        for nm in names:
            for (reg, kind, peng, psem, pval) in self.recs.get(nm, ()):
                if kind == 'W':
                    hit = False
                    for r in ins:
                        if _overlap(reg, r):
                            hit = True
                            break
                    if hit and not (peng == eng == "tensor"):
                        add(psem, pval)
                for r in outs:
                    if _overlap(reg, r):
                        if peng == eng == "tensor" and kind == 'W':
                            break
                        add(psem, pval)
                        break
        sat = self.sat[eng]
        out = []
        for s_, v_ in need.items():
            v_ = self._lift(s_, v_)
            if sat.get(s_, 0) < v_:
                out.append((s_, v_))
        return out

    def _record(self, reg, kind, eng, sem, val):
        lst = self.recs.setdefault(reg[0], [])
        lst.append((reg, kind, eng, sem, val))
        if len(lst) > 96:
            old, keep = lst[:48], lst[48:]
            coarse = {}
            for (r, k, pe, ps, pv) in old:
                key = (pe, ps, k)
                prev = coarse.get(key)
                if prev is None or pv > prev[4]:
                    big = (r[0], r[1], 0, 128, 0, 1 << 30)
                    coarse[key] = (big, k, pe, ps, pv)
            self.recs[reg[0]] = list(coarse.values()) + keep

    def op(self, eng, meth, args=(), kwargs=None, waits=(), inc=None, incn=1):
        kwargs = kwargs or {}
        if inc is None and meth not in ("dma_start",):
            inc = SEM_OF[eng]
        if inc is not None:
            self.cnt[inc] += incn
            tgt = self.cnt[inc]
        else:
            tgt = None
        oidx, iidx, ikw = _IO.get(meth, ((), (), ()))
        outs, ins = [], []
        for i, a in enumerate(args):
            r = _ap_region(a)
            if r is None:
                continue
            (outs if i in oidx else ins).append(r)
        for k in ikw:
            r = _ap_region(kwargs.get(k))
            if r is not None:
                (outs if k == "accum_out" else ins).append(r)
        if meth == "dma_start":
            r = _ap_region(kwargs.get("out"))
            if r is not None:
                outs.append(r)
            r = _ap_region(kwargs.get("in_"))
            if r is not None:
                ins.append(r)
        auto = self._auto_waits(eng, outs, ins) if (outs or ins) else []
        if meth == "dma_start" and inc in self.dma_groups:
            g = self.dma_groups[inc]
            prev_total = self.cnt[inc] - incn
            if g and g[-1] == prev_total and prev_total > 0:
                auto = list(auto) + [(inc, prev_total)]
        w = {}
        for (s_, v_) in tuple(waits) + tuple(auto):
            if not v_ or v_ <= 0:
                continue
            v_ = self._lift(s_, v_)
            if self.sat[eng].get(s_, 0) < v_:
                w[s_] = max(w.get(s_, 0), v_)
        for s_, v_ in w.items():
            self.sat[eng][s_] = v_
        self.plan[eng].append((meth, args, kwargs, tuple(w.items()), inc, tgt, incn))
        if inc is not None:
            for r in outs:
                self._record(r, 'W', eng, inc, tgt)
            for r in ins:
                self._record(r, 'R', eng, inc, tgt)
        return tgt

    def replay(self, eng_handle, eng_name, sems):
        for meth, args, kwargs, waits, inc, tgt, incn in self.plan[eng_name]:
            for s, v in waits:
                eng_handle.wait_ge(sems[s], v)
            inst = getattr(eng_handle, meth)(*args, **kwargs)
            if inc is not None:
                inst.then_inc(sems[inc], incn)


def build_core_program():
    nc = bass.Bass("TRN2", target_bir_lowering=False)
    kb = KB(nc)
    dram = {}

    def din(name, shape, dtype=F32):
        dram[name] = nc.dram_tensor(name, list(shape), dtype, kind="ExternalInput").ap()
        return dram[name]

    x0_d = din("x0", [L, D])
    conv_w_d = din("aw_conv_w", [D, 1, 4])
    conv_b_d = din("aw_conv_b", [D])
    xpw_d = din("aw_xpw", [96, D])
    dtw_d = din("aw_dtw", [D, 64])
    dtb_d = din("aw_dtb", [D])
    Alog_d = din("aw_Alog", [D, N])
    Dskip_d = din("aw_Dskip", [D])
    rw_d = din("fw_rw", [D, NEXP])
    rb_d = din("fw_rb", [NEXP])
    ew_d = din("fw_ew", [NEXP, D, D])
    eb_d = din("fw_eb", [NEXP, D])
    Wq_d = din("bw_Wq", [D, D])
    Wk_d = din("bw_Wk", [D, D])
    Wv_d = din("bw_Wv", [D, D])
    cwq_d = din("bw_cwq", [D, 4])
    cwk_d = din("bw_cwk", [D, 4])
    cwv_d = din("bw_cwv", [D, 4])
    Wb_d = din("bw_Wb", [D, NH])
    Wid_d = din("bw_Wid", [D, D])
    alpha_d = din("bw_alpha", [NH])
    firs_d = din("bw_firs", [NH, DV, 3])
    firl_d = din("bw_firl", [NH, DV, 63])
    gW1_d = din("bw_gW1", [D + 16, D])
    gb1_d = din("bw_gb1", [D])
    gW2_d = din("bw_gW2", [D, 4])
    gb2_d = din("bw_gb2", [4])
    ltemp_d = din("bw_ltemp", [NH, 4])
    floor_d = din("bw_floor", [NH, 4])
    cres_d = din("bw_cres", [NH])
    onw_d = din("bw_onw", [DV])
    Wo_d = din("bw_Wo", [D, D])
    y_d = nc.dram_tensor("y", [L, D], F32, kind="ExternalOutput").ap()
    fir_dram = nc.dram_tensor("fir_scratch", [DT, 128, 66 * 128], BF16, kind="Internal").ap()
    xptf_dram = nc.dram_tensor("xptf_scratch", [128, 4 * DT * 112], BF16, kind="Internal").ap()

    AL = nc.alloc_sbuf_tensor

    x0f = AL("x0f", [128, DT * T], BF16)
    x1f = AL("x1f", [128, DT * T], BF16)
    xtm = AL("xtm", [128, 4 * D], F32)
    u_f = AL("u_f", [128, DT * (T + 3)], BF16)
    lnt1 = AL("lnt1", [128, T], F32)
    xnf = AL("xnf", [128, T], F32)
    sqs = AL("sqs", [128, T], F32)
    rowbuf = AL("rowbuf", [1, 3 * T], F32)
    delta = AL("delta", [128, 2 * T], F32)
    dtBC = AL("dtBC", [64, T], BF16)
    bc48 = AL("bc48", [48, T], BF16)
    bmcb = AL("bmcb", [128, 2 * N * T], BF16)
    dAt = AL("dAt", [128, T], BF16)
    ubm = AL("ubm", [128, T], BF16)
    bbt = AL("bbt", [128, T], BF16)
    Ht = AL("Ht", [128, (N // 2) * T], BF16)
    ymt = AL("ymt", [128, (N // 2) * T], BF16)
    Hc = AL("Hc", [128, DT * N], F32)
    zt = AL("zt", [128, T], F32)
    at_ = AL("at_", [128, T], F32)
    guard = AL("guard", [128, 1], F32)

    k_f = AL("k_f", [128, DT * (T + 3)], BF16)
    q_f = AL("q_f", [128, DT * (T + 3)], BF16)
    v_f = AL("v_f", [128, DT * (T + 62)], BF16)

    xpw = AL("xpw", [96, D], BF16)
    xpT = None
    cw_c = AL("cw_c", [128, DT * 4], F32)
    dtw_sb = AL("dtw_sb", [128, DT * 64], BF16)
    dtwT = AL("dtwT", [64, DT * 128], BF16)
    dtb_c = AL("dtb_c", [128, DT], F32)
    Dskip_c = AL("Dskip_c", [128, DT], F32)
    Dsk_dg = AL("Dsk_dg", [128, DT * 128], BF16)
    Alog_c = AL("Alog_c", [128, DT * N], F32)
    A_c = AL("A_c", [128, DT * N], F32)
    nA_c = AL("nA_c", [128, DT * N], F32)
    invA_c = AL("invA_c", [128, DT * N], F32)
    cb96 = AL("cb96", [112, 1], F32)
    cbrow = AL("cbrow", [128, DT], F32)
    cbrow_b = AL("cbrow_b", [128, DT], BF16)
    enB = AL("enB", [48, N * 128], BF16)
    enC = AL("enC", [48, N * 128], BF16)
    diag01 = AL("diag01", [128, 128], BF16)
    identf = AL("identf", [128, 128], F32)
    onescf = AL("onescf", [128, 1], F32)
    onescb = AL("onescb", [128, 1], BF16)
    onesrow = AL("onesrow", [1, 128], F32)
    cone = AL("cone", [128, 1], F32)
    cepsln = AL("cepsln", [128, 1], F32)
    vec8a = AL("vec8a", [8, 128], F32)
    vec8b = AL("vec8b", [8, 128], F32)
    vec8c = AL("vec8c", [8, 128], F32)

    S_f = AL("S_f", [128, NH * 2 * 256], F32)
    qtails = AL("qtails", [128, DT * 3], BF16)
    ktails = AL("ktails", [128, DT * 3], BF16)
    vrawtails = Alog_c[:, 0:DT * 3 // 2].bitcast(BF16)
    cvscr = xnf[:, 0:3].bitcast(BF16)
    wb_sb = AL("wb_sb", [128, DT * NH], BF16)
    cwq_c = AL("cwq_c", [128, DT * 4], F32)
    cwk_c = AL("cwk_c", [128, DT * 4], F32)
    cwv_c = AL("cwv_c", [128, DT * 4], F32)
    acol = AL("acol", [128, NH], F32)
    scres = AL("scres", [128, NH], F32)
    onw_c = AL("onw_c", [128, 2], F32)
    gb1_c = AL("gb1_c", [128, DT], F32)
    gb2_c = AL("gb2_c", [4, 1], F32)
    gw2_sb = AL("gw2_sb", [128, DT * 4], BF16)
    invt_c = AL("invt_c", [4, NH], F32)
    fl_bc = AL("fl_bc", [128, 16], F32)
    fs_bc = AL("fs_bc", [128, NH], F32)
    gw1s_p1 = AL("gw1s_p1", [128, D], BF16)
    gw1s_p2 = AL("gw1s_p2", [4, D], BF16)
    fircl = bmcb[:, 0:1056].bitcast(F32)
    selst = AL("selst", [128, 12 * 68], BF16)
    mask_sl = AL("mask_sl", [128, 128], BF16)
    mask_ui = AL("mask_ui", [128, 128], BF16)
    E4 = AL("E4", [4, NH * 128], BF16)
    cepsl2 = AL("cepsl2", [128, 1], F32)
    cepso = AL("cepso", [128, 1], F32)
    srow16 = AL("srow16", [1, 16], F32)
    fsrow4 = AL("srow4b", [1, 4], F32)

    rw_sb = AL("rw_sb", [128, DT * NEXP], F32)
    rb_c = AL("rb_c", [8, 1], F32)
    eb_sb = AL("eb_sb", [8, DT * 128], BF16)
    E8 = AL("E8", [8, NEXP * 128], BF16)
    ew_slot = AL("ew_slot", [128, DT * D], BF16)
    xnb = AL("xnb", [128, DT * T], BF16)
    xeb = AL("xeb", [128, 2 * T], BF16)
    lgrow = AL("lgrow", [8, T], F32)
    lgtm = AL("lgtm", [128, 4 * NEXP], F32)
    mrow = AL("mrow", [128, NEXP], F32)
    mtmb = AL("mtmb", [128, 4 * NEXP], BF16)
    mrow4 = AL("mrow4", [128, 4 * NEXP], F32)
    srow4 = AL("srow4", [128, 4 * NEXP], F32)
    rc16 = AL("rc16", [128, 16], F32)
    m_fm = dtBC[0:8, 0:T]
    rcol = AL("rcol", [128, 4], F32)
    mbc_all = bmcb
    eslot2 = bmcb[:, 4096:4096 + DT * D]
    dAt2 = xeb[:, 0:T]
    ubm2 = xeb[:, T:2 * T]

    PSH = [nc.alloc_psum_tensor(f"bank{i}", [128, 512], F32) for i in range(8)]
    PS = [h[:] for h in PSH]

    bank_free = {i: ("t", 0) for i in range(8)}

    def mm(bank, out_ap, lhsT, rhs, start, stop, waits=(), **kw):
        w = list(waits)
        if start and bank is not None:
            w.append(bank_free[bank])
        return kb.op("tensor", "matmul", (out_ap, lhsT, rhs),
                     dict(start=start, stop=stop, **kw), waits=w, inc="t")

    def tp(bank, out_ap, in_ap, ident_ap, waits=()):
        w = list(waits)
        if bank is not None:
            w.append(bank_free[bank])
        return kb.op("tensor", "transpose", (out_ap, in_ap, ident_ap), waits=w, inc="t")

    def evac(eng, meth, args, kwargs=None, waits=(), bank=None):
        sem = SEM_OF[eng]
        tgt = kb.op(eng, meth, args, kwargs, waits=waits, inc=sem)
        if bank is not None:
            bank_free[bank] = (sem, tgt)
        return tgt

    def vguard(ap_col):
        return kb.op("vector", "tensor_copy", (guard[:ap_col.shape[0], 0:1], ap_col), inc="v")

    def sguard(ap_col):
        return kb.op("scalar", "copy", (guard[:ap_col.shape[0], 0:1], ap_col), inc="s")

    def load(dst, src, waits=(), sem="q"):
        return kb.op("sync", "dma_start", (), dict(out=dst, in_=src), waits=waits,
                     inc=sem, incn=16)

    def gload(dst, src, waits=(), sem="gq"):
        return kb.op("gpsimd", "dma_start", (), dict(out=dst, in_=src),
                     waits=waits, inc=sem, incn=16)

    xpT = Ht[:, 0:DT * 112]

    load(vec8a[0:8, :], dtb_d.rearrange("(dt p) -> dt p", p=128))
    load(vec8b[0:8, :], conv_b_d.rearrange("(dt p) -> dt p", p=128))
    load(vec8c[0:8, :], Dskip_d.rearrange("(dt p) -> dt p", p=128))
    load(Alog_c[:, :].rearrange("p (dt n) -> p dt n", n=N),
         Alog_d.rearrange("(dt p) n -> p dt n", p=128))
    load(cw_c[:, :].rearrange("p (dt k) -> p dt k", k=4),
         conv_w_d.rearrange("(dt p) o k -> p dt (o k)", p=128))
    load(rw_sb[:, :].rearrange("p (dt e) -> p dt e", e=NEXP),
         rw_d.rearrange("(dt p) e -> p dt e", p=128))
    load(rb_c[0:8, 0:1], rb_d.rearrange("(e o) -> e o", o=1))
    gload(xpw[:, :], xpw_d)
    gload(dtw_sb[:, :].rearrange("p (dt r) -> p dt r", r=64),
          dtw_d.rearrange("(dt p) r -> p dt r", p=128))
    gload(eb_sb[0:8, :].rearrange("e (ct q) -> e ct q", q=128),
          eb_d.rearrange("e (ct q) -> e ct q", q=128))
    ldq = kb.now("q")
    gldq = kb.now("gq")
    kb.dma_barrier("q")
    kb.dma_barrier("gq")

    kb.op("gpsimd", "memset", (diag01[:], 1.0), inc="g")
    kb.op("gpsimd", "affine_select",
          (diag01[:], diag01[:], [[-1, 128]], ALU.is_equal, 0.0),
          dict(base=0, channel_multiplier=1), waits=[("g", kb.now("g"))], inc="g")
    kb.op("gpsimd", "memset", (enB[:], 1.0), inc="g")
    kb.op("gpsimd", "memset", (enC[:], 1.0), inc="g")
    kb.op("gpsimd", "memset", (E8[:], 1.0), inc="g")
    g_c0 = kb.now("g")
    for n in range(N):
        kb.op("gpsimd", "affine_select",
              (enB[0:48, n * 128:(n + 1) * 128], enB[0:48, n * 128:(n + 1) * 128],
               [[0, 128]], ALU.is_equal, 0.0),
              dict(base=-n, channel_multiplier=1), waits=[("g", g_c0)], inc="g")
        kb.op("gpsimd", "affine_select",
              (enC[0:48, n * 128:(n + 1) * 128], enC[0:48, n * 128:(n + 1) * 128],
               [[0, 128]], ALU.is_equal, 0.0),
              dict(base=-(32 + n), channel_multiplier=1), waits=[("g", g_c0)], inc="g")
    for e in range(NEXP):
        kb.op("gpsimd", "affine_select",
              (E8[0:8, e * 128:(e + 1) * 128], E8[0:8, e * 128:(e + 1) * 128],
               [[0, 128]], ALU.is_equal, 0.0),
              dict(base=-e, channel_multiplier=1), waits=[("g", g_c0)], inc="g")
    g_en = kb.now("g")

    kb.op("vector", "memset", (onescf[:], 1.0), inc="v")
    kb.op("vector", "memset", (onescb[:], 1.0), inc="v")
    kb.op("vector", "memset", (onesrow[:], 1.0), inc="v")
    kb.op("vector", "memset", (cone[:], 1.0), inc="v")
    kb.op("vector", "memset", (cepsln[:], 1e-5), inc="v")
    kb.op("vector", "memset", (Hc[:], 0.0), inc="v")
    kb.op("vector", "memset", (u_f[:], 0.0), inc="v")
    kb.op("vector", "tensor_copy", (identf[:], diag01[:]), waits=[("g", g_c0)], inc="v")
    v_c0 = kb.now("v")
    gv_id = vguard(identf[:, 127:128])

    s_nA = kb.op("scalar", "activation", (nA_c[:], Alog_c[:], AF.Exp),
                 waits=[("q", ldq)], inc="s")
    kb.op("vector", "tensor_scalar", (A_c[:], nA_c[:], -1.0, None, ALU.mult),
          waits=[("s", s_nA)], inc="v")
    kb.op("vector", "reciprocal", (invA_c[:], A_c[:]), inc="v")
    v_A = kb.now("v")

    tp(3, PS[3][0:128, 0:8], vec8a[0:8, :], identf[0:8, 0:8],
       waits=[("q", ldq), ("v", gv_id)])
    tp(None, PS[3][0:128, 8:16], vec8b[0:8, :], identf[0:8, 0:8])
    tp(None, PS[3][0:128, 16:24], vec8c[0:8, :], identf[0:8, 0:8])
    t_vec = kb.now("t")
    evac("scalar", "copy", (dtb_c[:], PS[3][0:128, 0:8]), waits=[("t", t_vec)])
    evac("scalar", "copy", (cbrow[:], PS[3][0:128, 8:16]))
    evac("scalar", "copy", (Dskip_c[:], PS[3][0:128, 16:24]), bank=3)
    s_vec = kb.now("s")

    for dt in range(DT):
        kb.op("vector", "tensor_tensor",
              (Dsk_dg[:, dt * 128:(dt + 1) * 128],
               Dskip_c[:, dt:dt + 1].broadcast_to((128, 128)), diag01[:], ALU.mult),
              waits=[("s", s_vec)], inc="v")
    v_dsk = kb.now("v")

    for dt in range(DT):
        tp(3, PS[3][0:128, 0:48].bitcast(BF16), xpw[0:96, dt * 128:(dt + 1) * 128],
           diag01[0:96, 0:96], waits=[("gq", gldq), ("v", gv_id)])
        tnow = kb.now("t")
        evac("scalar", "copy", (xpT[:, dt * 112: dt * 112 + 64],
                                PS[3][0:128, 0:32].bitcast(BF16)), waits=[("t", tnow)])
        evac("scalar", "copy", (xpT[:, dt * 112 + 64: dt * 112 + 80],
                                PS[3][0:128, 32:40].bitcast(BF16)))
        evac("scalar", "copy", (xpT[:, dt * 112 + 96: dt * 112 + 112],
                                PS[3][0:128, 40:48].bitcast(BF16)), bank=3)
        kb.op("vector", "memset", (xpT[:, dt * 112 + 80: dt * 112 + 96], 0.0),
              waits=[("s", kb.now("s"))], inc="v")
    s_xpT = kb.now("s")
    v_xpT = kb.now("v")

    xpTf = k_f
    for tap in range(4):
        for dt in range(DT):
            kb.op("vector", "scalar_tensor_tensor",
                  (xpTf[:, (tap * DT + dt) * 112:(tap * DT + dt + 1) * 112],
                   xpT[:, dt * 112:(dt + 1) * 112],
                   cw_c[:, dt * 4 + tap: dt * 4 + tap + 1],
                   xpT[:, dt * 112:(dt + 1) * 112], ALU.mult, ALU.bypass),
                  waits=[("s", s_xpT), ("v", v_xpT), ("q", ldq)], inc="v")
    gv_fold = kb.now("v")
    kb.op("sync", "dma_start", (),
          dict(out=xptf_dram, in_=xpTf[:, 0: 4 * DT * 112]),
          waits=[("v", gv_fold)], inc="q", incn=16)
    kb.dma_barrier("q")

    kb.op("vector", "tensor_copy", (cbrow_b[:], cbrow[:]), waits=[("s", s_vec)], inc="v")
    gv_cb = vguard(cbrow_b[:, DT - 1:DT])
    for dt in range(DT):
        mm(2 if dt == 0 else None, PS[2][0:112, 0:1],
           xpT[:, dt * 112:(dt + 1) * 112], cbrow_b[:, dt:dt + 1],
           start=(dt == 0), stop=(dt == DT - 1), waits=[("v", gv_cb), ("s", s_xpT)])
    e_cb = evac("scalar", "copy", (cb96[:, 0:1], PS[2][0:112, 0:1]),
                waits=[("t", kb.now("t"))], bank=2)

    for dt in range(DT):
        tp(3, PS[3][0:64, 0:64].bitcast(BF16), dtw_sb[:, dt * 64:(dt + 1) * 64],
           diag01[:], waits=[("gq", gldq), ("v", gv_id)])
        evac("scalar", "copy", (dtwT[0:64, dt * 128:(dt + 1) * 128],
                                PS[3][0:64, 0:64].bitcast(BF16)),
             waits=[("t", kb.now("t"))], bank=3)

    load(cwq_c[:, :].rearrange("p (dt k) -> p dt k", k=4),
         cwq_d.rearrange("(dt p) k -> p dt k", p=128))
    load(cwk_c[:, :].rearrange("p (dt k) -> p dt k", k=4),
         cwk_d.rearrange("(dt p) k -> p dt k", p=128))
    load(cwv_c[:, :].rearrange("p (dt k) -> p dt k", k=4),
         cwv_d.rearrange("(dt p) k -> p dt k", p=128))
    gload(wb_sb[:, :].rearrange("p (dt h) -> p dt h", h=NH),
          Wb_d.rearrange("(dt p) h -> p dt h", p=128))
    gload(gw2_sb[:, :].rearrange("p (dt h) -> p dt h", h=4),
          gW2_d.rearrange("(dt p) h -> p dt h", p=128))
    load(gb2_c[0:4, 0:1], gb2_d.rearrange("(e o) -> e o", o=1))
    load(fircl[:, 0:504].rearrange("p (h dc k) -> p h dc k", h=NH, k=63),
         firl_d.rearrange("h (dc p) k -> p h dc k", p=128))
    load(fircl[:, 504:528].rearrange("p (h dc k) -> p h dc k", h=NH, k=3),
         firs_d.rearrange("h (dc p) k -> p h dc k", p=128))
    alpha_r = Ht[0:1, 896:904].bitcast(F32)
    cres_r = Ht[0:1, 904:912].bitcast(F32)
    lt_r = Ht[0:4, 912:920].bitcast(F32)
    fp_r = Ht[0:1, 920:952].bitcast(F32)
    onw_r = Ht[0:2, 960:1216].bitcast(F32)
    gb1_r = Ht[0:8, 1216:1472].bitcast(F32)
    load(alpha_r[0:1, :], alpha_d.rearrange("(o h) -> o h", o=1))
    load(cres_r[0:1, :], cres_d.rearrange("(o h) -> o h", o=1))
    load(lt_r[0:4, :], ltemp_d)
    load(fp_r[0:1, :], floor_d.rearrange("h b -> (h b)").rearrange("(o s) -> o s", o=1))
    load(onw_r[0:2, :], onw_d.rearrange("(c p) -> c p", p=128))
    load(gb1_r[0:8, :], gb1_d.rearrange("(dt p) -> dt p", p=128))
    kb.op("vector", "memset", (gw1s_p1[:, :], 0.0))
    kb.op("vector", "memset", (gw1s_p2[:, :], 0.0))
    for b in range(4):
        gload(gw1s_p1[b:b + 1, :], gW1_d[D + b * 4 + 0: D + b * 4 + 1, :])
        gload(gw1s_p1[32 + b:33 + b, :], gW1_d[D + b * 4 + 1: D + b * 4 + 2, :])
        gload(gw1s_p1[64 + b:65 + b, :], gW1_d[D + b * 4 + 2: D + b * 4 + 3, :])
        gload(gw1s_p2[b:b + 1, :], gW1_d[D + b * 4 + 3: D + b * 4 + 4, :])
    ldq2 = kb.now("q")
    gldq2 = kb.now("gq")
    kb.dma_barrier("q")
    kb.dma_barrier("gq")

    kb.op("vector", "memset", (cepsl2[:], 1e-6))
    kb.op("vector", "memset", (cepso[:], 1e-5))
    kb.op("vector", "memset", (S_f[:], 0.0))
    kb.op("vector", "memset", (qtails[:], 0.0))
    kb.op("vector", "memset", (ktails[:], 0.0))
    kb.op("vector", "memset", (vrawtails[:], 0.0))
    for ct in range(DT):
        kb.op("vector", "memset", (v_f[:, ct * (T + 62): ct * (T + 62) + 62], 0.0))
    kb.op("gpsimd", "memset", (mask_sl[:], 1.0))
    kb.op("gpsimd", "affine_select",
          (mask_sl[:], mask_sl[:], [[-1, 128]], ALU.is_gt, 0.0),
          dict(base=0, channel_multiplier=1))
    kb.op("gpsimd", "memset", (mask_ui[:], 1.0))
    kb.op("gpsimd", "affine_select",
          (mask_ui[:], mask_ui[:], [[1, 128]], ALU.is_gt, 0.0),
          dict(base=1, channel_multiplier=-1))
    kb.op("gpsimd", "memset", (E4[:], 1.0))
    for b in range(NH):
        kb.op("gpsimd", "affine_select",
              (E4[0:4, b * 128:(b + 1) * 128], E4[0:4, b * 128:(b + 1) * 128],
               [[0, 128]], ALU.is_equal, 0.0),
              dict(base=-b, channel_multiplier=1))
    for b in range(4):
        for qi, base_row in ((0, 0), (1, 32), (2, 64)):
            sl = selst[:, (b * 3 + qi) * 68:(b * 3 + qi + 1) * 68]
            kb.op("gpsimd", "memset", (sl, 1.0 / 256.0))
            kb.op("gpsimd", "affine_select", (sl, sl, [[1, 68]], ALU.is_equal, 0.0),
                  dict(base=-(base_row + b), channel_multiplier=0))
    kb.op("scalar", "activation", (cres_r[0:1, :], cres_r[0:1, :], AF.Sigmoid),
          waits=[("q", ldq2)])
    mm(1, PS[1][0:128, 0:4], onesrow[0:1, :], alpha_r[0:1, :], start=True, stop=True,
       waits=[("q", ldq2)])
    mm(5, PS[5][0:128, 0:4], onesrow[0:1, :], cres_r[0:1, :], start=True, stop=True,
       waits=[("q", ldq2)])
    evac("scalar", "copy", (acol[:, :], PS[1][0:128, 0:4]), bank=1)
    evac("scalar", "copy", (scres[:, :], PS[5][0:128, 0:4]), bank=5)
    kb.op("scalar", "activation", (srow16[0:1, :], fp_r[0:1, :], AF.Sigmoid),
          waits=[("q", ldq2)])
    kb.op("vector", "tensor_scalar", (srow16[0:1, :], srow16[0:1, :], 0.05, None, ALU.mult))
    kb.op("vector", "tensor_reduce",
          (fsrow4[0:1, 0:4].rearrange("o (h u) -> o h u", u=1),
           srow16[0:1, :].rearrange("o (h b) -> o h b", b=4)),
          dict(axis=AX.X, op=ALU.add))
    kb.op("vector", "tensor_scalar", (fsrow4[0:1, :], fsrow4[0:1, :], -1.0, 1.0,
                                      ALU.mult, ALU.add))
    mm(1, PS[1][0:128, 0:16], onesrow[0:1, :], srow16[0:1, :], start=True, stop=True)
    evac("scalar", "copy", (fl_bc[:, :], PS[1][0:128, 0:16]), bank=1)
    mm(1, PS[1][0:128, 0:4], onesrow[0:1, :], fsrow4[0:1, :], start=True, stop=True)
    evac("scalar", "copy", (fs_bc[:, :], PS[1][0:128, 0:4]), bank=1)
    tp(1, PS[1][0:4, 0:4], lt_r[0:4, 0:4], identf[0:4, 0:4], waits=[("q", ldq2)])
    evac("scalar", "copy", (invt_c[0:4, :], PS[1][0:4, 0:4]), bank=1)
    kb.op("scalar", "activation", (invt_c[0:4, :], invt_c[0:4, :], AF.Exp),
          dict(scale=-1.0))
    tp(1, PS[1][0:128, 0:2], onw_r[0:2, :], identf[0:2, 0:2])
    evac("scalar", "copy", (onw_c[:, :], PS[1][0:128, 0:2]), bank=1)
    tp(1, PS[1][0:128, 0:8], gb1_r[0:8, :], identf[0:8, 0:8])
    evac("scalar", "copy", (gb1_c[:, :], PS[1][0:128, 0:8]), bank=1)
    for ct in range(DT):
        for halfsel in range(2):
            lo, hi = (0, 33) if halfsel == 0 else (33, 66)
            for tau in range(lo, hi):
                col = fircl[:, ct * 63 + tau: ct * 63 + tau + 1] if tau < 63 else                     fircl[:, 504 + ct * 3 + (tau - 63): 504 + ct * 3 + (tau - 63) + 1]
                kb.op("gpsimd", "tensor_tensor",
                      (ew_slot[:, (tau - lo) * 128:(tau - lo + 1) * 128],
                       col.broadcast_to((128, 128)), diag01[:], ALU.mult),
                      waits=[("q", ldq2)])
            kb.op("sync", "dma_start", (),
                  dict(out=fir_dram[ct][:, lo * 128: hi * 128],
                       in_=ew_slot[:, 0:(hi - lo) * 128]),
                  waits=[("g", kb.now("g"))], inc="qf", incn=16)
            kb.dma_barrier("qf")

    def emit_ln(x_in, write_dt, prod=()):
        gs = kb.op("scalar", "copy", (guard[0:128, 0:1], x_in[:, DT * T - 1: DT * T]),
                   waits=prod, inc="s")
        for dt in range(DT):
            mm(2 if dt == 0 else None, PS[2][0:1, 0:T], onescb[:],
               x_in[:, dt * T:(dt + 1) * T],
               start=(dt == 0), stop=(dt == DT - 1), waits=[("s", gs)])
        t_mu = kb.now("t")
        for dt in range(DT):
            kb.op("scalar", "activation", (sqs[:], x_in[:, dt * T:(dt + 1) * T], AF.Square),
                  waits=[("s", gs), ("t", kb.now("t"))], inc="s")
            gq = sguard(sqs[:, T - 1:T])
            mm(6 if dt == 0 else None, PS[6][0:1, 0:T], onescf[:], sqs[:],
               start=(dt == 0), stop=(dt == DT - 1), waits=[("s", gq)])
        t_sq = kb.now("t")
        evac("scalar", "activation", (rowbuf[0:1, 0:T], PS[2][0:1, 0:T], AF.Copy),
             dict(scale=1.0 / D), waits=[("t", t_mu)], bank=2)
        s_musq = kb.op("scalar", "activation",
                       (rowbuf[0:1, T:2 * T], rowbuf[0:1, 0:T], AF.Square), inc="s")
        e_var = evac("vector", "scalar_tensor_tensor",
                     (rowbuf[0:1, 2 * T:3 * T], PS[6][0:1, 0:T], 1.0 / D,
                      rowbuf[0:1, T:2 * T], ALU.mult, ALU.subtract),
                     waits=[("t", t_sq), ("s", s_musq)], bank=6)
        s_sd = kb.op("scalar", "activation",
                     (rowbuf[0:1, T:2 * T], rowbuf[0:1, 2 * T:3 * T], AF.Sqrt),
                     dict(bias=cepsln[0:1, 0:1]), waits=[("v", e_var)], inc="s")
        s_sd = kb.op("scalar", "copy", (guard[0:1, 0:1], rowbuf[0:1, 2 * T - 1:2 * T]),
                     inc="s")
        kb.op("vector", "reciprocal", (rowbuf[0:1, 2 * T:3 * T], rowbuf[0:1, T:2 * T]),
              waits=[("s", s_sd)], inc="v")
        kb.op("vector", "scalar_tensor_tensor",
              (rowbuf[0:1, T:2 * T], rowbuf[0:1, 0:T], -1.0,
               rowbuf[0:1, 2 * T:3 * T], ALU.mult, ALU.mult), inc="v")
        gv_r = vguard(rowbuf[0:1, 3 * T - 1:3 * T])
        mm(2, PS[2][:, 0:T], onesrow[0:1, :], rowbuf[0:1, 2 * T:3 * T],
           start=True, stop=True, waits=[("v", gv_r)])
        mm(6, PS[6][:, 0:T], onesrow[0:1, :], rowbuf[0:1, T:2 * T],
           start=True, stop=True, waits=[("v", gv_r)])
        t_bc = kb.now("t")
        for dt in range(DT):
            write_dt(dt, t_bc)
        v_done = kb.now("v")
        bank_free[2] = ("v", v_done)
        bank_free[6] = ("v", v_done)
        return v_done

    def emit_mamba(x_in, x_out, prod=()):
        def wdt(dt, t_bc):
            kb.op("vector", "tensor_tensor",
                  (lnt1[:], x_in[:, dt * T:(dt + 1) * T], PS[2][:, 0:T], ALU.mult),
                  waits=[("t", t_bc)], inc="v")
            kb.op("vector", "tensor_tensor",
                  (u_f[:, dt * (T + 3) + 3: dt * (T + 3) + 3 + T], lnt1[:],
                   PS[6][:, 0:T], ALU.add), inc="v")
        emit_ln(x_in, wdt, prod=prod)
        gv_u = vguard(u_f[:, DT * (T + 3) - 1: DT * (T + 3)])

        for tap in range(4):
            for dt in range(DT):
                first = (tap == 0 and dt == 0)
                mm(0 if first else None, PS[0][0:112, 0:T],
                   xpTf[:, (tap * DT + dt) * 112:(tap * DT + dt + 1) * 112],
                   u_f[:, dt * (T + 3) + tap: dt * (T + 3) + tap + T],
                   start=first, stop=(tap == 3 and dt == DT - 1),
                   waits=[("v", max(gv_fold, gv_u))])
        t_proj = kb.now("t")
        evac("scalar", "activation", (dtBC[0:64, 0:T], PS[0][0:64, 0:T], AF.Identity),
             dict(bias=cb96[0:64, 0:1]), waits=[("t", t_proj), ("s", e_cb)])
        e_proj = evac("scalar", "activation", (bc48[0:48, 0:T], PS[0][64:112, 0:T],
                                               AF.Identity),
                      dict(bias=cb96[64:112, 0:1]), bank=0)
        gs_proj = sguard(bc48[0:48, T - 1:T])

        for n in range(N):
            mm(2, PS[2][:, 0:T], enB[:, n * 128:(n + 1) * 128], bc48[0:48, 0:T],
               start=True, stop=True, waits=[("s", gs_proj), ("g", g_en)])
            evac("scalar", "copy", (bmcb[:, n * T:(n + 1) * T], PS[2][:, 0:T]),
                 waits=[("t", kb.now("t"))], bank=2)
            mm(6, PS[6][:, 0:T], enC[:, n * 128:(n + 1) * 128], bc48[0:48, 0:T],
               start=True, stop=True, waits=[("s", gs_proj), ("g", g_en)])
            evac("scalar", "copy", (bmcb[:, (N + n) * T:(N + n + 1) * T], PS[6][:, 0:T]),
                 waits=[("t", kb.now("t"))], bank=6)
        s_bc = kb.now("s")

        for dt in range(DT):
            dsl = delta[:, (dt % 2) * T:(dt % 2 + 1) * T]
            mm(5, PS[5][:, 0:T], dtwT[0:64, dt * 128:(dt + 1) * 128], dtBC[0:64, 0:T],
               start=True, stop=True, waits=[("s", gs_proj)])
            t_dt = kb.now("t")
            e_z = evac("scalar", "activation", (zt[:], PS[5][:, 0:T], AF.Identity),
                       dict(bias=dtb_c[:, dt:dt + 1]),
                       waits=[("t", t_dt), ("s", s_vec)], bank=5)
            kb.op("scalar", "activation", (sqs[:], zt[:], AF.Abs), inc="s")
            kb.op("scalar", "activation", (at_[:], sqs[:], AF.Exp), dict(scale=-1.0), inc="s")
            s_l = kb.op("scalar", "activation", (sqs[:], at_[:], AF.Ln),
                        dict(bias=cone[0:128, 0:1]), inc="s")
            kb.op("vector", "scalar_tensor_tensor",
                  (dsl, zt[:], 0.0, sqs[:], ALU.max, ALU.add),
                  waits=[("s", s_l)], inc="v")
            for half in range(2):
                for nn in range(N // 2):
                    n = half * (N // 2) + nn
                    col = dt * N + n
                    dab = dAt[:, 0:T] if n % 2 == 0 else dAt2
                    s_dA = kb.op("scalar", "activation",
                                 (dab, dsl, AF.Exp),
                                 dict(scale=A_c[:, col:col + 1]), inc="s")
                    ub = ubm[:, 0:T] if n % 2 == 0 else ubm2
                    s_ub = kb.op("scalar", "activation",
                                 (ub, u_f[:, dt * (T + 3) + 3: dt * (T + 3) + 3 + T],
                                  AF.Identity), dict(scale=invA_c[:, col:col + 1]),
                                 inc="s")
                    kb.op("vector", "tensor_tensor",
                          (ub, ub, bmcb[:, n * T:(n + 1) * T], ALU.mult),
                          waits=[("s", max(s_ub, s_bc))], inc="v")
                    kb.op("vector", "scalar_tensor_tensor",
                          (bbt[:], dab, -1.0, ub, ALU.add, ALU.mult),
                          waits=[("s", s_dA)], inc="v")
                    kb.op("vector", "tensor_tensor_scan",
                          (Ht[:, nn * T:(nn + 1) * T], dab, bbt[:], Hc[:, col:col + 1],
                           ALU.mult, ALU.add), inc="v")
                    kb.op("vector", "tensor_tensor",
                          (ymt[:, nn * T:(nn + 1) * T], Ht[:, nn * T:(nn + 1) * T],
                           bmcb[:, (N + n) * T:(N + n + 1) * T], ALU.mult), inc="v")
                hlast = bass.AP(Ht, T - 1, [[(N // 2) * T, 128], [T, N // 2]])
                kb.op("vector", "tensor_copy",
                      (Hc[:, dt * N + half * (N // 2): dt * N + (half + 1) * (N // 2)],
                       hlast), inc="v")
                for nn in range(N // 2):
                    mm(4 if (half == 0 and nn == 0) else None, PS[4][:, 0:T], diag01[:],
                       ymt[:, nn * T:(nn + 1) * T],
                       start=(half == 0 and nn == 0), stop=False)
            mm(None, PS[4][:, 0:T], Dsk_dg[:, dt * 128:(dt + 1) * 128],
               u_f[:, dt * (T + 3) + 3: dt * (T + 3) + 3 + T],
               start=False, stop=True, waits=[("v", v_dsk)])
            evac("vector", "tensor_tensor",
                 (x_out[:, dt * T:(dt + 1) * T], x_in[:, dt * T:(dt + 1) * T],
                  PS[4][:, 0:T], ALU.add),
                 waits=[("t", kb.now("t"))], bank=4)
        t_last = kb.now("t")
        for dt in range(DT):
            kb.op("vector", "tensor_copy",
                  (u_f[:, dt * (T + 3): dt * (T + 3) + 3],
                   u_f[:, dt * (T + 3) + T: dt * (T + 3) + 3 + T]),
                  waits=[("t", t_last)] if dt == 0 else (), inc="v")
        return kb.now("s")

    def emit_ffn(x_in, x_out, ti, prod=()):
        def wdt(dt, t_bc):
            kb.op("vector", "tensor_tensor",
                  (lnt1[:], x_in[:, dt * T:(dt + 1) * T], PS[2][:, 0:T], ALU.mult),
                  waits=[("t", t_bc)], inc="v")
            kb.op("vector", "tensor_tensor", (xnf[:], lnt1[:], PS[6][:, 0:T], ALU.add),
                  waits=[("t", kb.now("t"))], inc="v")
            kb.op("vector", "tensor_copy",
                  (xnb[:, dt * T:(dt + 1) * T], xnf[:]), inc="v")
            gvx = vguard(xnf[:, T - 1:T])
            mm(1 if dt == 0 else None, PS[1][0:8, 0:T],
               rw_sb[:, dt * NEXP:(dt + 1) * NEXP], xnf[:],
               start=(dt == 0), stop=(dt == DT - 1), waits=[("v", gvx)])
        emit_ln(x_in, wdt, prod=prod)
        gv_xnb = vguard(xnb[:, DT * T - 1: DT * T])
        t_rt = kb.now("t")
        e_lg = evac("scalar", "activation", (lgrow[0:8, 0:T], PS[1][0:8, 0:T], AF.Identity),
                    dict(bias=rb_c[0:8, 0:1]), waits=[("t", t_rt), ("q", ldq)], bank=1)
        gs_lg = sguard(lgrow[0:8, T - 1:T])
        for pt in range(4):
            tp(1 if pt == 0 else None, PS[1][0:128, pt * 8:(pt + 1) * 8],
               lgrow[0:8, pt * 128:(pt + 1) * 128], identf[0:8, 0:8],
               waits=[("s", gs_lg)])
        t_lgt = kb.now("t")
        e_lgtm = evac("vector", "tensor_copy", (lgtm[:, :], PS[1][0:128, 0:32]),
                      waits=[("t", t_lgt)], bank=1)
        for pt in range(4):
            kb.op("vector", "reduce_max", (rc16[:, pt:pt + 1], lgtm[:, pt * 8:(pt + 1) * 8]),
                  dict(axis=AX.X), waits=[("v", e_lgtm)], inc="v")
            kb.op("vector", "tensor_scalar", (rc16[:, 4 + pt:5 + pt], rc16[:, pt:pt + 1],
                                              -1.0, None, ALU.mult), inc="v")
        v_nm = vguard(rc16[:, 7:8])
        for pt in range(4):
            kb.op("scalar", "activation", (mrow4[:, pt * 8:(pt + 1) * 8],
                                           lgtm[:, pt * 8:(pt + 1) * 8], AF.Exp),
                  dict(bias=rc16[:, 4 + pt:5 + pt]), waits=[("v", v_nm)], inc="s")
        for pt in range(4):
            kb.op("scalar", "activation", (srow4[:, pt * 8:(pt + 1) * 8],
                                           lgtm[:, pt * 8:(pt + 1) * 8], AF.Sign),
                  dict(bias=rc16[:, 4 + pt:5 + pt]), inc="s")
        s_es = sguard(srow4[:, 31:32])
        for pt in range(4):
            kb.op("vector", "reduce_sum", (rc16[:, 8 + pt:9 + pt],
                                           mrow4[:, pt * 8:(pt + 1) * 8]),
                  dict(axis=AX.X), waits=[("s", s_es)], inc="v")
            kb.op("vector", "reciprocal", (rc16[:, 12 + pt:13 + pt],
                                           rc16[:, 8 + pt:9 + pt]), inc="v")
            kb.op("vector", "tensor_scalar", (srow4[:, pt * 8:(pt + 1) * 8],
                                              srow4[:, pt * 8:(pt + 1) * 8], 1.0, None,
                                              ALU.add), inc="v")
            kb.op("vector", "tensor_scalar", (mtmb[:, pt * 8:(pt + 1) * 8],
                                              srow4[:, pt * 8:(pt + 1) * 8],
                                              rc16[:, 12 + pt:13 + pt], None, ALU.mult),
                  inc="v")
        gv_m = vguard(mtmb[:, 31:32])
        for pt in range(4):
            tp(1, PS[1][0:8, pt * 64:(pt + 1) * 64].bitcast(BF16),
               mtmb[:, pt * 8:(pt + 1) * 8], diag01[:], waits=[("v", gv_m)])
        t_mtp = kb.now("t")
        for pt in range(4):
            evac("scalar", "copy", (m_fm[0:8, pt * 128:(pt + 1) * 128],
                                    PS[1][0:8, pt * 64:(pt + 1) * 64].bitcast(BF16)),
                 waits=[("t", t_mtp)], bank=1)
        gs_mfm = sguard(m_fm[0:8, T - 1:T])

        for e in range(NEXP):
            mm(1, PS[1][:, 0:T], E8[:, e * 128:(e + 1) * 128], m_fm[0:8, 0:T],
               start=True, stop=True, waits=[("s", gs_mfm), ("g", g_en)])
            evac("scalar", "copy", (mbc_all[:, e * T:(e + 1) * T], PS[1][:, 0:T]),
                 waits=[("t", kb.now("t"))], bank=1)
        gs_mbc = sguard(mbc_all[:, NEXP * T - 1:NEXP * T])
        slot_consumed = [0, 0]

        def eload(e):
            slot = ew_slot if e % 2 == 0 else eslot2
            for i in range(DT):
                for ph in range(2):
                    gload(slot[ph * 64:(ph + 1) * 64, i * DT * 128:(i + 1) * DT * 128],
                          ew_d[e][i * 128 + ph * 64: i * 128 + (ph + 1) * 64, :],
                          waits=([("g", g_en)] if i == 0 and ph == 0 else ()),
                          sem="qe")
            kb.dma_barrier("qe")
            return kb.now("qe"), slot

        elq = {0: eload(0), 1: eload(1)}
        for e in range(NEXP):
            lqe, slot = elq.pop(e)
            for dt in range(DT):
                sl = dt % 2
                wv = [("s", gs_mbc), ("v", gv_xnb), ("t", slot_consumed[sl])]
                kb.op("vector", "tensor_tensor",
                      (xeb[:, sl * T:(sl + 1) * T], xnb[:, dt * T:(dt + 1) * T],
                       mbc_all[:, e * T:(e + 1) * T], ALU.mult), waits=wv, inc="v")
                gv_xe = vguard(xeb[:, (sl + 1) * T - 1:(sl + 1) * T])
                for ct in range(DT):
                    mm(ct if (e == 0 and dt == 0) else None, PS[ct][:, 0:T],
                       slot[:, (dt * DT + ct) * 128:(dt * DT + ct + 1) * 128],
                       xeb[:, sl * T:(sl + 1) * T],
                       start=(e == 0 and dt == 0), stop=False,
                       waits=[("v", gv_xe), ("qe", lqe)])
                slot_consumed[sl] = kb.now("t")
            if e + 2 < NEXP:
                elq[e + 2] = eload(e + 2)
        for ct in range(DT):
            mm(None, PS[ct][:, 0:T], eb_sb[0:8, ct * 128:(ct + 1) * 128], m_fm[0:8, 0:T],
               start=False, stop=True, waits=[("gq", gldq)])
            evac("vector", "tensor_tensor",
                 (x_out[:, ct * T:(ct + 1) * T], x_in[:, ct * T:(ct + 1) * T],
                  PS[ct][:, 0:T], ALU.add),
                 waits=[("t", kb.now("t"))], bank=ct)
        return kb.now("s")


    hsb = xnb
    wslot = ew_slot
    fir_slot = bmcb[:, 0:8448]
    pb_bc = bmcb[:, 8448:10496]
    q_f = bmcb[:, 10496:14616]
    hsW = bmcb[:, 0:4096]
    gelu_sb = bmcb[:, 4096:8192]
    do_f = x1f
    ls_f = ymt[:, 0:(N // 2) * T]
    ll_f = Ht[:, 0:(N // 2) * T]
    st68 = lnt1
    Xb = zt[:, 0:256].bitcast(BF16)
    uadj_sb = zt[:, 256:384].bitcast(BF16)
    kTM = zt[:, 384:512].bitcast(BF16)
    A_sb = at_[:, 0:64].bitcast(BF16)
    AT_sb = at_[:, 64:128].bitcast(BF16)
    Mp = [at_[:, 128:192].bitcast(BF16), at_[:, 192:256].bitcast(BF16)]
    MpT = [at_[:, 256:320].bitcast(BF16), at_[:, 320:384].bitcast(BF16)]
    wTn = at_[:, 384:512].bitcast(BF16)
    Sb_s = dAt[:, 0:T]
    attnT = ubm[:, 0:128]
    bT_c = rc16
    stat_mu = xnf
    lgt4 = lgtm

    def emit_attnb(x_in, x_out, ti):
        def wdt(dt, t_bc):
            kb.op("vector", "tensor_tensor",
                  (xnf[:], x_in[:, dt * T:(dt + 1) * T], PS[2][:, 0:T], ALU.mult),
                  waits=[("t", t_bc)], inc="v")
            kb.op("vector", "tensor_tensor",
                  (hsb[:, dt * T:(dt + 1) * T], xnf[:], PS[6][:, 0:T], ALU.add), inc="v")
        emit_ln(x_in, wdt, prod=[("s", kb.now("s"))])

        for ct in range(DT):
            kb.op("vector", "tensor_copy",
                  (q_f[:, ct * (T + 3): ct * (T + 3) + 3],
                   qtails[:, ct * 3:(ct + 1) * 3]), inc="v")
            kb.op("vector", "tensor_copy",
                  (k_f[:, ct * (T + 3): ct * (T + 3) + 3],
                   ktails[:, ct * 3:(ct + 1) * 3]), inc="v")

        def wload(W_d, slot):
            for i in range(DT):
                for ph in range(2):
                    gload(slot[ph * 64:(ph + 1) * 64, i * D:(i + 1) * D],
                          W_d[i * 128 + ph * 64: i * 128 + (ph + 1) * 64, :],
                          sem="qb")
            kb.dma_barrier("qb")
            return kb.now("qb")

        def proj_conv(lw, slot, cw_col, dst, span, off, tails, do_silu=True):
            for ct in range(DT):
                for dt in range(DT):
                    mm(0 if dt == 0 else None, PS[0][:, 0:T],
                       slot[:, dt * D + ct * 128: dt * D + (ct + 1) * 128],
                       hsb[:, dt * T:(dt + 1) * T],
                       start=(dt == 0), stop=(dt == DT - 1), waits=[("qb", lw)])
                base = ct * span + off
                evac("scalar", "copy", (dst[:, base: base + T], PS[0][:, 0:T]),
                     waits=[("t", kb.now("t"))], bank=0)
                if off == 62:
                    kb.op("vector", "tensor_copy",
                          (cvscr[:, 0:3], tails[:, ct * 3:(ct + 1) * 3]), inc="v")
                    kb.op("vector", "tensor_copy",
                          (cvscr[:, 3:6], dst[:, base: base + 3]), inc="v")
                kb.op("vector", "tensor_copy",
                      (tails[:, ct * 3:(ct + 1) * 3],
                       dst[:, base + T - 3: base + T]), inc="v")
                for i in range(4):
                    if i == 0:
                        kb.op("vector", "scalar_tensor_tensor",
                              (sqs[:], dst[:, base - 3 + i: base - 3 + i + T],
                               cw_col[:, ct * 4 + i: ct * 4 + i + 1],
                               dst[:, base - 3 + i: base - 3 + i + T],
                               ALU.mult, ALU.bypass), inc="v")
                    else:
                        kb.op("vector", "scalar_tensor_tensor",
                              (sqs[:], dst[:, base - 3 + i: base - 3 + i + T],
                               cw_col[:, ct * 4 + i: ct * 4 + i + 1],
                               sqs[:], ALU.mult, ALU.add), inc="v")
                if off == 62:
                    for i in range(4):
                        kb.op("vector", "scalar_tensor_tensor",
                              (sqs[:, 0:3], cvscr[:, i:i + 3],
                               cw_col[:, ct * 4 + i: ct * 4 + i + 1],
                               cvscr[:, i:i + 3] if i == 0 else sqs[:, 0:3],
                               ALU.mult, ALU.bypass if i == 0 else ALU.add),
                              inc="v")
                if do_silu:
                    kb.op("scalar", "activation", (dst[:, base: base + T], sqs[:],
                                                   AF.Silu), inc="s")
                else:
                    kb.op("vector", "tensor_copy", (dst[:, base: base + T], sqs[:]),
                          inc="v")

        pslot2 = bmcb[:, 0:DT * D]
        lq_w = wload(Wq_d, wslot)
        lk_w = wload(Wk_d, pslot2)
        proj_conv(lq_w, wslot, cwq_c, q_f, T + 3, 3, qtails)
        lv_w = wload(Wv_d, wslot)
        proj_conv(lk_w, pslot2, cwk_c, k_f, T + 3, 3, ktails)
        proj_conv(lv_w, wslot, cwv_c, v_f, T + 62, 62, vrawtails)

        for dt in range(DT):
            mm(1 if dt == 0 else None, PS[1][0:4, 0:T],
               wb_sb[:, dt * 4:(dt + 1) * 4], hsb[:, dt * T:(dt + 1) * T],
               start=(dt == 0), stop=(dt == DT - 1), waits=[("gq", gldq2)])
        evac("scalar", "activation", (lgrow[0:4, 0:T], PS[1][0:4, 0:T], AF.Sigmoid),
             waits=[("t", kb.now("t"))], bank=1)
        for c in range(4):
            tp(1, PS[1][0:128, 0:4], lgrow[0:4, c * 128:(c + 1) * 128],
               identf[0:4, 0:4])
            evac("scalar", "copy", (bT_c[:, c * 4:(c + 1) * 4], PS[1][0:128, 0:4]),
                 waits=[("t", kb.now("t"))], bank=1)
        kb.op("vector", "tensor_scalar", (mrow4[:, 0:16], bT_c[:, 0:16], -1.0, None,
                                          ALU.mult), inc="v")
        negbT = mrow4

        for th, dst in ((0, q_f), (1, k_f)):
            for h in range(NH):
                for half in range(2):
                    ct = 2 * h + half
                    kb.op("scalar", "activation",
                          (sqs[:], dst[:, ct * (T + 3) + 3: ct * (T + 3) + 3 + T],
                           AF.Square), inc="s")
                    mm(2 if half == 0 else None, PS[2][0:1, 0:T], onescf[:], sqs[:],
                       start=(half == 0), stop=(half == 1))
                e_r = evac("scalar", "activation",
                           (rowbuf[0:1, 0:T], PS[2][0:1, 0:T], AF.Sqrt),
                           dict(bias=cepsl2[0:1, 0:1]),
                           waits=[("t", kb.now("t"))], bank=2)
                kb.op("vector", "reciprocal", (rowbuf[0:1, T:2 * T], rowbuf[0:1, 0:T]),
                      inc="v")
                mm(2, PS[2][:, 0:T], onesrow[0:1, :], rowbuf[0:1, T:2 * T],
                   start=True, stop=True)
                e_bc = evac("scalar", "copy", (sqs[:], PS[2][:, 0:T]),
                            waits=[("t", kb.now("t"))], bank=2)
                for half in range(2):
                    ct = 2 * h + half
                    sl = dst[:, ct * (T + 3) + 3: ct * (T + 3) + 3 + T]
                    kb.op("vector", "tensor_tensor", (sl, sl, sqs[:], ALU.mult),
                          inc="v")

        def fload(ct, half):
            base = half * 33 * 128
            for i in range(4):
                for ph in range(2):
                    load(fir_slot[ph * 64:(ph + 1) * 64,
                                  base + i * 1056: base + (i + 1) * 1056],
                         fir_dram[ct][ph * 64:(ph + 1) * 64,
                                      base + i * 1056: base + (i + 1) * 1056],
                         sem="qf")
            kb.dma_barrier("qf")
            return kb.now("qf")

        lfh = {(0, 0): fload(0, 0), (0, 1): fload(0, 1)}
        for ct in range(DT):
            base = ct * (T + 62)
            for tau in range(33):
                mm(7 if tau == 0 else None, PS[7][:, 0:T],
                   fir_slot[:, tau * 128:(tau + 1) * 128],
                   v_f[:, base + tau: base + tau + T],
                   start=(tau == 0), stop=False, waits=[("qf", lfh[(ct, 0)])])
            if ct + 1 < DT:
                lfh[(ct + 1, 0)] = fload(ct + 1, 0)
            for tau in range(33, 63):
                mm(None, PS[7][:, 0:T],
                   fir_slot[:, tau * 128:(tau + 1) * 128],
                   v_f[:, base + tau: base + tau + T],
                   start=False, stop=(tau == 62), waits=[("qf", lfh[(ct, 1)])])
            evac("scalar", "copy", (ll_f[:, ct * T:(ct + 1) * T], PS[7][:, 0:T]),
                 waits=[("t", kb.now("t"))], bank=7)
            for i in range(3):
                mm(7 if i == 0 else None, PS[7][:, 0:T],
                   fir_slot[:, (63 + i) * 128:(64 + i) * 128],
                   v_f[:, base + 60 + i: base + 60 + i + T],
                   start=(i == 0), stop=(i == 2), waits=[("qf", lfh[(ct, 1)])])
            evac("scalar", "copy", (ls_f[:, ct * T:(ct + 1) * T], PS[7][:, 0:T]),
                 waits=[("t", kb.now("t"))], bank=7)
            if ct + 1 < DT:
                lfh[(ct + 1, 1)] = fload(ct + 1, 1)

        for i in range(DT):
            for ph in range(2):
                gload(wslot[ph * 64:(ph + 1) * 64, i * D:(i + 1) * D],
                  gW1_d[i * 128 + ph * 64: i * 128 + (ph + 1) * 64, :],
                  sem="qb")
        kb.dma_barrier("qb")
        lw1 = kb.now("qb")
        for ct in range(DT):
            for dt in range(DT):
                mm(0 if dt == 0 else None, PS[0][:, 0:T],
                   wslot[:, dt * D + ct * 128: dt * D + (ct + 1) * 128],
                   hsb[:, dt * T:(dt + 1) * T],
                   start=(dt == 0), stop=(dt == DT - 1), waits=[("qb", lw1)])
            evac("scalar", "copy", (hsW[:, ct * T:(ct + 1) * T], PS[0][:, 0:T]),
                 waits=[("t", kb.now("t"))], bank=0)
        for i in range(DT):
            for ph in range(2):
                gload(wslot[ph * 64:(ph + 1) * 64, i * D:(i + 1) * D],
                  Wid_d[i * 128 + ph * 64: i * 128 + (ph + 1) * 64, :],
                  sem="qb")
        kb.dma_barrier("qb")
        lid = kb.now("qb")
        wslot2 = wslot
        dscr = delta[:, :].bitcast(BF16)
        obuf1 = bbt
        obuf2 = dscr[:, 2 * T:3 * T]
        stb = dscr[:, 3 * T:4 * T]
        l2b = dscr[0:4, T:2 * T]
        for h in range(NH):
            kb.op("vector", "tensor_copy", (Sb_s[:, 0:T], S_f[:, h * 512: h * 512 + 512]),
                  inc="v")
            for c in range(4):
                c0 = c * 128
                kcol = lambda half: k_f[:, (2 * h + half) * (T + 3) + 3 + c0:
                                        (2 * h + half) * (T + 3) + 3 + c0 + 128]
                qcol = lambda half: q_f[:, (2 * h + half) * (T + 3) + 3 + c0:
                                        (2 * h + half) * (T + 3) + 3 + c0 + 128]
                vcol = lambda half: v_f[:, (2 * h + half) * (T + 62) + 62 + c0:
                                        (2 * h + half) * (T + 62) + 62 + c0 + 128]
                bcol = bT_c[:, c * 4 + h: c * 4 + h + 1]
                nbcol = negbT[:, c * 4 + h: c * 4 + h + 1]
                for half in range(2):
                    mm(1 if half == 0 else None, PS[1][:, 0:128], kcol(half), kcol(half),
                       start=(half == 0), stop=(half == 1))
                evac("vector", "scalar_tensor_tensor",
                     (A_sb[:, :], PS[1][:, 0:128], nbcol, mask_sl[:], ALU.mult,
                      ALU.mult), waits=[("t", kb.now("t"))], bank=1)
                for half in range(2):
                    mm(1 if half == 0 else None, PS[1][:, 0:128], kcol(half), qcol(half),
                       start=(half == 0), stop=(half == 1))
                evac("vector", "tensor_tensor", (attnT[:, :], PS[1][:, 0:128],
                                                 mask_ui[:], ALU.mult),
                     waits=[("t", kb.now("t"))], bank=1)
                for half in range(2):
                    tp(1, PS[1][0:128, half * 64:(half + 1) * 64].bitcast(BF16),
                       kcol(half), diag01[:])
                t_ktp = kb.now("t")
                evac("scalar", "copy", (kTM[:, 0:256], PS[1][0:128, 0:128].bitcast(BF16)),
                     waits=[("t", t_ktp)], bank=1)
                kb.op("vector", "scalar_tensor_tensor",
                      (Xb[:, 256:512], kTM[:, 0:256], bcol, kTM[:, 0:256],
                       ALU.mult, ALU.bypass), inc="v")
                for half in range(2):
                    tp(1, PS[1][0:128, half * 64:(half + 1) * 64].bitcast(BF16),
                       vcol(half), diag01[:])
                t_vtp = kb.now("t")
                evac("vector", "scalar_tensor_tensor",
                     (Xb[:, 0:256], PS[1][0:128, 0:128].bitcast(BF16), bcol,
                      kTM[:, 0:256], ALU.mult, ALU.bypass),
                     waits=[("t", t_vtp)], bank=1)
                tp(1, PS[1][0:128, 0:64].bitcast(BF16), A_sb[:, :], diag01[:])
                evac("scalar", "copy", (AT_sb[:, :], PS[1][0:128, 0:64].bitcast(BF16)),
                     waits=[("t", kb.now("t"))], bank=1)
                if STAGE == "dbgb" and ti == 0 and h == 0 and c == 0:
                    kb.op("gpsimd", "dma_start", (),
                          dict(out=y_d[776:904, 0:512], in_=Xb[:, 0:512]),
                          waits=[("v", kb.now("v")), ("s", kb.now("s"))],
                          inc="store", incn=16)
                    kb.op("gpsimd", "dma_start", (),
                          dict(out=y_d[904:1032, 0:128], in_=A_sb[:, :]),
                          inc="store", incn=16)
                    kb.op("gpsimd", "dma_start", (),
                          dict(out=y_d[1032:1160, 0:128], in_=attnT[:, :]),
                          inc="store", incn=16)
                    kb.dma_barrier("store")
                    kb.op("vector", "nop", (), waits=[("store", kb.now("store"))], inc="v")
                    kb.op("scalar", "nop", (), waits=[("store", kb.now("store"))], inc="s")
                cur, nxt = 0, 1
                kb.op("vector", "tensor_copy", (Mp[0][:, :], A_sb[:, :]), inc="v")
                kb.op("vector", "tensor_copy", (MpT[0][:, :], AT_sb[:, :]), inc="v")
                for j in range(7):
                    mm(4, PS[4][:, 0:T], MpT[cur][:, :], Xb[:, 0:512],
                       start=True, stop=True)
                    evac("vector", "tensor_tensor",
                         (Xb[:, 0:512], Xb[:, 0:512], PS[4][:, 0:T], ALU.add),
                         waits=[("t", kb.now("t"))], bank=4)
                    if j < 6:
                        mm(1, PS[1][:, 0:128], MpT[cur][:, :],
                           Mp[cur][:, :], start=True, stop=True)
                        mm(5, PS[5][:, 0:128], Mp[cur][:, :],
                           MpT[cur][:, :], start=True, stop=True)
                        evac("scalar", "copy", (Mp[nxt][:, :], PS[1][:, 0:128]),
                             waits=[("t", kb.now("t"))], bank=1)
                        evac("scalar", "copy", (MpT[nxt][:, :], PS[5][:, 0:128]),
                             bank=5)
                        cur, nxt = nxt, cur
                for half in range(2):
                    tp(1, PS[1][0:128, half * 64:(half + 1) * 64].bitcast(BF16),
                       Xb[:, 256 + half * 128: 256 + (half + 1) * 128], diag01[:])
                evac("vector", "tensor_scalar",
                     (wTn[:, 0:256], PS[1][0:128, 0:128].bitcast(BF16), -1.0, None,
                      ALU.mult), waits=[("t", kb.now("t"))], bank=1)
                for half in range(2):
                    mm(5 if half == 0 else None, PS[5][:, 0:256],
                       wTn[:, half * 128:(half + 1) * 128],
                       Sb_s[:, half * 256:(half + 1) * 256],
                       start=(half == 0), stop=(half == 1))
                evac("vector", "tensor_tensor",
                     (uadj_sb[:, 0:256], Xb[:, 0:256], PS[5][:, 0:256], ALU.add),
                     waits=[("t", kb.now("t"))], bank=5)
                if STAGE == "dbgb" and ti == 0 and h == 0 and c == 0:
                    kb.op("gpsimd", "dma_start", (),
                          dict(out=y_d[1160:1288, 0:512], in_=Xb[:, 0:512]),
                          waits=[("v", kb.now("v")), ("s", kb.now("s"))],
                          inc="store", incn=16)
                    kb.op("gpsimd", "dma_start", (),
                          dict(out=y_d[1288:1416, 0:256], in_=uadj_sb[:, 0:256]),
                          inc="store", incn=16)
                    kb.dma_barrier("store")
                    kb.op("vector", "nop", (), waits=[("store", kb.now("store"))], inc="v")
                    kb.op("scalar", "nop", (), waits=[("store", kb.now("store"))], inc="s")
                for half in range(2):
                    mm(5, PS[5][:, 0:256], kTM[:, half * 128:(half + 1) * 128],
                       uadj_sb[:, 0:256], start=True, stop=True)
                    kb.op("vector", "tensor_tensor",
                          (S_f[:, h * 512 + half * 256: h * 512 + (half + 1) * 256],
                           S_f[:, h * 512 + half * 256: h * 512 + (half + 1) * 256],
                           PS[5][:, 0:256], ALU.add),
                          waits=[("t", kb.now("t"))], inc="v")
                    bank_free[5] = ("v", kb.now("v"))
                for half in range(2):
                    mm(4 if half == 0 else None, PS[4][:, 0:256], qcol(half),
                       Sb_s[:, half * 256:(half + 1) * 256],
                       start=(half == 0), stop=False)
                mm(None, PS[4][:, 0:256], attnT[:, :], uadj_sb[:, 0:256],
                   start=False, stop=True)
                evac("scalar", "copy", (kTM[:, 0:256], PS[4][:, 0:256]),
                     waits=[("t", kb.now("t"))], bank=4)
                for half in range(2):
                    tp(1, PS[1][0:128, half * 64:(half + 1) * 64].bitcast(BF16),
                       kTM[:, half * 128:(half + 1) * 128], diag01[:])
                t_otp = kb.now("t")
                for half in range(2):
                    evac("scalar", "copy",
                         (do_f[:, (2 * h + half) * T + c0: (2 * h + half) * T + c0 + 128],
                          PS[1][0:128, half * 64:(half + 1) * 64].bitcast(BF16)),
                         waits=[("t", t_otp)], bank=(1 if half == 1 else None))
                kb.op("vector", "tensor_copy", (Sb_s[:, 0:T],
                                                S_f[:, h * 512: h * 512 + 512]), inc="v")


            first = True
            for bi, brt in ((0, ls_f), (1, ll_f), (2, do_f), (3, None)):
                for half in range(2):
                    ct = 2 * h + half
                    if brt is None:
                        xs = v_f[:, ct * (T + 62) + 62: ct * (T + 62) + 62 + T]
                    else:
                        xs = brt[:, ct * T:(ct + 1) * T]
                    mm(6 if first else None, PS[6][0:68, 0:T],
                       selst[:, (bi * 3 + 0) * 68:(bi * 3 + 1) * 68], xs,
                       start=first, stop=False)
                    first = False
                    sqb = dscr[:, 0:T]
                    kb.op("scalar", "activation", (sqb, xs, AF.Square), inc="s")
                    mm(None, PS[6][0:68, 0:T],
                       selst[:, (bi * 3 + 1) * 68:(bi * 3 + 2) * 68], sqb,
                       start=False, stop=False)
                    kb.op("scalar", "activation", (sqb, xs, AF.Abs), inc="s")
                    last = (bi == 3 and half == 1)
                    mm(None, PS[6][0:68, 0:T],
                       selst[:, (bi * 3 + 2) * 68:(bi * 3 + 3) * 68], sqb,
                       start=False, stop=last)
            evac("scalar", "copy", (st68[0:68, 0:T], PS[6][0:68, 0:T]),
                 waits=[("t", kb.now("t"))], bank=6)
            kb.op("scalar", "activation", (stat_mu[32:36, 0:T], st68[32:36, 0:T],
                                           AF.Sqrt), dict(scale=256.0), inc="s")
            kb.op("vector", "tensor_copy", (l2b, stat_mu[32:36, 0:T]), inc="v")
            kb.op("scalar", "activation", (stat_mu[32:36, 0:T], st68[0:4, 0:T],
                                           AF.Square), inc="s")
            kb.op("vector", "tensor_sub", (st68[32:36, 0:T], st68[32:36, 0:T],
                                           stat_mu[32:36, 0:T]), inc="v")
            kb.op("vector", "tensor_copy", (stb[:, 0:T], st68[:, 0:T]), inc="v")
            for ct in range(DT):
                mm(0, PS[0][:, 0:T], diag01[:], hsW[:, ct * T:(ct + 1) * T],
                   start=True, stop=False)
                mm(None, PS[0][:, 0:T], gw1s_p1[:, ct * 128:(ct + 1) * 128],
                   stb[:, 0:T], start=False, stop=False)
                mm(None, PS[0][:, 0:T], gw1s_p2[:, ct * 128:(ct + 1) * 128],
                   l2b, start=False, stop=True)
                evac("scalar", "activation", (gelu_sb[:, ct * T:(ct + 1) * T],
                                              PS[0][:, 0:T], AF.Gelu),
                     dict(bias=gb1_c[:, ct:ct + 1]),
                     waits=[("t", kb.now("t"))], bank=0)
            for ct in range(DT):
                mm(3 if ct == 0 else None, PS[3][0:4, 0:T],
                   gw2_sb[:, ct * 4:(ct + 1) * 4], gelu_sb[:, ct * T:(ct + 1) * T],
                   start=(ct == 0), stop=(ct == DT - 1), waits=[("gq", gldq2)])
            evac("scalar", "activation", (lgrow[0:4, 0:T], PS[3][0:4, 0:T],
                                          AF.Identity), dict(bias=gb2_c[0:4, 0:1]),
                 waits=[("t", kb.now("t"))], bank=3)
            kb.op("vector", "tensor_scalar", (lgrow[0:4, 0:T], lgrow[0:4, 0:T],
                                              invt_c[0:4, h:h + 1], None, ALU.mult),
                  inc="v")
            for c in range(4):
                tp(3, PS[3][0:128, c * 4:(c + 1) * 4],
                   lgrow[0:4, c * 128:(c + 1) * 128], identf[0:4, 0:4])
            evac("vector", "tensor_copy", (lgt4[:, 0:16], PS[3][0:128, 0:16]),
                 waits=[("t", kb.now("t"))], bank=3)
            for c in range(4):
                sl = lgt4[:, c * 4:(c + 1) * 4]
                kb.op("vector", "reduce_max", (rcol[:, 0:1], sl), dict(axis=AX.X),
                      inc="v")
                kb.op("vector", "tensor_scalar", (rcol[:, 1:2], rcol[:, 0:1], -1.0,
                                                  None, ALU.mult), inc="v")
                kb.op("scalar", "activation", (sl, sl, AF.Exp), dict(bias=rcol[:, 1:2]),
                      inc="s")
                kb.op("vector", "reduce_sum", (rcol[:, 2:3], sl), dict(axis=AX.X),
                      inc="v")
                kb.op("vector", "reciprocal", (rcol[:, 3:4], rcol[:, 2:3]), inc="v")
                kb.op("vector", "tensor_scalar", (sl, sl, rcol[:, 3:4], None,
                                                  ALU.mult), inc="v")
                kb.op("vector", "tensor_scalar", (sl, sl, fs_bc[:, h:h + 1], None,
                                                  ALU.mult), inc="v")
                kb.op("vector", "tensor_add", (sl, sl, fl_bc[:, h * 4:(h + 1) * 4]),
                      inc="v")
            kb.op("vector", "tensor_copy", (mtmb[:, 0:16], lgt4[:, 0:16]), inc="v")
            for c in range(4):
                tp(3, PS[3][0:4, c * 64:(c + 1) * 64].bitcast(BF16),
                   mtmb[:, c * 4:(c + 1) * 4], diag01[:])
            t_ptp = kb.now("t")
            evac("scalar", "copy", (m_fm[0:4, 0:T],
                                    PS[3][0:4, 0:256].bitcast(BF16)),
                 waits=[("t", t_ptp)], bank=3)
            for b in range(4):
                mm(3, PS[3][:, 0:T], E4[:, b * 128:(b + 1) * 128], m_fm[0:4, 0:T],
                   start=True, stop=True)
                evac("scalar", "copy", (pb_bc[:, b * T:(b + 1) * T], PS[3][:, 0:T]),
                     waits=[("t", kb.now("t"))], bank=3)
            for half in range(2):
                ct = 2 * h + half
                for dt in range(DT):
                    mm(7 if dt == 0 else None, PS[7][:, 0:T],
                       wslot2[:, dt * D + ct * 128: dt * D + (ct + 1) * 128],
                       hsb[:, dt * T:(dt + 1) * T],
                       start=(dt == 0), stop=(dt == DT - 1), waits=[("qb", lid)])
                vsl = v_f[:, ct * (T + 62) + 62: ct * (T + 62) + 62 + T]
                kb.op("vector", "tensor_tensor", (obuf1[:, 0:T],
                      ls_f[:, ct * T:(ct + 1) * T], pb_bc[:, 0:T], ALU.mult), inc="v")
                kb.op("vector", "tensor_tensor", (obuf2[:, 0:T],
                      ll_f[:, ct * T:(ct + 1) * T], pb_bc[:, T:2 * T], ALU.mult),
                      inc="v")
                kb.op("vector", "tensor_add", (obuf1[:, 0:T], obuf1[:, 0:T],
                                               obuf2[:, 0:T]), inc="v")
                kb.op("vector", "tensor_tensor", (obuf2[:, 0:T],
                      do_f[:, ct * T:(ct + 1) * T], pb_bc[:, 2 * T:3 * T], ALU.mult),
                      inc="v")
                kb.op("vector", "tensor_add", (obuf1[:, 0:T], obuf1[:, 0:T],
                                               obuf2[:, 0:T]), inc="v")
                kb.op("vector", "tensor_tensor", (obuf2[:, 0:T], vsl,
                      pb_bc[:, 3 * T:4 * T], ALU.mult), inc="v")
                kb.op("vector", "tensor_add", (obuf1[:, 0:T], obuf1[:, 0:T],
                                               obuf2[:, 0:T]), inc="v")
                kb.op("vector", "scalar_tensor_tensor",
                      (obuf1[:, 0:T], ls_f[:, ct * T:(ct + 1) * T],
                       scres[:, h:h + 1], obuf1[:, 0:T], ALU.mult, ALU.add), inc="v")
                kb.op("vector", "scalar_tensor_tensor",
                      (do_f[:, ct * T:(ct + 1) * T], PS[7][:, 0:T],
                       acol[:, h:h + 1], obuf1[:, 0:T], ALU.mult, ALU.add),
                      waits=[("t", kb.now("t"))], inc="v")
                bank_free[7] = ("v", kb.now("v"))
            for half in range(2):
                ct = 2 * h + half
                kb.op("scalar", "activation", (sqs[:], do_f[:, ct * T:(ct + 1) * T],
                                               AF.Square), inc="s")
                mm(6 if half == 0 else None, PS[6][0:1, 0:T], onescf[:], sqs[:],
                   start=(half == 0), stop=(half == 1))
            e_ms = evac("scalar", "activation", (rowbuf[0:1, 0:T], PS[6][0:1, 0:T],
                                                 AF.Sqrt),
                        dict(scale=1.0 / DV, bias=cepso[0:1, 0:1]),
                        waits=[("t", kb.now("t"))], bank=6)
            kb.op("vector", "reciprocal", (rowbuf[0:1, T:2 * T], rowbuf[0:1, 0:T]),
                  inc="v")
            mm(6, PS[6][:, 0:T], onesrow[0:1, :], rowbuf[0:1, T:2 * T],
               start=True, stop=True)
            e_rb = evac("scalar", "copy", (obuf2[:, 0:T], PS[6][:, 0:T]),
                        waits=[("t", kb.now("t"))], bank=6)
            for half in range(2):
                ct = 2 * h + half
                kb.op("vector", "scalar_tensor_tensor",
                      (do_f[:, ct * T:(ct + 1) * T], do_f[:, ct * T:(ct + 1) * T],
                       onw_c[:, half:half + 1], obuf2[:, 0:T], ALU.mult, ALU.mult),
                      inc="v")

        for i in range(DT):
            for ph in range(2):
                gload(wslot[ph * 64:(ph + 1) * 64, i * D:(i + 1) * D],
                  Wo_d[i * 128 + ph * 64: i * 128 + (ph + 1) * 64, :],
                  sem="qb")
        kb.dma_barrier("qb")
        lwo = kb.now("qb")
        for ct in range(DT):
            for dt in range(DT):
                mm(ct if dt == 0 else None, PS[ct][:, 0:T],
                   wslot[:, dt * D + ct * 128: dt * D + (ct + 1) * 128],
                   do_f[:, dt * T:(dt + 1) * T],
                   start=(dt == 0), stop=(dt == DT - 1), waits=[("qb", lwo)])
        for ct in range(DT):
            evac("vector", "tensor_tensor",
                 (x_out[:, ct * T:(ct + 1) * T], x_in[:, ct * T:(ct + 1) * T],
                  PS[ct][:, 0:T], ALU.add),
                 waits=[("t", kb.now("t"))], bank=ct)

        for ct in range(DT):
            kb.op("vector", "tensor_copy",
                  (v_f[:, ct * (T + 62): ct * (T + 62) + 62],
                   v_f[:, ct * (T + 62) + T: ct * (T + 62) + T + 62]), inc="v")
        return kb.now("s")

    for ti in range(NTILE):
        for pt in range(4):
            load(xtm[:, pt * D:(pt + 1) * D],
                 x0_d[ti * T + pt * 128: ti * T + (pt + 1) * 128, :])
        lq = kb.now("q")
        kb.dma_barrier("q")
        for dt in range(DT):
            for pt in range(4):
                tp(3 if pt == 0 else None,
                   PS[3][0:128, pt * 128:(pt + 1) * 128],
                   xtm[:, pt * D + dt * 128: pt * D + (dt + 1) * 128],
                   identf[:], waits=[("q", lq), ("v", gv_id)])
            evac("scalar", "copy", (x0f[:, dt * T:(dt + 1) * T], PS[3][0:128, 0:512]),
                 waits=[("t", kb.now("t"))], bank=3)

        s_ld = kb.now("s")
        if ti > 0 and STAGE in ("x3", "x4", "full", "dbgb"):
            for i in range(4):
                load(k_f[:, i * DT * 112:(i + 1) * DT * 112],
                     xptf_dram[:, i * DT * 112:(i + 1) * DT * 112])
            kb.dma_barrier("q")
        emit_mamba(x0f, x1f, prod=[("s", s_ld)])
        if STAGE != "x1":
            emit_ffn(x1f, x0f, ti, prod=[("s", kb.now("s"))])
        if STAGE in ("x3", "x4", "full", "dbgb"):
            emit_attnb(x0f, x1f, ti)
        if STAGE == "dbgb" and ti == 0:
            for nm, buf, ncols in (("qf", bmcb[:, 10496:14616], 4120),
                                   ("kf", k_f[:, 0:4120], 4120),
                                   ("vf", v_f[:, 0:4592], 4592),
                                   ("dof", x1f[:, 0:4096], 4096),
                                   ("lsf", ymt[:, 0:4096], 4096),
                                   ("llf", Ht[:, 0:4096], 4096)):
                pass
            kb.op("gpsimd", "dma_start", (),
                  dict(out=y_d[0:128, :], in_=bmcb[:, 10496:11520]),
                  waits=[("v", kb.now("v")), ("s", kb.now("s"))], inc="store", incn=16)
            kb.op("gpsimd", "dma_start", (),
                  dict(out=y_d[128:256, :], in_=k_f[:, 0:1024]), inc="store", incn=16)
            kb.op("gpsimd", "dma_start", (),
                  dict(out=y_d[256:384, :], in_=v_f[:, 0:1024]), inc="store", incn=16)
            kb.op("gpsimd", "dma_start", (),
                  dict(out=y_d[384:512, :], in_=x1f[:, 0:1024]), inc="store", incn=16)
            kb.op("gpsimd", "dma_start", (),
                  dict(out=y_d[512:640, :], in_=ymt[:, 0:1024]), inc="store", incn=16)
            kb.op("gpsimd", "dma_start", (),
                  dict(out=y_d[640:768, :], in_=Ht[:, 0:1024]), inc="store", incn=16)
            kb.op("gpsimd", "dma_start", (),
                  dict(out=y_d[768:776, :], in_=hsb[:, 0:0 + 1024][0:8, :]),
                  inc="store", incn=16)
            kb.dma_barrier("store")
            kb.op("vector", "nop", (), waits=[("store", kb.now("store"))], inc="v")
            kb.op("scalar", "nop", (), waits=[("store", kb.now("store"))], inc="s")
        if STAGE in ("x4", "full"):
            emit_ffn(x1f, x0f, ti, prod=[("s", kb.now("s"))])

        if STAGE == "dbg" and ti == 0:
            kb.op("gpsimd", "dma_start", (),
                  dict(out=y_d[512:1024, :].rearrange("(g p) c -> p g c", p=128),
                       in_=xnb[:, :].rearrange("p (g c) -> p g c", c=1024)),
                  waits=[("v", kb.now("v"))], inc="store", incn=16)
            kb.op("gpsimd", "dma_start", (),
                  dict(out=y_d[128:256, 0:32], in_=lgtm[:, :]),
                  waits=[("s", kb.now("s"))], inc="store", incn=16)
            kb.op("gpsimd", "dma_start", (),
                  dict(out=y_d[256:264, 0:T], in_=m_fm[0:8, :]),
                  waits=[("s", kb.now("s"))], inc="store", incn=16)
            kb.op("gpsimd", "dma_start", (),
                  dict(out=y_d[384:512, 0:32], in_=mrow4[:, :]),
                  waits=[("s", kb.now("s"))], inc="store", incn=16)
            kb.op("gpsimd", "dma_start", (),
                  dict(out=y_d[384:512, 32:64], in_=srow4[:, :]),
                  waits=[("s", kb.now("s"))], inc="store", incn=16)
            kb.op("gpsimd", "dma_start", (),
                  dict(out=y_d[384:512, 64:80], in_=rc16[:, :]),
                  waits=[("s", kb.now("s"))], inc="store", incn=16)
            kb.op("gpsimd", "dma_start", (),
                  dict(out=y_d[384:512, 80:112], in_=mtmb[:, :]),
                  waits=[("s", kb.now("s"))], inc="store", incn=16)
            kb.op("gpsimd", "dma_start", (),
                  dict(out=y_d[1024:1536, :].rearrange("(g p) c -> p g c", p=128),
                       in_=x0f[:, :].rearrange("p (g c) -> p g c", c=1024)),
                  waits=[("s", kb.now("s"))], inc="store", incn=16)
        if STAGE == "dbg" and ti == 0:
            kb.op("vector", "nop", (), waits=[("store", kb.now("store"))], inc="v")
            kb.op("scalar", "nop", (), waits=[("store", kb.now("store"))], inc="s")
        if STAGE in ("dbg", "dbgb"):
            continue
        out_src = {"x1": x1f, "x2": x0f, "dbg": x0f, "x3": x1f, "x4": x0f, "full": x0f, "dbgb": x1f}[STAGE]
        gs_o = sguard(out_src[:, DT * T - 1: DT * T])
        for pt in range(4):
            for dt in range(DT):
                bank = 3 if dt == 0 else (7 if dt == 4 else None)
                dst = PS[3][0:128, dt * 64:(dt + 1) * 64].bitcast(BF16) if dt < 4 else \
                    PS[7][0:128, (dt - 4) * 64:(dt - 3) * 64].bitcast(BF16)
                tp(bank, dst, out_src[:, dt * T + pt * 128: dt * T + (pt + 1) * 128],
                   diag01[:], waits=[("s", gs_o)])
            t_o = kb.now("t")
            evac("vector", "tensor_copy",
                 (xtm[:, pt * D: pt * D + 512], PS[3][0:128, 0:256].bitcast(BF16)),
                 waits=[("t", t_o)], bank=3)
            e2 = evac("vector", "tensor_copy",
                      (xtm[:, pt * D + 512: pt * D + 1024],
                       PS[7][0:128, 0:256].bitcast(BF16)),
                      waits=[("t", t_o)], bank=7)
            kb.op("sync", "dma_start", (),
                  dict(out=y_d[ti * T + pt * 128: ti * T + (pt + 1) * 128, :],
                       in_=xtm[:, pt * D:(pt + 1) * D]),
                  waits=[("v", e2)], inc="store", incn=16)
        kb.dma_barrier("store")
    n_store = kb.now("store")

    with contextlib.ExitStack() as st_:
        sems = {s: st_.enter_context(nc.semaphore(s)) for s in kb.semnames}
        with nc.Block() as block:
            @block.sync
            def _(e):
                kb.replay(e, "sync", sems)
                e.wait_ge(sems["store"], n_store)

            @block.gpsimd
            def _(e):
                kb.replay(e, "gpsimd", sems)

            @block.vector
            def _(e):
                kb.replay(e, "vector", sems)

            @block.scalar
            def _(e):
                kb.replay(e, "scalar", sems)

            @block.tensor
            def _(e):
                kb.replay(e, "tensor", sems)

    return nc


def kernel(**inputs):
    nc = build_core_program()
    x0 = np.ascontiguousarray(np.asarray(inputs["L0_attn_a_u"], np.float32))
    base = {
        "aw_conv_w": inputs["L0_attn_a_conv_w"],
        "aw_conv_b": inputs["L0_attn_a_conv_b"],
        "aw_xpw": inputs["L0_attn_a_x_proj_w"],
        "aw_dtw": inputs["L0_attn_a_dt_proj_w"],
        "aw_dtb": inputs["L0_attn_a_dt_proj_b"],
        "aw_Alog": inputs["L0_attn_a_A_log"],
        "aw_Dskip": inputs["L0_attn_a_D_skip"],
        "fw_rw": inputs["L0_ffn_router_w"],
        "fw_rb": inputs["L0_ffn_router_b"],
        "fw_ew": inputs["L0_ffn_expert_w"],
        "fw_eb": inputs["L0_ffn_expert_b"],
        "bw_Wq": inputs["L0_attn_b_Wq"],
        "bw_Wk": inputs["L0_attn_b_Wk"],
        "bw_Wv": inputs["L0_attn_b_Wv"],
        "bw_cwq": inputs["L0_attn_b_cwq"],
        "bw_cwk": inputs["L0_attn_b_cwk"],
        "bw_cwv": inputs["L0_attn_b_cwv"],
        "bw_Wb": inputs["L0_attn_b_Wb"],
        "bw_Wid": inputs["L0_attn_b_Wid"],
        "bw_alpha": inputs["L0_attn_b_alpha_id"],
        "bw_firs": inputs["L0_attn_b_firs"],
        "bw_firl": inputs["L0_attn_b_firl"],
        "bw_gW1": inputs["L0_attn_b_gW1"],
        "bw_gb1": inputs["L0_attn_b_gb1"],
        "bw_gW2": inputs["L0_attn_b_gW2"],
        "bw_gb2": inputs["L0_attn_b_gb2"],
        "bw_ltemp": inputs["L0_attn_b_log_temp"],
        "bw_floor": inputs["L0_attn_b_floor_param"],
        "bw_cres": inputs["L0_attn_b_conv_res_logit"],
        "bw_onw": inputs["L0_attn_b_o_norm_w"],
        "bw_Wo": inputs["L0_attn_b_Wo"],
    }
    base = {k: np.ascontiguousarray(np.asarray(v, np.float32)) for k, v in base.items()}
    in_maps = []
    for c in range(8):
        m = dict(base)
        m["x0"] = np.ascontiguousarray(x0[c])
        in_maps.append(m)
    res = run_bass_kernel_spmd(nc, in_maps, core_ids=list(range(8)))
    return np.stack([res.results[c]["y"] for c in range(8)], axis=0)
